# version 02
# baseline (speedup 1.0000x reference)
import numpy as np
from contextlib import ExitStack

import concourse.bass as bass
import concourse.mybir as mybir
from concourse.bass_utils import run_bass_kernel_spmd

F32 = mybir.dt.float32
BF16 = mybir.dt.bfloat16
AF = mybir.ActivationFunctionType
OP = mybir.AluOpType

D = 4096
SEQ = 8192
NCORES = 8
TPC = 1024
NSLOT = 8
KC = 32
NKV = 8
GSIZE = 4
ATTN_MULT = 0.08838834764831845
INTER0 = 11008
FCH = 86
EPS = 1e-5
ROPE_THETA = 10000.0

_uid = [0]


def uid():
    _uid[0] += 1
    return _uid[0]


class Sem:
    def __init__(self, stack, nc, name):
        self.h = stack.enter_context(nc.semaphore(name))
        self.v = 0

    def inc(self, ins, k=1):
        ins.then_inc(self.h, k)
        self.v += k
        return ins


class Ctx:
    def __init__(self, nc, stack):
        self.nc = nc
        self.stack = stack
        self.sems = {}
        self.t = {}

    def sem(self, name):
        if name not in self.sems:
            self.sems[name] = Sem(self.stack, self.nc, name)
        return self.sems[name]




def emit_consts(ctx):
    nc = ctx.nc
    s_dve = ctx.sem("s_dve")
    with nc.Block() as blk:
        @blk.vector
        def _(v):
            v.memset(ctx.t["ones_col"][:, :], 1.0)
            v.memset(ctx.t["cm30"][:, :], -30.0)
            s_dve.inc(v.memset(ctx.t["ones_row"][:, :], 1.0))


def emit_ln(ctx, x_dram, xn_sb):
    nc = ctx.nc
    lnx = [ctx.sem("lnx0"), ctx.sem("lnx1")]
    s_act = ctx.sem("s_act")
    s_pe = ctx.sem("s_pe")
    s_dve = ctx.sem("s_dve")
    ones_col = ctx.t["ones_col"]
    ones_row = ctx.t["ones_row"]

    with (
        nc.sbuf_tensor(f"ln_xin_{uid()}", [128, 2, 1024], F32) as xin,
        nc.sbuf_tensor(f"ln_xb_{uid()}", [128, 2, 1024], BF16) as xb,
        nc.sbuf_tensor(f"ln_xq_{uid()}", [128, 2, 1024], BF16) as xq,
        nc.sbuf_tensor(f"ln_rows_{uid()}", [1, 8, 1024], F32) as rows,
        nc.sbuf_tensor(f"ln_mb_{uid()}", [128, 1024], F32) as mb,
        nc.sbuf_tensor(f"ln_rb_{uid()}", [128, 1024], F32) as rb,
    ):
        with (
            nc.psum_tensor(f"ln_sum_{uid()}", [1, 1024], F32) as sum_ps,
            nc.psum_tensor(f"ln_sq_{uid()}", [1, 1024], F32) as sq_ps,
            nc.Block() as blk,
        ):
            xb0 = [lnx[0].v, lnx[1].v]
            abase, pbase = s_act.v, s_pe.v

            @blk.sync
            def _(sy):
                for c in range(KC):
                    if c >= 2:
                        sy.wait_ge(s_act.h, abase + c - 1)
                    lnx[c % 2].inc(sy.dma_start(xin[:, c % 2, :],
                                                x_dram[128 * c:128 * c + 128, :]), 16)

            @blk.scalar
            def _(s):
                for c in range(KC):
                    s.wait_ge(lnx[c % 2].h, xb0[c % 2] + 16 * (c // 2 + 1))
                    if c >= 2:
                        s.wait_ge(s_pe.h, pbase + c - 1)
                    s.copy(xb[:, c % 2, :], xin[:, c % 2, :])
                    s_act.inc(s.activation(xq[:, c % 2, :], xin[:, c % 2, :], AF.Square))

            @blk.tensor
            def _(t):
                for c in range(KC):
                    t.wait_ge(s_act.h, abase + c + 1)
                    for h in range(2):
                        sl = slice(512 * h, 512 * h + 512)
                        t.matmul(sum_ps[:, sl], ones_col[:, :], xb[:, c % 2, sl],
                                 start=(c == 0), stop=(c == KC - 1))
                        ins = t.matmul(sq_ps[:, sl], ones_col[:, :], xq[:, c % 2, sl],
                                       start=(c == 0), stop=(c == KC - 1))
                    s_pe.inc(ins)

            pe_done = s_pe.v

            @blk.vector
            def _(v):
                v.wait_ge(s_pe.h, pe_done)
                m = rows[:, 0, :]
                v.tensor_scalar_mul(m, sum_ps[:, :], 1.0 / D)
                v.tensor_scalar_mul(rows[:, 1, :], sq_ps[:, :], 1.0 / D)
                v.tensor_mul(rows[:, 2, :], m, m)
                v.tensor_sub(rows[:, 3, :], rows[:, 1, :], rows[:, 2, :])
                s_dve.inc(v.tensor_scalar_add(rows[:, 6, :], rows[:, 3, :], float(EPS)))

            vdone = s_dve.v

            @blk.scalar
            def _(s):
                s.wait_ge(s_dve.h, vdone)
                s.activation(rows[:, 4, :], rows[:, 6, :], AF.Ln)
                s_act.inc(s.activation(rows[:, 5, :], rows[:, 4, :], AF.Exp, scale=-0.5))

            adone = s_act.v

        with (
            nc.psum_tensor(f"ln_mbp_{uid()}", [128, 1024], F32) as mb_ps,
            nc.psum_tensor(f"ln_rbp_{uid()}", [128, 1024], F32) as rb_ps,
            nc.Block() as blk,
        ):
            @blk.tensor
            def _(t):
                t.wait_ge(s_act.h, adone)
                for h in range(2):
                    sl = slice(512 * h, 512 * h + 512)
                    t.matmul(mb_ps[:, sl], ones_row[:, :], rows[:, 0, sl], start=True, stop=True)
                    ins = t.matmul(rb_ps[:, sl], ones_row[:, :], rows[:, 5, sl], start=True, stop=True)
                s_pe.inc(ins)

            bdone = s_pe.v

            @blk.vector
            def _(v):
                v.wait_ge(s_pe.h, bdone)
                v.tensor_copy(mb[:, :], mb_ps[:, :])
                s_dve.inc(v.tensor_copy(rb[:, :], rb_ps[:, :]))

            cdone = s_dve.v
            xb2 = [lnx[0].v, lnx[1].v]
            vbase2 = s_dve.v

            @blk.sync
            def _(sy):
                for c in range(KC):
                    if c >= 2:
                        sy.wait_ge(s_dve.h, vbase2 + (c - 1))
                    lnx[c % 2].inc(sy.dma_start(xin[:, c % 2, :],
                                                x_dram[128 * c:128 * c + 128, :]), 16)

            @blk.vector
            def _(v):
                v.wait_ge(s_dve.h, cdone)
                for c in range(KC):
                    v.wait_ge(lnx[c % 2].h, xb2[c % 2] + 16 * (c // 2 + 1))
                    v.tensor_sub(xin[:, c % 2, :], xin[:, c % 2, :], mb[:, :])
                    s_dve.inc(v.tensor_mul(xn_sb[:, c, :], xin[:, c % 2, :], rb[:, :]))


def _emit_jobs_inline(ctx, blk, ps, jobs, outs, pe_waits=None, dve_extra=None):
    s_pe = ctx.sem("s_pe")
    s_dve2 = ctx.sem("s_dve2")
    pebase, evbase = s_pe.v, s_dve2.v

    @blk.tensor
    def _(t):
        for it, (lhs_list, rhs_list) in enumerate(jobs):
            if pe_waits and it in pe_waits:
                for sem, val in pe_waits[it]:
                    t.wait_ge(sem.h, val)
            if it >= 2:
                t.wait_ge(s_dve2.h, evbase + it - 1)
            p = ps[it % 2]
            n = len(lhs_list)
            for k in range(n):
                ins = t.matmul(p[:, :], lhs_list[k], rhs_list[k],
                               start=(k == 0), stop=(k == n - 1))
            s_pe.inc(ins)

    @blk.vector
    def _(v):
        if dve_extra:
            for sem, val in dve_extra:
                v.wait_ge(sem.h, val)
        for it, out_ap in enumerate(outs):
            v.wait_ge(s_pe.h, pebase + it + 1)
            s_dve2.inc(v.tensor_copy(out_ap, ps[it % 2][:, :]))
    return s_pe.v, s_dve2.v


def emit_kv_l0(ctx, wk_dram, wv_dram, xn_sb, k_sb, v_sb):
    nc = ctx.nc
    with (
        nc.sbuf_tensor(f"wk_sb_{uid()}", [128, 32, 1024], BF16) as wk,
        nc.psum_tensor(f"kvps{uid()}a", [128, 512], F32) as p0,
        nc.psum_tensor(f"kvps{uid()}b", [128, 512], F32) as p1,
        nc.Block() as blk,
    ):
        ldk = ctx.sem("ld_wk")
        b0 = ldk.v

        @blk.gpsimd
        def _(g):
            ldk.inc(g.dma_start(wk[:], wk_dram.rearrange("(kc p) m -> p kc m", p=128)), 16)

        jobs, outs = [], []
        for m in range(8):
            for th in range(2):
                jobs.append((
                    [wk[:, kc, 128 * m:128 * m + 128] for kc in range(KC)],
                    [xn_sb[:, kc, 512 * th:512 * th + 512] for kc in range(KC)],
                ))
                outs.append(k_sb[:, m, 512 * th:512 * th + 512])
        _emit_jobs_inline(ctx, blk, [p0, p1], jobs, outs, {0: [(ldk, b0 + 16)]})

    with (
        nc.sbuf_tensor(f"wv_sb_{uid()}", [128, 32, 1024], BF16) as wv,
        nc.psum_tensor(f"kvps{uid()}a", [128, 512], F32) as p0,
        nc.psum_tensor(f"kvps{uid()}b", [128, 512], F32) as p1,
        nc.Block() as blk,
    ):
        ldv = ctx.sem("ld_wv")
        b0 = ldv.v

        @blk.gpsimd
        def _(g):
            ldv.inc(g.dma_start(wv[:], wv_dram.rearrange("(kc p) m -> p kc m", p=128)), 16)

        jobs, outs = [], []
        for s in range(8):
            for vh in range(2):
                jobs.append((
                    [xn_sb[:, kc, 128 * s:128 * s + 128] for kc in range(KC)],
                    [wv[:, kc, 512 * vh:512 * vh + 512] for kc in range(KC)],
                ))
                outs.append(v_sb[:, s, 512 * vh:512 * vh + 512])
        _emit_jobs_inline(ctx, blk, [p0, p1], jobs, outs, {0: [(ldv, b0 + 16)]})


def emit_kv_l1(ctx, dram, xn_sb, v_sb, k_dram, prep_wait):
    nc = ctx.nc
    s_s = ctx.sem("s_s")
    s_dve = ctx.sem("s_dve")
    s_dve2 = ctx.sem("s_dve2")
    s_c = ctx.sem("s_c")
    kcos, ksin = ctx.t["kcos"], ctx.t["ksin"]

    with (
        nc.sbuf_tensor(f"ckv_sb_{uid()}", [128, 4, 1024], BF16) as ckv,
        nc.sbuf_tensor(f"krot_sb_{uid()}", [128, 4, 1024], BF16) as krot,
        nc.sbuf_tensor(f"kc_sb_{uid()}", [128, 4, 1024], BF16) as kcs,
    ):
        with (
            nc.sbuf_tensor(f"wdkT_{uid()}", [128, 32, 512], BF16) as wdkT,
            nc.psum_tensor(f"kv1ps{uid()}a", [128, 512], F32) as p0,
            nc.psum_tensor(f"kv1ps{uid()}b", [128, 512], F32) as p1,
            nc.Block() as blk,
        ):
            lddk = ctx.sem("ld_wdk")
            sbase = lddk.v

            @blk.sync
            def _(sy):
                sy.wait_ge(prep_wait[0].h, prep_wait[1])
                for kc in range(KC):
                    lddk.inc(sy.dma_start(wdkT[:, kc, :],
                                          dram["wdk_bf"][0:512, 128 * kc:128 * kc + 128],
                                          transpose=True), 16)

            jobs, outs = [], []
            for m in range(4):
                for th in range(2):
                    jobs.append((
                        [wdkT[:, kc, 128 * m:128 * m + 128] for kc in range(KC)],
                        [xn_sb[:, kc, 512 * th:512 * th + 512] for kc in range(KC)],
                    ))
                    outs.append(ckv[:, m, 512 * th:512 * th + 512])
            _emit_jobs_inline(ctx, blk, [p0, p1], jobs, outs,
                              {0: [(lddk, sbase + 16 * 32)]})

        with (
            nc.sbuf_tensor(f"wkrT_{uid()}", [128, 32, 512], BF16) as wkrT,
            nc.sbuf_tensor(f"kr_sb_{uid()}", [128, 4, 1024], F32) as kr,
            nc.sbuf_tensor(f"kv1_scr_{uid()}", [128, 2, 1024], F32) as scr,
            nc.psum_tensor(f"kv1ps{uid()}a", [128, 512], F32) as p0,
            nc.psum_tensor(f"kv1ps{uid()}b", [128, 512], F32) as p1,
            nc.Block() as blk,
        ):
            ldkr = ctx.sem("ld_wkr")
            sbase = ldkr.v

            @blk.sync
            def _(sy):
                for kc in range(KC):
                    ldkr.inc(sy.dma_start(wkrT[:, kc, :],
                                          dram["wkr_bf"][0:512, 128 * kc:128 * kc + 128],
                                          transpose=True), 16)

            jobs, outs = [], []
            for m in range(4):
                for th in range(2):
                    jobs.append((
                        [wkrT[:, kc, 128 * m:128 * m + 128] for kc in range(KC)],
                        [xn_sb[:, kc, 512 * th:512 * th + 512] for kc in range(KC)],
                    ))
                    outs.append(kr[:, m, 512 * th:512 * th + 512])
            _emit_jobs_inline(ctx, blk, [p0, p1], jobs, outs,
                              {0: [(ldkr, sbase + 16 * 32)]})

            @blk.vector
            def _(v):
                for j in range(2):
                    a = kr[:, j, :]
                    b = kr[:, j + 2, :]
                    v.tensor_mul(scr[:, 0, :], a, kcos[:, :])
                    v.tensor_mul(scr[:, 1, :], b, ksin[:, :])
                    v.tensor_sub(krot[:, j, :], scr[:, 0, :], scr[:, 1, :])
                    v.tensor_mul(scr[:, 0, :], b, kcos[:, :])
                    v.tensor_mul(scr[:, 1, :], a, ksin[:, :])
                    s_dve.inc(v.tensor_add(krot[:, j + 2, :], scr[:, 0, :], scr[:, 1, :]))

        rope_done = s_dve.v

        with (
            nc.sbuf_tensor(f"wupkT_{uid()}", [128, 4, 512], BF16) as wupkT,
            nc.sbuf_tensor(f"wupvT_{uid()}", [128, 4, 2, 512], BF16) as wupvT,
            nc.psum_tensor(f"kv1ps{uid()}a", [128, 512], F32) as p0,
            nc.psum_tensor(f"kv1ps{uid()}b", [128, 512], F32) as p1,
            nc.Block() as blk,
        ):
            ldupk = ctx.sem("ld_upk")
            ldupv = ctx.sem("ld_upv")
            bk0, bv0 = ldupk.v, ldupv.v

            @blk.sync
            def _(sy):
                for kc in range(4):
                    ldupk.inc(sy.dma_start(wupkT[:, kc, :],
                                           dram["wupk_bf"][0:512, 128 * kc:128 * kc + 128],
                                           transpose=True), 16)
                for kc in range(4):
                    for vhh in range(2):
                        ldupv.inc(sy.dma_start(wupvT[:, kc, vhh, :],
                                               dram["wupv_bf"][512 * vhh:512 * vhh + 512,
                                                               128 * kc:128 * kc + 128],
                                               transpose=True), 16)

            jobs2, outs2 = [], []
            for m in range(4):
                for th in range(2):
                    jobs2.append((
                        [wupkT[:, kc, 128 * m:128 * m + 128] for kc in range(4)],
                        [ckv[:, kc, 512 * th:512 * th + 512] for kc in range(4)],
                    ))
                    outs2.append(kcs[:, m, 512 * th:512 * th + 512])
            for s in range(8):
                for vhh in range(2):
                    jobs2.append((
                        [ckv[:, kc, 128 * s:128 * s + 128] for kc in range(4)],
                        [wupvT[:, kc, vhh, :] for kc in range(4)],
                    ))
                    outs2.append(v_sb[:, s, 512 * vhh:512 * vhh + 512])
            pe_waits2 = {0: [(ldupk, bk0 + 16 * 4)], 8: [(ldupv, bv0 + 16 * 8)]}
            _, dve_v2 = _emit_jobs_inline(ctx, blk, [p0, p1], jobs2, outs2, pe_waits2)

            kvst = ctx.sem("kvst")

            @blk.scalar
            def _(sc):
                sc.wait_ge(s_dve.h, rope_done)
                sc.wait_ge(s_dve2.h, dve_v2)
                for j in range(2):
                    for h in range(4):
                        g = 4 * j + h
                        kvst.inc(sc.dma_start(
                            k_dram[128 * g:128 * g + 32, :],
                            krot[32 * h:32 * h + 32, j, :]), 16)
                        kvst.inc(sc.dma_start(
                            k_dram[128 * g + 64:128 * g + 96, :],
                            krot[32 * h:32 * h + 32, j + 2, :]), 16)
                for m in range(4):
                    for gp in range(2):
                        g = 2 * m + gp
                        kvst.inc(sc.dma_start(
                            k_dram[128 * g + 32:128 * g + 64, :],
                            kcs[64 * gp:64 * gp + 32, m, :]), 16)
                        kvst.inc(sc.dma_start(
                            k_dram[128 * g + 96:128 * g + 128, :],
                            kcs[64 * gp + 32:64 * gp + 64, m, :]), 16)


def emit_q(ctx, layer, dram, xn_sb, q_dram, prep_wait=None):
    nc = ctx.nc
    s_pe = ctx.sem("s_pe")
    s_act = ctx.sem("s_act")
    s_dve = ctx.sem("s_dve")
    qw = [ctx.sem("qw0"), ctx.sem("qw1")]
    qs = [ctx.sem("qs0"), ctx.sem("qs1")]
    kvst = ctx.sem("kvst")
    NG = 8
    per_g = 16 if layer == 0 else 16 * 32

    with (
        nc.sbuf_tensor(f"wq_sb_{uid()}", [128, 2, 32, 512], BF16) as wq,
        nc.sbuf_tensor(f"q_out_{uid()}", [128, 2, 1024], BF16) as qo,
        nc.sbuf_tensor(f"q_sw_{uid()}", [128, 2, 1024], F32) as qsw,
        nc.sbuf_tensor(f"q_t1_{uid()}", [128, 1024], F32) as qt1,
        nc.sbuf_tensor(f"q_t2_{uid()}", [128, 1024], F32) as qt2,
        nc.psum_tensor(f"q_ps0_{uid()}", [128, 1024], F32) as qps0,
        nc.psum_tensor(f"q_ps1_{uid()}", [128, 1024], F32) as qps1,
        nc.Block() as blk,
    ):
        qps = [qps0, qps1]
        pbase, abase, vbase = s_pe.v, s_act.v, s_dve.v
        qwb = [qw[0].v, qw[1].v]
        qsb = [qs[0].v, qs[1].v]
        kvst_tot = kvst.v

        if layer == 0:
            @blk.gpsimd
            def _(g):
                g.wait_ge(kvst.h, kvst_tot)
                for mg in range(NG):
                    if mg >= 2:
                        g.wait_ge(s_pe.h, pbase + 4 * (mg - 1))
                    qw[mg % 2].inc(g.dma_start(
                        wq[:, mg % 2, :, :],
                        dram["wq0"][:, 512 * mg:512 * mg + 512]
                        .rearrange("(kc p) m -> p kc m", p=128)), 16)
        else:
            @blk.sync
            def _(sy):
                sy.wait_ge(kvst.h, kvst_tot)
                sy.wait_ge(prep_wait[0].h, prep_wait[1])
                for mg in range(NG):
                    if mg >= 2:
                        sy.wait_ge(s_pe.h, pbase + 4 * (mg - 1))
                    for kc in range(KC):
                        qw[mg % 2].inc(sy.dma_start(
                            wq[:, mg % 2, kc, :],
                            dram["wq1_bf"][512 * mg:512 * mg + 512, 128 * kc:128 * kc + 128],
                            transpose=True), 16)

        @blk.tensor
        def _(t):
            it = 0
            for mg in range(NG):
                t.wait_ge(qw[mg % 2].h, qwb[mg % 2] + per_g * (mg // 2 + 1))
                for mi in range(4):
                    if it >= 2:
                        t.wait_ge(s_dve.h, vbase + it - 1)
                    p = qps[it % 2]
                    for th in range(2):
                        for kc in range(KC):
                            ins = t.matmul(p[:, 512 * th:512 * th + 512],
                                           wq[:, mg % 2, kc, 128 * mi:128 * mi + 128],
                                           xn_sb[:, kc, 512 * th:512 * th + 512],
                                           start=(kc == 0), stop=(kc == KC - 1))
                    s_pe.inc(ins)
                    it += 1

        if layer == 1:
            qcos, qsin = ctx.t["qcos"], ctx.t["qsin"]

            @blk.scalar
            def _(sc):
                for it in range(KC):
                    sc.wait_ge(s_pe.h, pbase + it + 1)
                    if it >= 2:
                        sc.wait_ge(s_dve.h, vbase + it - 1)
                    p = qps[it % 2]
                    sc.copy(qsw[0:64, it % 2, :], p[64:128, :])
                    s_act.inc(sc.activation(qsw[64:128, it % 2, :], p[0:64, :], AF.Copy))
                    if it >= 2:
                        jt = it - 2
                        qs[jt % 2].inc(sc.dma_start(q_dram[128 * jt:128 * jt + 128, :],
                                                    qo[:, jt % 2, :]), 16)
                for jt in (KC - 2, KC - 1):
                    sc.wait_ge(s_dve.h, vbase + jt + 1)
                    qs[jt % 2].inc(sc.dma_start(q_dram[128 * jt:128 * jt + 128, :],
                                                qo[:, jt % 2, :]), 16)

            @blk.vector
            def _(v):
                for it in range(KC):
                    v.wait_ge(s_act.h, abase + it + 1)
                    if it >= 2:
                        v.wait_ge(qs[it % 2].h, qsb[it % 2] + 16 * (it // 2))
                    p = qps[it % 2]
                    v.tensor_mul(qt1[:, :], p[:, :], qcos[:, :])
                    v.tensor_mul(qt2[:, :], qsw[:, it % 2, :], qsin[:, :])
                    s_dve.inc(v.tensor_add(qo[:, it % 2, :], qt1[:, :], qt2[:, :]))
        else:
            @blk.vector
            def _(v):
                for it in range(KC):
                    v.wait_ge(s_pe.h, pbase + it + 1)
                    if it >= 2:
                        v.wait_ge(qs[it % 2].h, qsb[it % 2] + 16 * (it // 2))
                    s_dve.inc(v.tensor_copy(qo[:, it % 2, :], qps[it % 2][:, :]))

            @blk.scalar
            def _(sc):
                for it in range(KC):
                    sc.wait_ge(s_dve.h, vbase + it + 1)
                    qs[it % 2].inc(sc.dma_start(q_dram[128 * it:128 * it + 128, :],
                                                qo[:, it % 2, :]), 16)


def emit_ffn_l0(ctx, dram, xn_sb, ffn_dram, upto='all', dbg_out=None):
    nc = ctx.nc
    s_g = ctx.sem("s_g")
    s_pe = ctx.sem("s_pe")
    s_pe2 = ctx.sem("s_pe2")
    s_act = ctx.sem("s_act")
    s_dve = ctx.sem("s_dve")
    s_dve2 = ctx.sem("s_dve2")
    s_c = ctx.sem("s_c")
    b1g, b1u, b2t = ctx.t["b1g"], ctx.t["b1u"], ctx.t["b2t"]

    with nc.sbuf_tensor(f"p_sb_{uid()}", [128, 8, 1024], BF16) as p_sb:
        with (
            nc.sbuf_tensor(f"u1_sb_{uid()}", [128, 32, 1024], BF16) as u1,
            nc.psum_tensor(f"pps{uid()}a", [128, 512], F32) as pp0,
            nc.psum_tensor(f"pps{uid()}b", [128, 512], F32) as pp1,
            nc.Block() as blk,
        ):
            ldu1 = ctx.sem("ld_u1")
            gbase = ldu1.v
            qs = [ctx.sem("qs0"), ctx.sem("qs1")]
            qs_tot = [qs[0].v, qs[1].v]

            @blk.gpsimd
            def _(g):
                g.wait_ge(qs[0].h, qs_tot[0])
                g.wait_ge(qs[1].h, qs_tot[1])
                ldu1.inc(g.dma_start(u1[:], dram["u1"].rearrange("(kc p) m -> p kc m", p=128)), 16)

            jobs, outs = [], []
            for m in range(8):
                for th in range(2):
                    jobs.append((
                        [u1[:, kc, 128 * m:128 * m + 128] for kc in range(KC)],
                        [xn_sb[:, kc, 512 * th:512 * th + 512] for kc in range(KC)],
                    ))
                    outs.append(p_sb[:, m, 512 * th:512 * th + 512])
            _emit_jobs_inline(ctx, blk, [pp0, pp1], jobs, outs, {0: [(ldu1, gbase + 16)]})

        if upto == "ffnp":
            dbg = ctx.sem("dbg")
            with nc.Block() as blk:
                @blk.scalar
                def _(sc):
                    dbg.inc(sc.dma_start(
                        dbg_out.rearrange("(c p) t -> p c t", p=128), p_sb[:]), 16)
                    sc.wait_ge(dbg.h, dbg.v)
            return

        fblocks = [list(range(4 * b, min(4 * b + 4, FCH))) for b in range((FCH + 3) // 4)]
        NB = len(fblocks)
        cum_units = [0]
        for fb in fblocks:
            cum_units.append(cum_units[-1] + 2 * len(fb))
        par_loads = [[0], [0]]
        for b, fb in enumerate(fblocks):
            par_loads[b % 2].append(par_loads[b % 2][-1] + 3 * len(fb))
        def par_load_val(b):
            return 16 * par_loads[b % 2][b // 2 + 1]

        with (
            nc.sbuf_tensor(f"v1g_sb_{uid()}", [128, 2, 4, 8, 128], BF16) as v1g,
            nc.sbuf_tensor(f"v1u_sb_{uid()}", [128, 2, 4, 8, 128], BF16) as v1u,
            nc.sbuf_tensor(f"u2_sb_{uid()}", [128, 2, 4, 1024], BF16) as u2,
            nc.sbuf_tensor(f"act0_sb_{uid()}", [128, 2, 4, 1024], BF16) as act_sb,
            nc.sbuf_tensor(f"sg_sb_{uid()}", [128, 2, 512], BF16) as sg_sb,
            nc.sbuf_tensor(f"su_sb_{uid()}", [128, 2, 512], BF16) as su_sb,
            nc.sbuf_tensor(f"z_sb_{uid()}", [128, 8192], F32) as z_sb,
            nc.sbuf_tensor(f"zb_sb_{uid()}", [128, 8, 1024], BF16) as zb_sb,
            nc.psum_tensor(f"hg_ps0_{uid()}", [128, 1024], F32) as hg0,
            nc.psum_tensor(f"hg_ps1_{uid()}", [128, 1024], F32) as hg1,
            nc.psum_tensor(f"z_ps_{uid()}", [128, 2048], F32) as z_ps,
            nc.Block() as blk,
        ):
            hg = [hg0, hg1]
            ldv = [ctx.sem("ldv0"), ctx.sem("ldv1")]
            ldvb = [ldv[0].v, ldv[1].v]
            pbase = s_pe.v
            p2base = s_pe2.v
            abase = s_act.v
            vbase = s_dve.v
            v2base = s_dve2.v

            @blk.vector
            def _(v):
                s_dve2.inc(v.memset(z_sb[:, :], 0.0))

            @blk.gpsimd
            def _(g):
                for b, fb in enumerate(fblocks):
                    if b >= 2:
                        g.wait_ge(s_pe2.h, p2base + 4 * (b - 1))
                    for fi, f in enumerate(fb):
                        ldv[b % 2].inc(g.dma_start(
                            v1g[:, b % 2, fi, :, :],
                            dram["v1"][:, 128 * f:128 * f + 128]
                            .rearrange("(kc p) m -> p kc m", p=128)), 16)
                        ldv[b % 2].inc(g.dma_start(
                            v1u[:, b % 2, fi, :, :],
                            dram["v1"][:, INTER0 + 128 * f:INTER0 + 128 * f + 128]
                            .rearrange("(kc p) m -> p kc m", p=128)), 16)
                        ldv[b % 2].inc(g.dma_start(
                            u2[:, b % 2, fi, :],
                            dram["u2"][128 * f:128 * f + 128, :]), 16)

            def z_group(t, bb, tq):
                t.wait_ge(s_dve.h, vbase + cum_units[bb + 1])
                t.wait_ge(s_dve2.h, v2base + 1 + 4 * bb + tq)
                nf = len(fblocks[bb])
                for fi, f in enumerate(fblocks[bb]):
                    for m in range(8):
                        ins = t.matmul(
                            z_ps[:, 256 * m:256 * m + 256],
                            u2[:, bb % 2, fi, 128 * m:128 * m + 128],
                            act_sb[:, bb % 2, fi, 256 * tq:256 * tq + 256],
                            start=(fi == 0 and m % 2 == 0), stop=(fi == nf - 1),
                            skip_group_check=True)
                s_pe2.inc(ins)

            @blk.tensor
            def _(t):
                for b, fb in enumerate(fblocks):
                    t.wait_ge(ldv[b % 2].h, ldvb[b % 2] + par_load_val(b))
                    for fi, f in enumerate(fb):
                        for th in range(2):
                            u = cum_units[b] + 2 * fi + th
                            if u >= 2:
                                t.wait_ge(s_dve.h, vbase + (u - 1))
                            hp = hg[u % 2]
                            for kc in range(8):
                                t.matmul(hp[:, 0:512],
                                         v1g[:, b % 2, fi, kc, :],
                                         p_sb[:, kc, 512 * th:512 * th + 512],
                                         start=(kc == 0), stop=(kc == 7))
                            for kc in range(8):
                                ins = t.matmul(hp[:, 512:1024],
                                               v1u[:, b % 2, fi, kc, :],
                                               p_sb[:, kc, 512 * th:512 * th + 512],
                                               start=(kc == 0), stop=(kc == 7))
                            s_pe.inc(ins)
                    if b >= 1:
                        for tq in range(4):
                            z_group(t, b - 1, tq)
                for tq in range(4):
                    z_group(t, NB - 1, tq)

            @blk.scalar
            def _(sc):
                for b, fb in enumerate(fblocks):
                    for fi, f in enumerate(fb):
                        for th in range(2):
                            u = cum_units[b] + 2 * fi + th
                            sc.wait_ge(s_pe.h, pbase + u + 1)
                            if u >= 2:
                                sc.wait_ge(s_dve.h, vbase + u - 1)
                            s_act.inc(sc.activation(sg_sb[:, u % 2, :], hg[u % 2][:, 0:512],
                                                    AF.Silu, bias=b1g[:, f:f + 1]))

            @blk.vector
            def _(v):
                for b, fb in enumerate(fblocks):
                    for fi, f in enumerate(fb):
                        for th in range(2):
                            u = cum_units[b] + 2 * fi + th
                            v.wait_ge(s_act.h, abase + (u + 1))
                            if b >= 2:
                                v.wait_ge(s_pe2.h, p2base + 4 * (b - 1))
                            s_dve.inc(v.scalar_tensor_tensor(
                                act_sb[:, b % 2, fi, 512 * th:512 * th + 512],
                                hg[u % 2][:, 512:1024], b1u[:, f:f + 1],
                                sg_sb[:, u % 2, :], OP.add, OP.mult))
                    if b >= 1:
                        for tq in range(4):
                            v.wait_ge(s_pe2.h, p2base + 4 * (b - 1) + tq + 1)
                            dst = z_sb[:, :].rearrange("p (m x) -> p m x", x=1024)[
                                :, :, 256 * tq:256 * tq + 256]
                            src = z_ps[:, :].rearrange("p (m x) -> p m x", x=256)
                            s_dve2.inc(v.tensor_add(dst, dst, src))
                for tq in range(4):
                    v.wait_ge(s_pe2.h, p2base + 4 * (NB - 1) + tq + 1)
                    dst = z_sb[:, :].rearrange("p (m x) -> p m x", x=1024)[
                        :, :, 256 * tq:256 * tq + 256]
                    src = z_ps[:, :].rearrange("p (m x) -> p m x", x=256)
                    s_dve2.inc(v.tensor_add(dst, dst, src))
                zv = z_sb[:, :].rearrange("p (m x) -> p m x", x=1024)
                for m in range(8):
                    s_dve.inc(v.tensor_copy(zb_sb[:, m, :], zv[:, m, :]))

            zcast_done = s_dve.v

        if upto == "ffna":
            dbg = ctx.sem("dbg")
            with nc.Block() as blk:
                ad = s_dve.v

                @blk.scalar
                def _(sc):
                    sc.wait_ge(s_dve.h, ad)
                    dbg.inc(sc.dma_start(
                        dbg_out[0:512].rearrange("(c p) t -> p c t", p=128),
                        act_sb[:, 0, :, :]), 16)
                    sc.wait_ge(dbg.h, dbg.v)
            return

        if upto == "ffnz":
            dbg = ctx.sem("dbg")
            with nc.Block() as blk:
                zd = s_dve.v

                @blk.scalar
                def _(sc):
                    sc.wait_ge(s_dve.h, zd)
                    dbg.inc(sc.dma_start(
                        dbg_out.rearrange("(c p) t -> p c t", p=128), zb_sb[:]), 16)
                    sc.wait_ge(dbg.h, dbg.v)
            return

        with (
            nc.sbuf_tensor(f"v2_sb_{uid()}", [128, 2, 8, 1024], BF16) as v2,
            nc.sbuf_tensor(f"fo_sb_{uid()}", [128, 2, 512], F32) as fo,
            nc.psum_tensor(f"fo_ps0_{uid()}", [128, 512], F32) as fps0,
            nc.psum_tensor(f"fo_ps1_{uid()}", [128, 512], F32) as fps1,
            nc.Block() as blk,
        ):
            fps = [fps0, fps1]
            pbase, abase = s_pe.v, s_act.v
            vbase_f = s_dve.v
            ldv2 = [ctx.sem("ldv2a"), ctx.sem("ldv2b")]
            ldv2b_ = [ldv2[0].v, ldv2[1].v]
            fs = [ctx.sem("fs0"), ctx.sem("fs1")]
            fsb = [fs[0].v, fs[1].v]

            @blk.gpsimd
            def _(g):
                for mg in range(4):
                    if mg >= 2:
                        g.wait_ge(s_pe.h, pbase + 16 * (mg - 1))
                    ldv2[mg % 2].inc(g.dma_start(
                        v2[:, mg % 2, :, :],
                        dram["v2"][:, 1024 * mg:1024 * mg + 1024]
                        .rearrange("(kc p) m -> p kc m", p=128)), 16)

            @blk.tensor
            def _(t):
                t.wait_ge(s_dve.h, zcast_done)
                it = 0
                for mg in range(4):
                    t.wait_ge(ldv2[mg % 2].h, ldv2b_[mg % 2] + 16 * (mg // 2 + 1))
                    for mi in range(8):
                        for th in range(2):
                            if it >= 2:
                                t.wait_ge(s_dve.h, vbase_f + it - 1)
                            p = fps[it % 2]
                            for kc in range(8):
                                ins = t.matmul(p[:, :],
                                               v2[:, mg % 2, kc, 128 * mi:128 * mi + 128],
                                               zb_sb[:, kc, 512 * th:512 * th + 512],
                                               start=(kc == 0), stop=(kc == 7))
                            s_pe.inc(ins)
                            it += 1

            vbase_c = s_dve.v

            @blk.vector
            def _(v):
                it = 0
                for mg in range(4):
                    for mi in range(8):
                        m = 8 * mg + mi
                        for th in range(2):
                            v.wait_ge(s_pe.h, pbase + it + 1)
                            if it >= 2:
                                v.wait_ge(fs[it % 2].h, fsb[it % 2] + 16 * (it // 2))
                            s_dve.inc(v.tensor_scalar_add(fo[:, it % 2, :], fps[it % 2][:, :],
                                                          b2t[:, m:m + 1]))
                            it += 1

            @blk.scalar
            def _(sc):
                it = 0
                for mg in range(4):
                    for mi in range(8):
                        m = 8 * mg + mi
                        for th in range(2):
                            sc.wait_ge(s_dve.h, vbase_c + it + 1)
                            fs[it % 2].inc(sc.dma_start(
                                ffn_dram[128 * m:128 * m + 128, 512 * th:512 * th + 512],
                                fo[:, it % 2, :]), 16)
                            it += 1


def emit_ffn_l1(ctx, dram, xn_sb, act_sb, ffn_dram, prep_wait):
    nc = ctx.nc
    s_s = ctx.sem("s_s")
    s_pe = ctx.sem("s_pe")
    s_act = ctx.sem("s_act")
    s_dve = ctx.sem("s_dve")
    s_c = ctx.sem("s_c")
    b12g, b12u, b3t = ctx.t["b12g"], ctx.t["b12u"], ctx.t["b3t"]

    with (
        nc.sbuf_tensor(f"w12g_sb_{uid()}", [128, 2, 32, 256], BF16) as wg,
        nc.sbuf_tensor(f"w12u_sb_{uid()}", [128, 2, 32, 256], BF16) as wu,
        nc.sbuf_tensor(f"sg1_sb_{uid()}", [128, 2, 512], BF16) as sg_sb,
        nc.sbuf_tensor(f"su1_sb_{uid()}", [128, 2, 512], BF16) as su_sb,
        nc.psum_tensor(f"h1_ps0_{uid()}", [128, 1024], F32) as h0,
        nc.psum_tensor(f"h1_ps1_{uid()}", [128, 1024], F32) as h1,
        nc.Block() as blk,
    ):
        hp = [h0, h1]
        pbase, abase, vbase = s_pe.v, s_act.v, s_dve.v
        lw = [ctx.sem("lw0"), ctx.sem("lw1")]
        lwb = [lw[0].v, lw[1].v]
        qs = [ctx.sem("qs0"), ctx.sem("qs1")]
        qs_tot = [qs[0].v, qs[1].v]

        @blk.sync
        def _(sy):
            sy.wait_ge(qs[0].h, qs_tot[0])
            sy.wait_ge(qs[1].h, qs_tot[1])
            sy.wait_ge(prep_wait[0].h, prep_wait[1])
            for og in range(16):
                if og >= 2:
                    sy.wait_ge(s_pe.h, pbase + 4 * (og - 1))
                for kc in range(KC):
                    lw[og % 2].inc(sy.dma_start(
                        wg[:, og % 2, kc, :],
                        dram["w12_bf"][256 * og:256 * og + 256, 128 * kc:128 * kc + 128],
                        transpose=True), 16)
                for kc in range(KC):
                    lw[og % 2].inc(sy.dma_start(
                        wu[:, og % 2, kc, :],
                        dram["w12_bf"][4096 + 256 * og:4096 + 256 * og + 256,
                                       128 * kc:128 * kc + 128],
                        transpose=True), 16)

        @blk.tensor
        def _(t):
            u = 0
            for og in range(16):
                t.wait_ge(lw[og % 2].h, lwb[og % 2] + 16 * 64 * (og // 2 + 1))
                for fo_ in range(2):
                    for th in range(2):
                        if u >= 2:
                            t.wait_ge(s_dve.h, vbase + (u - 1))
                        p = hp[u % 2]
                        for kc in range(KC):
                            t.matmul(p[:, 0:512],
                                     wg[:, og % 2, kc, 128 * fo_:128 * fo_ + 128],
                                     xn_sb[:, kc, 512 * th:512 * th + 512],
                                     start=(kc == 0), stop=(kc == KC - 1))
                        for kc in range(KC):
                            ins = t.matmul(p[:, 512:1024],
                                           wu[:, og % 2, kc, 128 * fo_:128 * fo_ + 128],
                                           xn_sb[:, kc, 512 * th:512 * th + 512],
                                           start=(kc == 0), stop=(kc == KC - 1))
                        s_pe.inc(ins)
                        u += 1

        @blk.scalar
        def _(sc):
            u = 0
            for og in range(16):
                for fo_ in range(2):
                    f = 2 * og + fo_
                    for th in range(2):
                        sc.wait_ge(s_pe.h, pbase + u + 1)
                        if u >= 2:
                            sc.wait_ge(s_dve.h, vbase + u - 1)
                        s_act.inc(sc.activation(sg_sb[:, u % 2, :], hp[u % 2][:, 0:512],
                                                AF.Silu, bias=b12g[:, f:f + 1]))
                        u += 1

        @blk.vector
        def _(v):
            u = 0
            for og in range(16):
                for fo_ in range(2):
                    f = 2 * og + fo_
                    for th in range(2):
                        v.wait_ge(s_act.h, abase + (u + 1))
                        s_dve.inc(v.scalar_tensor_tensor(
                            act_sb[:, f, 512 * th:512 * th + 512],
                            hp[u % 2][:, 512:1024], b12u[:, f:f + 1],
                            sg_sb[:, u % 2, :], OP.add, OP.mult))
                        u += 1

    act_done = ctx.sem("s_dve").v

    with (
        nc.sbuf_tensor(f"w3_sb_{uid()}", [128, 2, 32, 512], BF16) as w3,
        nc.sbuf_tensor(f"fo1_sb_{uid()}", [128, 2, 512], F32) as fo,
        nc.psum_tensor(f"fo1_ps0_{uid()}", [128, 512], F32) as fps0,
        nc.psum_tensor(f"fo1_ps1_{uid()}", [128, 512], F32) as fps1,
        nc.Block() as blk,
    ):
        fps = [fps0, fps1]
        pbase, abase = s_pe.v, s_act.v
        vbase2 = s_dve.v
        lw3 = [ctx.sem("lw3a"), ctx.sem("lw3b")]
        lw3b_ = [lw3[0].v, lw3[1].v]
        fs = [ctx.sem("fs0"), ctx.sem("fs1")]
        fsb = [fs[0].v, fs[1].v]

        @blk.sync
        def _(sy):
            for mg in range(8):
                if mg >= 2:
                    sy.wait_ge(s_pe.h, pbase + 8 * (mg - 1))
                for kc in range(KC):
                    lw3[mg % 2].inc(sy.dma_start(
                        w3[:, mg % 2, kc, :],
                        dram["w3_bf"][512 * mg:512 * mg + 512, 128 * kc:128 * kc + 128],
                        transpose=True), 16)

        @blk.tensor
        def _(t):
            t.wait_ge(s_dve.h, act_done)
            it = 0
            for mg in range(8):
                t.wait_ge(lw3[mg % 2].h, lw3b_[mg % 2] + 16 * 32 * (mg // 2 + 1))
                for mi in range(4):
                    for th in range(2):
                        if it >= 2:
                            t.wait_ge(s_dve.h, vbase2 + it - 1)
                        p = fps[it % 2]
                        for kc in range(KC):
                            ins = t.matmul(p[:, :],
                                           w3[:, mg % 2, kc, 128 * mi:128 * mi + 128],
                                           act_sb[:, kc, 512 * th:512 * th + 512],
                                           start=(kc == 0), stop=(kc == KC - 1))
                        s_pe.inc(ins)
                        it += 1

        @blk.vector
        def _(v):
            it = 0
            for mg in range(8):
                for mi in range(4):
                    m = 4 * mg + mi
                    for th in range(2):
                        v.wait_ge(s_pe.h, pbase + it + 1)
                        if it >= 2:
                            v.wait_ge(fs[it % 2].h, fsb[it % 2] + 16 * (it // 2))
                        s_dve.inc(v.tensor_scalar_add(fo[:, it % 2, :], fps[it % 2][:, :],
                                                      b3t[:, m:m + 1]))
                        it += 1

        @blk.scalar
        def _(sc):
            it = 0
            for mg in range(8):
                for mi in range(4):
                    for th in range(2):
                        m = 4 * mg + mi
                        sc.wait_ge(s_dve.h, vbase2 + it + 1)
                        fs[it % 2].inc(sc.dma_start(
                            ffn_dram[128 * m:128 * m + 128, 512 * th:512 * th + 512],
                            fo[:, it % 2, :]), 16)
                        it += 1


def emit_store_kv(ctx, k_sb, v_sb, k_dram, v_dram, which=("k", "v")):
    nc = ctx.nc
    kvst = ctx.sem("kvst")
    s_dve2 = ctx.sem("s_dve2")
    dve_done = s_dve2.v
    with nc.Block() as blk:
        @blk.scalar
        def _(sc):
            sc.wait_ge(s_dve2.h, dve_done)
            if "k" in which:
                kvst.inc(sc.dma_start(
                    k_dram.rearrange("(m p) t -> p m t", p=128), k_sb[:]), 16)
            if "v" in which:
                kvst.inc(sc.dma_start(
                    v_dram.rearrange("(s p) t -> p s t", p=128), v_sb[:]), 16)


def cum_pairs(g, s):
    return g * 144 + 2 * s * s + 2 * s


def emit_attention(ctx, layer, q_dram, k_full, v_full, o_all,
                   mask_wait=None, gather_wait=None):
    nc = ctx.nc
    s_kv = [ctx.sem("at_kv0"), ctx.sem("at_kv1")]
    s_q = [ctx.sem("at_q0"), ctx.sem("at_q1")]
    s_sp = ctx.sem("s_at_s")
    s_e = ctx.sem("s_at_e")
    s_m = ctx.sem("s_at_m")
    s_pv = ctx.sem("s_at_pv")
    s_n = ctx.sem("s_at_n")
    s_b = ctx.sem("s_at_b")
    mask = ctx.t["mask"]
    ones_col = ctx.t["ones_col"]
    ones_row = ctx.t["ones_row"]

    kvb = [s_kv[0].v, s_kv[1].v]
    qb = [s_q[0].v, s_q[1].v]
    sb, eb, mb_, pvb, nb, bb = (s_sp.v, s_e.v, s_m.v, s_pv.v, s_n.v, s_b.v)

    def P(g, s, p):
        return cum_pairs(g, s) + p

    def BI(g, s, p):
        return g * 32 + 4 * s + (p - 4 * s)

    with (
        nc.sbuf_tensor(f"at_kh_{uid()}", [128, 2, 8192], BF16) as kh,
        nc.sbuf_tensor(f"at_vh_{uid()}", [128, 2, 8192], BF16) as vh,
        nc.sbuf_tensor(f"at_qg_{uid()}", [128, 2, 512], BF16) as qg,
        nc.sbuf_tensor(f"at_e_{uid()}", [128, 2, 1024], BF16) as e_sb,
        nc.sbuf_tensor(f"at_t_{uid()}", [128, 2, 1024], F32) as t_sb,
        nc.sbuf_tensor(f"at_rrow_{uid()}", [1, 512], F32) as rrow,
        nc.sbuf_tensor(f"at_rsb_{uid()}", [128, 512], F32) as rsb,
        nc.psum_tensor(f"at_s0_{uid()}", [128, 1024], F32) as sp0,
        nc.psum_tensor(f"at_s1_{uid()}", [128, 1024], F32) as sp1,
        nc.psum_tensor(f"at_o_{uid()}", [128, 512], F32) as o_ps,
        nc.psum_tensor(f"at_d_{uid()}", [1, 512], F32) as d_ps,
        nc.psum_tensor(f"at_rb_{uid()}", [128, 512], F32) as rb_ps,
        nc.Block() as blk,
    ):
        sps = [sp0, sp1]

        @blk.sync
        def _(sy):
            if gather_wait is not None:
                sy.wait_ge(gather_wait[0].h, gather_wait[1])
            for g in range(NKV):
                if g >= 2:
                    sy.wait_ge(s_pv.h, pvb + (g - 1) * 144)
                for r in range(8):
                    src = k_full[1024 * r + 128 * g:1024 * r + 128 * g + 128, :] \
                        .rearrange("p (u q) -> p u q", q=128)
                    dst = kh[:, g % 2, :].rearrange("p (u r q) -> p r u q", r=8, q=128)[:, r, :, :]
                    s_kv[g % 2].inc(sy.dma_start(dst, src), 16)
                for r in range(8):
                    src = v_full[1024 * r:1024 * r + 1024, 128 * g:128 * g + 128] \
                        .rearrange("(u p) dv -> p u dv", p=128)
                    dst = vh[:, g % 2, :].rearrange("p (u r dv) -> p r u dv", r=8, dv=128)[:, r, :, :]
                    s_kv[g % 2].inc(sy.dma_start(dst, src), 16)
                for s in range(NSLOT):
                    gs = 8 * g + s
                    if gs >= 2:
                        pg, ps_ = divmod(gs - 2, 8)
                        sy.wait_ge(s_sp.h, sb + cum_pairs(pg, ps_) + 4 * ps_ + 4)
                    src = q_dram[512 * g:512 * g + 512, 128 * s:128 * s + 128] \
                        .rearrange("(h d) q -> d h q", d=128)
                    s_q[gs % 2].inc(sy.dma_start(
                        qg[:, gs % 2, :].rearrange("d (h q) -> d h q", q=128), src), 16)

        @blk.scalar
        def _(sc):
            for g in range(NKV):
                for s in range(NSLOT):
                    for p in range(4 * s + 4):
                        pi = P(g, s, p)
                        sc.wait_ge(s_sp.h, sb + pi + 1)
                        if pi >= 2:
                            sc.wait_ge(s_pv.h, pvb + pi - 1)
                        if layer == 0:
                            sc.activation(t_sb[:, pi % 2, :], sps[pi % 2][:, :], AF.Tanh,
                                          scale=ATTN_MULT / 30.0)
                            s_e.inc(sc.activation(e_sb[:, pi % 2, :], t_sb[:, pi % 2, :],
                                                  AF.Exp, scale=30.0,
                                                  bias=ctx.t["cm30"][:, 0:1]))
                        else:
                            s_e.inc(sc.activation(e_sb[:, pi % 2, :], sps[pi % 2][:, :],
                                                  AF.Exp, scale=ATTN_MULT))

        @blk.tensor
        def _(t):
            for g in range(NKV):
                for s in range(NSLOT):
                    gs = 8 * g + s
                    if s == 0:
                        t.wait_ge(s_kv[g % 2].h, kvb[g % 2] + 256 * (g // 2 + 1))
                    t.wait_ge(s_q[gs % 2].h, qb[gs % 2] + 16 * (gs // 2 + 1))
                    if gs >= 1:
                        t.wait_ge(s_n.h, nb + 3 * gs)
                    npair = 4 * s + 4
                    for p in range(npair):
                        pi = P(g, s, p)
                        if pi >= 2:
                            t.wait_ge(s_e.h, eb + pi - 1)
                        sp = sps[pi % 2]
                        for jj in range(2):
                            j = 2 * p + jj
                            ins = t.matmul(sp[:, 512 * jj:512 * jj + 512],
                                           kh[:, g % 2, 128 * j:128 * j + 128],
                                           qg[:, gs % 2, :],
                                           start=True, stop=True)
                        s_sp.inc(ins)
                        if p >= 4 * s:
                            t.wait_ge(s_m.h, mb_ + BI(g, s, p) + 1)
                        else:
                            t.wait_ge(s_e.h, eb + pi + 1)
                        for jj in range(2):
                            j = 2 * p + jj
                            t.matmul(o_ps[:, :],
                                     vh[:, g % 2, 128 * j:128 * j + 128],
                                     e_sb[:, pi % 2, 512 * jj:512 * jj + 512],
                                     start=(p == 0 and jj == 0),
                                     stop=(p == npair - 1 and jj == 1),
                                     skip_group_check=True)
                        for jj in range(2):
                            ins = t.matmul(d_ps[:, :], ones_col[:, :],
                                           e_sb[:, pi % 2, 512 * jj:512 * jj + 512],
                                           start=(p == 0 and jj == 0),
                                           stop=(p == npair - 1 and jj == 1),
                                           skip_group_check=True)
                        s_pv.inc(ins)
                    t.wait_ge(s_n.h, nb + 3 * gs + 1)
                    s_b.inc(t.matmul(rb_ps[:, :], ones_row[:, :], rrow[:, :],
                                     start=True, stop=True))

        @blk.vector
        def _(v):
            if mask_wait is not None:
                v.wait_ge(mask_wait[0].h, mask_wait[1])
            for g in range(NKV):
                for s in range(NSLOT):
                    gs = 8 * g + s
                    for p in range(4 * s, 4 * s + 4):
                        pi = P(g, s, p)
                        v.wait_ge(s_e.h, eb + pi + 1)
                        jb0 = 2 * p - 8 * s
                        s_m.inc(v.tensor_mul(e_sb[:, pi % 2, :], e_sb[:, pi % 2, :],
                                             mask[:, 512 * jb0:512 * jb0 + 1024]))
                    v.wait_ge(s_pv.h, pvb + cum_pairs(g, s) + 4 * s + 4)
                    s_n.inc(v.reciprocal(rrow[:, :], d_ps[:, :]))
                    v.wait_ge(s_b.h, bb + gs + 1)
                    s_n.inc(v.tensor_copy(rsb[:, :], rb_ps[:, :]))
                    dst = o_all[:, 4 * g:4 * g + 4, 128 * s:128 * s + 128]
                    s_n.inc(v.tensor_mul(
                        dst,
                        o_ps[:, :].rearrange("p (h q) -> p h q", q=128),
                        rsb[:, :].rearrange("p (h q) -> p h q", q=128)))

    return s_n.v


def emit_wo_epilogue(ctx, layer, dram, o_all, x_dram, ffn_dram, out_dram,
                     attn_done, prep_wait=None):
    nc = ctx.nc
    s_pe = ctx.sem("s_pe")
    s_dve = ctx.sem("s_dve")
    s_n = ctx.sem("s_at_n")
    wo_s = [ctx.sem("wo0s"), ctx.sem("wo1s")]
    xf = [ctx.sem("xf0"), ctx.sem("xf1")]
    eps = [ctx.sem("eps0"), ctx.sem("eps1")]
    fs = [ctx.sem("fs0"), ctx.sem("fs1")]

    with (
        nc.sbuf_tensor(f"wo_sb_{uid()}", [128, 2, 32, 512], BF16) as wo,
        nc.sbuf_tensor(f"ep_x_{uid()}", [128, 2, 512], F32) as xbuf,
        nc.sbuf_tensor(f"ep_f_{uid()}", [128, 2, 512], F32) as fbuf,
        nc.sbuf_tensor(f"ep_o_{uid()}", [128, 2, 512], F32) as obuf,
        nc.psum_tensor(f"ep_ps0_{uid()}", [128, 512], F32) as ps0,
        nc.psum_tensor(f"ep_ps1_{uid()}", [128, 512], F32) as ps1,
        nc.Block() as blk,
    ):
        ps = [ps0, ps1]
        pbase, vbase = s_pe.v, s_dve.v
        wob = [wo_s[0].v, wo_s[1].v]
        xfb = [xf[0].v, xf[1].v]
        epb = [eps[0].v, eps[1].v]
        fs_tot = [fs[0].v, fs[1].v]
        per_g = 16 if layer == 0 else 16 * 32

        if layer == 0:
            @blk.gpsimd
            def _(g):
                for mg in range(8):
                    if mg >= 2:
                        g.wait_ge(s_pe.h, pbase + 8 * (mg - 1))
                    wo_s[mg % 2].inc(g.dma_start(
                        wo[:, mg % 2, :, :],
                        dram["wo0"][:, 512 * mg:512 * mg + 512]
                        .rearrange("(kc p) m -> p kc m", p=128)), 16)

        @blk.sync
        def _(sy):
            if prep_wait is not None:
                sy.wait_ge(prep_wait[0].h, prep_wait[1])
            sy.wait_ge(fs[0].h, fs_tot[0])
            sy.wait_ge(fs[1].h, fs_tot[1])
            it = 0
            for mg in range(8):
                if layer == 1:
                    if mg >= 2:
                        sy.wait_ge(s_pe.h, pbase + 8 * (mg - 1))
                    for kc in range(KC):
                        wo_s[mg % 2].inc(sy.dma_start(
                            wo[:, mg % 2, kc, :],
                            dram["wo1_bf"][512 * mg:512 * mg + 512, 128 * kc:128 * kc + 128],
                            transpose=True), 16)
                for mi in range(4):
                    m = 4 * mg + mi
                    for th in range(2):
                        if it >= 2:
                            sy.wait_ge(s_dve.h, vbase + it - 1)
                        sl = slice(512 * th, 512 * th + 512)
                        xf[it % 2].inc(sy.dma_start(xbuf[:, it % 2, :],
                                                    x_dram[128 * m:128 * m + 128, sl]), 16)
                        xf[it % 2].inc(sy.dma_start(fbuf[:, it % 2, :],
                                                    ffn_dram[128 * m:128 * m + 128, sl]), 16)
                        it += 1

        @blk.tensor
        def _(t):
            t.wait_ge(s_n.h, attn_done)
            it = 0
            for mg in range(8):
                t.wait_ge(wo_s[mg % 2].h, wob[mg % 2] + per_g * (mg // 2 + 1))
                for mi in range(4):
                    for th in range(2):
                        if it >= 2:
                            t.wait_ge(s_dve.h, vbase + it - 1)
                        p = ps[it % 2]
                        for kc in range(KC):
                            ins = t.matmul(p[:, :],
                                           wo[:, mg % 2, kc, 128 * mi:128 * mi + 128],
                                           o_all[:, kc, 512 * th:512 * th + 512],
                                           start=(kc == 0), stop=(kc == KC - 1))
                        s_pe.inc(ins)
                        it += 1

        @blk.vector
        def _(v):
            it = 0
            for mg in range(8):
                for mi in range(4):
                    for th in range(2):
                        v.wait_ge(s_pe.h, pbase + it + 1)
                        v.wait_ge(xf[it % 2].h, xfb[it % 2] + 32 * (it // 2 + 1))
                        if it >= 2:
                            v.wait_ge(eps[it % 2].h, epb[it % 2] + 16 * (it // 2))
                        v.tensor_add(obuf[:, it % 2, :], ps[it % 2][:, :], xbuf[:, it % 2, :])
                        s_dve.inc(v.tensor_add(obuf[:, it % 2, :], obuf[:, it % 2, :],
                                               fbuf[:, it % 2, :]))
                        it += 1

        @blk.scalar
        def _(sc):
            it = 0
            for mg in range(8):
                for mi in range(4):
                    m = 4 * mg + mi
                    for th in range(2):
                        sc.wait_ge(s_dve.h, vbase + it + 1)
                        eps[it % 2].inc(sc.dma_start(
                            out_dram[128 * m:128 * m + 128, 512 * th:512 * th + 512],
                            obuf[:, it % 2, :]), 16)
                        it += 1


def emit_prep_casts(ctx, groups):
    nc = ctx.nc
    out = {}
    with nc.Block() as blk:
        @blk.gpsimd
        def _(g):
            for gname, pairs in groups:
                sem = ctx.sem("prep_" + gname)
                for src_, dst_ in pairs:
                    sem.inc(g.dma_start(dst_[:, :], src_[:, :]), 16)
                out[gname] = (sem, sem.v)
    return out




def build_pre(layer, upto='all'):
    nc = bass.Bass("TRN2", target_bir_lowering=False)
    E_IN, E_OUT = "ExternalInput", "ExternalOutput"
    d = {}
    d["x"] = nc.dram_tensor("x", [D, TPC], F32, kind=E_IN).ap()
    if layer == 0:
        for nm, sh in [("wq0", [D, D]), ("wk", [D, 1024]), ("wv", [D, 1024]),
                       ("u1", [D, 1024]), ("v1", [1024, 2 * INTER0]),
                       ("u2", [INTER0, 1024]), ("v2", [1024, D])]:
            d[nm] = nc.dram_tensor(nm, sh, F32, kind=E_IN).ap()
        bias_names = [("b1g", FCH), ("b1u", FCH), ("b2t", 32)]
    else:
        for nm, sh in [("wq1", [D, D]), ("wkr", [512, D]), ("wdk", [512, D]),
                       ("wupk", [512, 512]), ("wupv", [1024, 512]),
                       ("w12", [2 * D, D]), ("w3", [D, D])]:
            d[nm] = nc.dram_tensor(nm, sh, F32, kind=E_IN).ap()
        for nm in ["qcos_in", "qsin_in", "kcos_in", "ksin_in"]:
            d[nm] = nc.dram_tensor(nm, [128, TPC], F32, kind=E_IN).ap()
        for nm, sh in [("wq1_bf", [D, D]), ("wkr_bf", [512, D]), ("wdk_bf", [512, D]),
                       ("wupk_bf", [512, 512]), ("wupv_bf", [1024, 512]),
                       ("w12_bf", [2 * D, D]), ("w3_bf", [D, D])]:
            d[nm] = nc.dram_tensor(nm, sh, BF16, kind="Internal").ap()
        bias_names = [("b12g", 32), ("b12u", 32), ("b3t", 32)]
    for nm, w in bias_names:
        d[nm] = nc.dram_tensor(nm, [128, w], F32, kind=E_IN).ap()
    d["xn_dbg"] = nc.dram_tensor("xn_dbg", [D, TPC], BF16, kind=E_OUT).ap() \
        if upto == "ln" else None
    if upto in ("ffnp", "ffna"):
        d["ffn_dbg"] = nc.dram_tensor("ffn_dbg", [1024, TPC], BF16, kind=E_OUT).ap()
    elif upto == "ffnz":
        d["ffn_dbg"] = nc.dram_tensor("ffn_dbg", [1024, TPC], BF16, kind=E_OUT).ap()
    d["k_loc"] = nc.dram_tensor("k_loc", [1024, TPC], BF16, kind=E_OUT).ap()
    d["v_loc"] = nc.dram_tensor("v_loc", [1024, TPC], BF16, kind=E_OUT).ap()
    d["q_d"] = nc.dram_tensor("q_d", [D, TPC], BF16, kind=E_OUT).ap()
    d["ffn_d"] = nc.dram_tensor("ffn_d", [D, TPC], F32, kind=E_OUT).ap()

    with ExitStack() as stack:
        ctx = Ctx(nc, stack)
        ctx.t["ones_col"] = stack.enter_context(nc.sbuf_tensor(f"ones_col_{uid()}", [128, 1], BF16))
        ctx.t["ones_row"] = stack.enter_context(nc.sbuf_tensor(f"ones_row_{uid()}", [1, 128], F32))
        ctx.t["cm30"] = stack.enter_context(nc.sbuf_tensor(f"cm30_{uid()}", [128, 1], F32))
        xn_sb = stack.enter_context(nc.sbuf_tensor(f"xn_sb_{uid()}", [128, KC, TPC], BF16))
        for nm, w in bias_names:
            ctx.t[nm] = stack.enter_context(nc.sbuf_tensor(nm + "_sb", [128, w], F32))

        prep_vals = None
        if layer == 1:
            prep_vals = emit_prep_casts(ctx, [
                ("kv", [(d["wkr"], d["wkr_bf"]), (d["wdk"], d["wdk_bf"]),
                        (d["wupk"], d["wupk_bf"]), (d["wupv"], d["wupv_bf"])]),
                ("q", [(d["wq1"], d["wq1_bf"])]),
                ("ffn", [(d["w12"], d["w12_bf"]), (d["w3"], d["w3_bf"])])])
        emit_consts(ctx)
        s_g = ctx.sem("s_g")
        with nc.Block() as blk:
            @blk.gpsimd
            def _(g):
                for nm, w in bias_names:
                    s_g.inc(g.dma_start(ctx.t[nm][:, :], d[nm]), 16)
                g.wait_ge(s_g.h, s_g.v)

        emit_ln(ctx, d["x"], xn_sb)

        if upto == "ln":
            dbg = ctx.sem("dbg")
            with nc.Block() as blk:
                @blk.scalar
                def _(sc):
                    dbg.inc(sc.dma_start(
                        d["xn_dbg"].rearrange("(c p) t -> p c t", p=128), xn_sb[:]), 16)
                    sc.wait_ge(dbg.h, dbg.v)
            return nc

        with ExitStack() as kvstack:
            v_sb = kvstack.enter_context(nc.sbuf_tensor(f"v_sb_{uid()}", [128, 8, 1024], BF16))
            if layer == 1:
                for nm, key in [("qcos", "qcos_in"), ("qsin", "qsin_in"),
                                ("kcos", "kcos_in"), ("ksin", "ksin_in")]:
                    ctx.t[nm] = kvstack.enter_context(nc.sbuf_tensor(nm, [128, TPC], F32))
                with nc.Block() as blk:
                    @blk.gpsimd
                    def _(g):
                        for nm, key in [("qcos", "qcos_in"), ("qsin", "qsin_in"),
                                        ("kcos", "kcos_in"), ("ksin", "ksin_in")]:
                            s_g.inc(g.dma_start(ctx.t[nm][:, :], d[key]), 16)
                        g.wait_ge(s_g.h, s_g.v)
            if layer == 0:
                k_sb = kvstack.enter_context(nc.sbuf_tensor(f"k_sb_{uid()}", [128, 8, 1024], BF16))
                emit_kv_l0(ctx, d["wk"], d["wv"], xn_sb, k_sb, v_sb)
                emit_store_kv(ctx, k_sb, v_sb, d["k_loc"], d["v_loc"])
            else:
                emit_kv_l1(ctx, d, xn_sb, v_sb, d["k_loc"],
                           prep_wait=prep_vals["kv"])
                emit_store_kv(ctx, None, v_sb, None, d["v_loc"], which=("v",))

            if upto == "kv":
                return nc
            emit_q(ctx, layer, d, xn_sb, d["q_d"],
                   prep_wait=prep_vals["q"] if layer == 1 else None)
        if upto == "q":
            return nc

        if layer == 0:
            emit_ffn_l0(ctx, d, xn_sb, d["ffn_d"], upto=upto,
                        dbg_out=d.get("ffn_dbg"))
        else:
            with nc.sbuf_tensor(f"act1_sb_{uid()}", [128, KC, TPC], BF16) as act_sb:
                emit_ffn_l1(ctx, d, xn_sb, act_sb, d["ffn_d"],
                            prep_wait=prep_vals["ffn"])

        with nc.Block() as blk:
            waits = []
            for nm in ["kvst", "qs0", "qs1", "fs0", "fs1"]:
                if nm in ctx.sems:
                    waits.append((ctx.sems[nm].h, ctx.sems[nm].v))

            @blk.sync
            def _(sy):
                for h, v in waits:
                    sy.wait_ge(h, v)
    return nc


def build_attn(layer):
    nc = bass.Bass("TRN2", target_bir_lowering=False)
    E_IN, E_OUT = "ExternalInput", "ExternalOutput"
    d = {}
    d["x"] = nc.dram_tensor("x", [D, TPC], F32, kind=E_IN).ap()
    d["q_d"] = nc.dram_tensor("q_d", [D, TPC], BF16, kind=E_IN).ap()
    d["k_full"] = nc.dram_tensor("k_full", [SEQ, 1024], BF16, kind=E_IN).ap()
    d["v_full"] = nc.dram_tensor("v_full", [SEQ, 1024], BF16, kind=E_IN).ap()
    d["ffn_d"] = nc.dram_tensor("ffn_d", [D, TPC], F32, kind=E_IN).ap()
    d["mask_in"] = nc.dram_tensor("mask_in", [128, NSLOT * GSIZE * 128], BF16, kind=E_IN).ap()
    if layer == 0:
        d["wo0"] = nc.dram_tensor("wo0", [D, D], F32, kind=E_IN).ap()
    else:
        d["wo1"] = nc.dram_tensor("wo1", [D, D], F32, kind=E_IN).ap()
        d["wo1_bf"] = nc.dram_tensor("wo1_bf", [D, D], BF16, kind="Internal").ap()
    d["xo"] = nc.dram_tensor("xo", [D, TPC], F32, kind=E_OUT).ap()

    with ExitStack() as stack:
        ctx = Ctx(nc, stack)
        ctx.t["ones_col"] = stack.enter_context(nc.sbuf_tensor(f"ones_col_{uid()}", [128, 1], BF16))
        ctx.t["ones_row"] = stack.enter_context(nc.sbuf_tensor(f"ones_row_{uid()}", [1, 128], F32))
        ctx.t["cm30"] = stack.enter_context(nc.sbuf_tensor(f"cm30_{uid()}", [128, 1], F32))
        ctx.t["mask"] = stack.enter_context(
            nc.sbuf_tensor(f"mask_sb_{uid()}", [128, NSLOT * GSIZE * 128], BF16))
        o_all = stack.enter_context(nc.sbuf_tensor(f"o_all_{uid()}", [128, KC, TPC], BF16))

        prep_vals = None
        if layer == 1:
            prep_vals = emit_prep_casts(ctx, [("wo", [(d["wo1"], d["wo1_bf"])])])
        emit_consts(ctx)
        s_g = ctx.sem("s_g")
        with nc.Block() as blk:
            @blk.gpsimd
            def _(g):
                s_g.inc(g.dma_start(ctx.t["mask"][:, :], d["mask_in"]), 16)
                g.wait_ge(s_g.h, s_g.v)
        mask_done = s_g.v

        attn_done = emit_attention(ctx, layer, d["q_d"], d["k_full"], d["v_full"],
                                   o_all, mask_wait=(s_g, mask_done))
        emit_wo_epilogue(ctx, layer, d, o_all, d["x"], d["ffn_d"], d["xo"], attn_done,
                         prep_wait=prep_vals["wo"] if layer == 1 else None)
        with nc.Block() as blk:
            waits = []
            for nm in ["eps0", "eps1"]:
                if nm in ctx.sems:
                    waits.append((ctx.sems[nm].h, ctx.sems[nm].v))

            @blk.sync
            def _(sy):
                for h, v in waits:
                    sy.wait_ge(h, v)
    return nc




def token_cols(core):
    return np.concatenate([np.arange(128 * (8 * s + core), 128 * (8 * s + core) + 128)
                           for s in range(NSLOT)])


def host_constants(core):
    import ml_dtypes
    pos = token_cols(core).astype(np.float64)
    inv = ROPE_THETA ** (-np.arange(64, dtype=np.float64) / 64.0)
    ang = pos[None, :] * inv[:, None]
    qcos = np.empty((128, TPC), np.float32)
    qsin = np.empty((128, TPC), np.float32)
    qcos[0:64] = np.cos(ang)
    qcos[64:128] = np.cos(ang)
    qsin[0:64] = -np.sin(ang)
    qsin[64:128] = np.sin(ang)
    kcos = np.empty((128, TPC), np.float32)
    ksin = np.empty((128, TPC), np.float32)
    for h in range(4):
        kcos[32 * h:32 * h + 32] = np.cos(ang[:32])
        ksin[32 * h:32 * h + 32] = np.sin(ang[:32])
    k_idx = np.arange(128)
    q_idx = np.arange(128)
    mask = np.zeros((128, NSLOT, GSIZE, 128), np.float32)
    for jb in range(NSLOT):
        m = ((128 * jb + k_idx[:, None]) <= (128 * core + q_idx[None, :]))
        mask[:, jb, :, :] = m[:, None, :]
    return {
        "qcos_in": qcos, "qsin_in": qsin, "kcos_in": kcos, "ksin_in": ksin,
        "mask_in": np.ascontiguousarray(
            mask.reshape(128, NSLOT * GSIZE * 128).astype(ml_dtypes.bfloat16)),
    }


def _f32(a):
    return np.ascontiguousarray(np.asarray(a, dtype=np.float32))


def make_weight_maps(inputs):
    wkr = _f32(inputs["L1_attn_Wkr"])
    a_rows = np.concatenate([np.arange(64 * g, 64 * g + 32) for g in range(8)])
    b_rows = np.concatenate([np.arange(64 * g + 32, 64 * g + 64) for g in range(8)])
    wkr_perm = np.ascontiguousarray(np.concatenate([wkr[a_rows], wkr[b_rows]], axis=0))
    b1 = _f32(inputs["L0_ffn_b1"])
    b2 = _f32(inputs["L0_ffn_b2"])
    b12 = _f32(inputs["L1_ffn_b12"])
    b3 = _f32(inputs["L1_ffn_b3"])
    w_l0pre = {
        "wq0": _f32(inputs["L0_attn_Wq"]), "wk": _f32(inputs["L0_attn_Wk"]),
        "wv": _f32(inputs["L0_attn_Wv"]),
        "u1": _f32(inputs["L0_ffn_U1"]), "v1": _f32(inputs["L0_ffn_V1"]),
        "u2": _f32(inputs["L0_ffn_U2"]), "v2": _f32(inputs["L0_ffn_V2"]),
        "b1g": np.ascontiguousarray(b1[:INTER0].reshape(FCH, 128).T),
        "b1u": np.ascontiguousarray(b1[INTER0:].reshape(FCH, 128).T),
        "b2t": np.ascontiguousarray(b2.reshape(32, 128).T),
    }
    w_l1pre = {
        "wq1": _f32(inputs["L1_attn_Wq"]), "wkr": wkr_perm,
        "wdk": _f32(inputs["L1_attn_Wdk"]), "wupk": _f32(inputs["L1_attn_Wupk"]),
        "wupv": _f32(inputs["L1_attn_Wupv"]),
        "w12": _f32(inputs["L1_ffn_w12"]), "w3": _f32(inputs["L1_ffn_w3"]),
        "b12g": np.ascontiguousarray(b12[:D].reshape(32, 128).T),
        "b12u": np.ascontiguousarray(b12[D:].reshape(32, 128).T),
        "b3t": np.ascontiguousarray(b3.reshape(32, 128).T),
    }
    return w_l0pre, w_l1pre


def kernel_multilaunch(**inputs):
    cols = [token_cols(c) for c in range(NCORES)]
    consts = [host_constants(c) for c in range(NCORES)]
    xT = _f32(inputs["L0_attn_hidden_states"][0]).T
    x_cores = [np.ascontiguousarray(xT[:, cols[c]]) for c in range(NCORES)]
    w_l0pre, w_l1pre = make_weight_maps(inputs)

    nc = build_pre(0)
    in_maps = [dict(x=x_cores[c], **w_l0pre) for c in range(NCORES)]
    res = run_bass_kernel_spmd(nc, in_maps, core_ids=list(range(NCORES)))
    k_full = np.concatenate([res.results[c]["k_loc"] for c in range(NCORES)], axis=0)
    v_full = np.concatenate([res.results[c]["v_loc"] for c in range(NCORES)], axis=0)
    q_d = [res.results[c]["q_d"] for c in range(NCORES)]
    ffn_d = [res.results[c]["ffn_d"] for c in range(NCORES)]

    nc = build_attn(0)
    in_maps = [dict(x=x_cores[c], q_d=q_d[c], k_full=k_full, v_full=v_full,
                    ffn_d=ffn_d[c], mask_in=consts[c]["mask_in"],
                    wo0=_f32(inputs["L0_attn_Wo"])) for c in range(NCORES)]
    res = run_bass_kernel_spmd(nc, in_maps, core_ids=list(range(NCORES)))
    x_mid = [res.results[c]["xo"] for c in range(NCORES)]

    nc = build_pre(1)
    in_maps = [dict(x=x_mid[c], **w_l1pre,
                    qcos_in=consts[c]["qcos_in"], qsin_in=consts[c]["qsin_in"],
                    kcos_in=consts[c]["kcos_in"], ksin_in=consts[c]["ksin_in"])
               for c in range(NCORES)]
    res = run_bass_kernel_spmd(nc, in_maps, core_ids=list(range(NCORES)))
    k_full = np.concatenate([res.results[c]["k_loc"] for c in range(NCORES)], axis=0)
    v_full = np.concatenate([res.results[c]["v_loc"] for c in range(NCORES)], axis=0)
    q_d = [res.results[c]["q_d"] for c in range(NCORES)]
    ffn_d = [res.results[c]["ffn_d"] for c in range(NCORES)]

    nc = build_attn(1)
    in_maps = [dict(x=x_mid[c], q_d=q_d[c], k_full=k_full, v_full=v_full,
                    ffn_d=ffn_d[c], mask_in=consts[c]["mask_in"],
                    wo1=_f32(inputs["L1_attn_Wo"])) for c in range(NCORES)]
    res = run_bass_kernel_spmd(nc, in_maps, core_ids=list(range(NCORES)))

    out = np.empty((SEQ, D), np.float32)
    for c in range(NCORES):
        out[cols[c], :] = res.results[c]["xo"].T
    return out[None, :, :]


def emit_gather(ctx, k_loc_t, v_loc_t, k_full_t, v_full_t, tag):
    nc = ctx.nc
    s_coll = ctx.sem("s_coll")
    kvst = ctx.sem("kvst")
    kv_tot = kvst.v
    with nc.Block() as blk:
        @blk.gpsimd
        def _(g):
            g.wait_ge(kvst.h, kv_tot)
            s_coll.inc(g.collective_compute(
                "AllGather", OP.bypass,
                replica_groups=[list(range(NCORES))],
                ins=[k_loc_t.ap().opt()],
                outs=[k_full_t.ap().opt()]))
            s_coll.inc(g.collective_compute(
                "AllGather", OP.bypass,
                replica_groups=[list(range(NCORES))],
                ins=[v_loc_t.ap().opt()],
                outs=[v_full_t.ap().opt()]))
    return (s_coll, s_coll.v)


def build_fused():
    nc = bass.Bass("TRN2", target_bir_lowering=False)
    E_IN, E_OUT = "ExternalInput", "ExternalOutput"
    d = {}
    d["x"] = nc.dram_tensor("x", [D, TPC], F32, kind=E_IN).ap()
    for nm, sh in [("wq0", [D, D]), ("wk", [D, 1024]), ("wv", [D, 1024]),
                   ("u1", [D, 1024]), ("v1", [1024, 2 * INTER0]),
                   ("u2", [INTER0, 1024]), ("v2", [1024, D]), ("wo0", [D, D]),
                   ("wq1", [D, D]), ("wkr", [512, D]), ("wdk", [512, D]),
                   ("wupk", [512, 512]), ("wupv", [1024, 512]),
                   ("w12", [2 * D, D]), ("w3", [D, D]), ("wo1", [D, D])]:
        d[nm] = nc.dram_tensor(nm, sh, F32, kind=E_IN).ap()
    for nm, w in [("b1g", FCH), ("b1u", FCH), ("b2t", 32),
                  ("b12g", 32), ("b12u", 32), ("b3t", 32)]:
        d[nm] = nc.dram_tensor(nm, [128, w], F32, kind=E_IN).ap()
    for nm in ["qcos_in", "qsin_in", "kcos_in", "ksin_in"]:
        d[nm] = nc.dram_tensor(nm, [128, TPC], F32, kind=E_IN).ap()
    d["mask_in"] = nc.dram_tensor("mask_in", [128, NSLOT * GSIZE * 128], BF16, kind=E_IN).ap()
    for nm, sh in [("wq1_bf", [D, D]), ("wkr_bf", [512, D]), ("wdk_bf", [512, D]),
                   ("wupk_bf", [512, 512]), ("wupv_bf", [1024, 512]),
                   ("w12_bf", [2 * D, D]), ("w3_bf", [D, D]), ("wo1_bf", [D, D])]:
        d[nm] = nc.dram_tensor(nm, sh, BF16, kind="Internal").ap()
    k_loc_t = nc.dram_tensor("k_loc", [1024, TPC], BF16, kind="Internal")
    v_loc_t = nc.dram_tensor("v_loc", [1024, TPC], BF16, kind="Internal")
    k_full_t = nc.dram_tensor("k_full", [SEQ, 1024], BF16, kind="Internal")
    v_full_t = nc.dram_tensor("v_full", [SEQ, 1024], BF16, kind="Internal")
    d["k_loc"], d["v_loc"] = k_loc_t.ap(), v_loc_t.ap()
    d["k_full"], d["v_full"] = k_full_t.ap(), v_full_t.ap()
    d["q_d"] = nc.dram_tensor("q_d", [D, TPC], BF16, kind="Internal").ap()
    d["ffn_d"] = nc.dram_tensor("ffn_d", [D, TPC], F32, kind="Internal").ap()
    d["x_mid"] = nc.dram_tensor("x_mid", [D, TPC], F32, kind="Internal").ap()
    d["xo"] = nc.dram_tensor("xo", [D, TPC], F32, kind=E_OUT).ap()

    with ExitStack() as stack:
        ctx = Ctx(nc, stack)
        ctx.t["ones_col"] = stack.enter_context(nc.sbuf_tensor(f"ones_col_{uid()}", [128, 1], BF16))
        ctx.t["ones_row"] = stack.enter_context(nc.sbuf_tensor(f"ones_row_{uid()}", [1, 128], F32))
        ctx.t["cm30"] = stack.enter_context(nc.sbuf_tensor(f"cm30_{uid()}", [128, 1], F32))
        ctx.t["mask"] = stack.enter_context(
            nc.sbuf_tensor(f"mask_sb_{uid()}", [128, NSLOT * GSIZE * 128], BF16))
        for nm, w in [("b1g", FCH), ("b1u", FCH), ("b2t", 32),
                      ("b12g", 32), ("b12u", 32), ("b3t", 32)]:
            ctx.t[nm] = stack.enter_context(nc.sbuf_tensor(nm + "_sb", [128, w], F32))
        emit_consts(ctx)
        s_g = ctx.sem("s_g")
        with nc.Block() as blk:
            @blk.gpsimd
            def _(g):
                for nm in ["b1g", "b1u", "b2t", "b12g", "b12u", "b3t"]:
                    s_g.inc(g.dma_start(ctx.t[nm][:, :], d[nm]), 16)
                s_g.inc(g.dma_start(ctx.t["mask"][:, :], d["mask_in"]), 16)
                g.wait_ge(s_g.h, s_g.v)

        with ExitStack() as l0:
            xn_sb = l0.enter_context(nc.sbuf_tensor(f"xn_sb0_{uid()}", [128, KC, TPC], BF16))
            emit_ln(ctx, d["x"], xn_sb)
            with ExitStack() as kvs:
                v_sb = kvs.enter_context(nc.sbuf_tensor(f"v_sb0_{uid()}", [128, 8, 1024], BF16))
                k_sb = kvs.enter_context(nc.sbuf_tensor(f"k_sb0_{uid()}", [128, 8, 1024], BF16))
                emit_kv_l0(ctx, d["wk"], d["wv"], xn_sb, k_sb, v_sb)
                emit_store_kv(ctx, k_sb, v_sb, d["k_loc"], d["v_loc"])
                gw0 = emit_gather(ctx, k_loc_t, v_loc_t, k_full_t, v_full_t, 0)
                prep_vals = emit_prep_casts(ctx, [
                    ("kv", [(d["wkr"], d["wkr_bf"]), (d["wdk"], d["wdk_bf"]),
                            (d["wupk"], d["wupk_bf"]), (d["wupv"], d["wupv_bf"])]),
                    ("q", [(d["wq1"], d["wq1_bf"])]),
                    ("ffn", [(d["w12"], d["w12_bf"]), (d["w3"], d["w3_bf"])]),
                    ("wo", [(d["wo1"], d["wo1_bf"])])])
                emit_q(ctx, 0, d, xn_sb, d["q_d"])
            emit_ffn_l0(ctx, d, xn_sb, d["ffn_d"])
        with ExitStack() as l0a:
            o_all = l0a.enter_context(nc.sbuf_tensor(f"o_all0_{uid()}", [128, KC, TPC], BF16))
            attn_done = emit_attention(ctx, 0, d["q_d"], d["k_full"], d["v_full"],
                                       o_all, gather_wait=gw0)
            emit_wo_epilogue(ctx, 0, d, o_all, d["x"], d["ffn_d"], d["x_mid"], attn_done)

        with ExitStack() as l1:
            xn_sb = l1.enter_context(nc.sbuf_tensor(f"xn_sb1_{uid()}", [128, KC, TPC], BF16))
            eps0, eps1 = ctx.sem("eps0"), ctx.sem("eps1")
            e0v, e1v = eps0.v, eps1.v
            with nc.Block() as blk:
                @blk.sync
                def _(sy):
                    sy.wait_ge(eps0.h, e0v)
                    sy.wait_ge(eps1.h, e1v)
            emit_ln(ctx, d["x_mid"], xn_sb)
            with ExitStack() as kvs:
                v_sb = kvs.enter_context(nc.sbuf_tensor(f"v_sb1_{uid()}", [128, 8, 1024], BF16))
                for nm, key in [("qcos", "qcos_in"), ("qsin", "qsin_in"),
                                ("kcos", "kcos_in"), ("ksin", "ksin_in")]:
                    ctx.t[nm] = kvs.enter_context(nc.sbuf_tensor(nm, [128, TPC], F32))
                with nc.Block() as blk:
                    @blk.gpsimd
                    def _(g):
                        for nm, key in [("qcos", "qcos_in"), ("qsin", "qsin_in"),
                                        ("kcos", "kcos_in"), ("ksin", "ksin_in")]:
                            s_g.inc(g.dma_start(ctx.t[nm][:, :], d[key]), 16)
                        g.wait_ge(s_g.h, s_g.v)
                emit_kv_l1(ctx, d, xn_sb, v_sb, d["k_loc"], prep_wait=prep_vals["kv"])
                emit_store_kv(ctx, None, v_sb, None, d["v_loc"], which=("v",))
                gw1 = emit_gather(ctx, k_loc_t, v_loc_t, k_full_t, v_full_t, 1)
                emit_q(ctx, 1, d, xn_sb, d["q_d"], prep_wait=prep_vals["q"])
            with nc.sbuf_tensor(f"act1_sb_{uid()}", [128, KC, TPC], BF16) as act_sb:
                emit_ffn_l1(ctx, d, xn_sb, act_sb, d["ffn_d"], prep_wait=prep_vals["ffn"])
        with ExitStack() as l1a:
            o_all = l1a.enter_context(nc.sbuf_tensor(f"o_all1_{uid()}", [128, KC, TPC], BF16))
            attn_done = emit_attention(ctx, 1, d["q_d"], d["k_full"], d["v_full"],
                                       o_all, gather_wait=gw1)
            emit_wo_epilogue(ctx, 1, d, o_all, d["x_mid"], d["ffn_d"], d["xo"], attn_done,
                             prep_wait=prep_vals["wo"])

        with nc.Block() as blk:
            waits = []
            for nm in ["eps0", "eps1"]:
                waits.append((ctx.sems[nm].h, ctx.sems[nm].v))

            @blk.sync
            def _(sy):
                for h, v in waits:
                    sy.wait_ge(h, v)
    return nc


def kernel(**inputs):
    cols = [token_cols(c) for c in range(NCORES)]
    consts = [host_constants(c) for c in range(NCORES)]
    xT = _f32(inputs["L0_attn_hidden_states"][0]).T
    w0, w1 = make_weight_maps(inputs)
    nc = build_fused()
    in_maps = []
    for c in range(NCORES):
        m = dict(x=np.ascontiguousarray(xT[:, cols[c]]), **w0, **w1,
                 wo0=_f32(inputs["L0_attn_Wo"]), wo1=_f32(inputs["L1_attn_Wo"]),
                 mask_in=consts[c]["mask_in"])
        for nm in ["qcos_in", "qsin_in", "kcos_in", "ksin_in"]:
            m[nm] = consts[c][nm]
        in_maps.append(m)
    res = run_bass_kernel_spmd(nc, in_maps, core_ids=list(range(NCORES)))
    out = np.empty((SEQ, D), np.float32)
    for c in range(NCORES):
        out[cols[c], :] = res.results[c]["xo"].T
    return out[None, :, :]


# version 05
# speedup vs baseline: 1.1216x; 1.1216x over previous
import numpy as np
from contextlib import ExitStack

import concourse.bass as bass
import concourse.mybir as mybir
from concourse.bass_utils import run_bass_kernel_spmd

F32 = mybir.dt.float32
BF16 = mybir.dt.bfloat16
AF = mybir.ActivationFunctionType
OP = mybir.AluOpType

D = 4096
SEQ = 8192
NCORES = 8
TPC = 1024
NSLOT = 8
KC = 32
NKV = 8
GSIZE = 4
ATTN_MULT = 0.08838834764831845
INTER0 = 11008
FCH = 86
EPS = 1e-5
ROPE_THETA = 10000.0

_uid = [0]


def uid():
    _uid[0] += 1
    return _uid[0]


class Sem:
    def __init__(self, stack, nc, name):
        self.h = stack.enter_context(nc.semaphore(name))
        self.v = 0

    def inc(self, ins, k=1):
        ins.then_inc(self.h, k)
        self.v += k
        return ins


class Ctx:
    def __init__(self, nc, stack):
        self.nc = nc
        self.stack = stack
        self.sems = {}
        self.t = {}

    def sem(self, name):
        if name not in self.sems:
            self.sems[name] = Sem(self.stack, self.nc, name)
        return self.sems[name]




def emit_consts(ctx):
    nc = ctx.nc
    s_dve = ctx.sem("s_dve")
    with nc.Block() as blk:
        @blk.vector
        def _(v):
            v.memset(ctx.t["ones_col"][:, :], 1.0)
            v.memset(ctx.t["cm30"][:, :], -30.0)
            s_dve.inc(v.memset(ctx.t["ones_row"][:, :], 1.0))


def emit_ln(ctx, x_dram, xn_sb):
    nc = ctx.nc
    lnx = [ctx.sem("lnx0"), ctx.sem("lnx1")]
    s_act = ctx.sem("s_act")
    s_pe = ctx.sem("s_pe")
    s_dve = ctx.sem("s_dve")
    ones_col = ctx.t["ones_col"]
    ones_row = ctx.t["ones_row"]

    with (
        nc.sbuf_tensor(f"ln_xin_{uid()}", [128, 2, 1024], F32) as xin,
        nc.sbuf_tensor(f"ln_xb_{uid()}", [128, 2, 1024], BF16) as xb,
        nc.sbuf_tensor(f"ln_xq_{uid()}", [128, 2, 1024], BF16) as xq,
        nc.sbuf_tensor(f"ln_rows_{uid()}", [1, 8, 1024], F32) as rows,
        nc.sbuf_tensor(f"ln_mb_{uid()}", [128, 1024], F32) as mb,
        nc.sbuf_tensor(f"ln_rb_{uid()}", [128, 1024], F32) as rb,
    ):
        with (
            nc.psum_tensor(f"ln_sum_{uid()}", [1, 1024], F32) as sum_ps,
            nc.psum_tensor(f"ln_sq_{uid()}", [1, 1024], F32) as sq_ps,
            nc.Block() as blk,
        ):
            xb0 = [lnx[0].v, lnx[1].v]
            abase, pbase = s_act.v, s_pe.v

            @blk.sync
            def _(sy):
                for c in range(KC):
                    if c >= 2:
                        sy.wait_ge(s_act.h, abase + c - 1)
                    lnx[c % 2].inc(sy.dma_start(xin[:, c % 2, :],
                                                x_dram[128 * c:128 * c + 128, :]), 16)

            @blk.scalar
            def _(s):
                for c in range(KC):
                    s.wait_ge(lnx[c % 2].h, xb0[c % 2] + 16 * (c // 2 + 1))
                    if c >= 2:
                        s.wait_ge(s_pe.h, pbase + c - 1)
                    s.copy(xb[:, c % 2, :], xin[:, c % 2, :])
                    s_act.inc(s.activation(xq[:, c % 2, :], xin[:, c % 2, :], AF.Square))

            @blk.tensor
            def _(t):
                for c in range(KC):
                    t.wait_ge(s_act.h, abase + c + 1)
                    for h in range(2):
                        sl = slice(512 * h, 512 * h + 512)
                        t.matmul(sum_ps[:, sl], ones_col[:, :], xb[:, c % 2, sl],
                                 start=(c == 0), stop=(c == KC - 1))
                        ins = t.matmul(sq_ps[:, sl], ones_col[:, :], xq[:, c % 2, sl],
                                       start=(c == 0), stop=(c == KC - 1))
                    s_pe.inc(ins)

            pe_done = s_pe.v

            @blk.vector
            def _(v):
                v.wait_ge(s_pe.h, pe_done)
                m = rows[:, 0, :]
                v.tensor_scalar_mul(m, sum_ps[:, :], 1.0 / D)
                v.tensor_scalar_mul(rows[:, 1, :], sq_ps[:, :], 1.0 / D)
                v.tensor_mul(rows[:, 2, :], m, m)
                v.tensor_sub(rows[:, 3, :], rows[:, 1, :], rows[:, 2, :])
                s_dve.inc(v.tensor_scalar_add(rows[:, 6, :], rows[:, 3, :], float(EPS)))

            vdone = s_dve.v

            @blk.scalar
            def _(s):
                s.wait_ge(s_dve.h, vdone)
                s.activation(rows[:, 4, :], rows[:, 6, :], AF.Ln)
                s_act.inc(s.activation(rows[:, 5, :], rows[:, 4, :], AF.Exp, scale=-0.5))

            adone = s_act.v

        with (
            nc.psum_tensor(f"ln_mbp_{uid()}", [128, 1024], F32) as mb_ps,
            nc.psum_tensor(f"ln_rbp_{uid()}", [128, 1024], F32) as rb_ps,
            nc.Block() as blk,
        ):
            @blk.tensor
            def _(t):
                t.wait_ge(s_act.h, adone)
                for h in range(2):
                    sl = slice(512 * h, 512 * h + 512)
                    t.matmul(mb_ps[:, sl], ones_row[:, :], rows[:, 0, sl], start=True, stop=True)
                    ins = t.matmul(rb_ps[:, sl], ones_row[:, :], rows[:, 5, sl], start=True, stop=True)
                s_pe.inc(ins)

            bdone = s_pe.v

            @blk.vector
            def _(v):
                v.wait_ge(s_pe.h, bdone)
                v.tensor_copy(mb[:, :], mb_ps[:, :])
                s_dve.inc(v.tensor_copy(rb[:, :], rb_ps[:, :]))

            cdone = s_dve.v
            xb2 = [lnx[0].v, lnx[1].v]
            vbase2 = s_dve.v

            @blk.sync
            def _(sy):
                for c in range(KC):
                    if c >= 2:
                        sy.wait_ge(s_dve.h, vbase2 + (c - 1))
                    lnx[c % 2].inc(sy.dma_start(xin[:, c % 2, :],
                                                x_dram[128 * c:128 * c + 128, :]), 16)

            @blk.vector
            def _(v):
                v.wait_ge(s_dve.h, cdone)
                for c in range(KC):
                    v.wait_ge(lnx[c % 2].h, xb2[c % 2] + 16 * (c // 2 + 1))
                    v.tensor_sub(xin[:, c % 2, :], xin[:, c % 2, :], mb[:, :])
                    s_dve.inc(v.tensor_mul(xn_sb[:, c, :], xin[:, c % 2, :], rb[:, :]))


def _emit_jobs_inline(ctx, blk, ps, jobs, outs, pe_waits=None, dve_extra=None):
    s_pe = ctx.sem("s_pe")
    s_dve2 = ctx.sem("s_dve2")
    pebase, evbase = s_pe.v, s_dve2.v

    @blk.tensor
    def _(t):
        for it, (lhs_list, rhs_list) in enumerate(jobs):
            if pe_waits and it in pe_waits:
                for sem, val in pe_waits[it]:
                    t.wait_ge(sem.h, val)
            if it >= 2:
                t.wait_ge(s_dve2.h, evbase + it - 1)
            p = ps[it % 2]
            n = len(lhs_list)
            for k in range(n):
                ins = t.matmul(p[:, :], lhs_list[k], rhs_list[k],
                               start=(k == 0), stop=(k == n - 1))
            s_pe.inc(ins)

    @blk.vector
    def _(v):
        if dve_extra:
            for sem, val in dve_extra:
                v.wait_ge(sem.h, val)
        for it, out_ap in enumerate(outs):
            v.wait_ge(s_pe.h, pebase + it + 1)
            s_dve2.inc(v.tensor_copy(out_ap, ps[it % 2][:, :]))
    return s_pe.v, s_dve2.v


def emit_kv_l0(ctx, wk_dram, wv_dram, xn_sb, k_sb, v_sb):
    nc = ctx.nc
    with (
        nc.sbuf_tensor(f"wk_sb_{uid()}", [128, 32, 1024], BF16) as wk,
        nc.psum_tensor(f"kvps{uid()}a", [128, 512], F32) as p0,
        nc.psum_tensor(f"kvps{uid()}b", [128, 512], F32) as p1,
        nc.Block() as blk,
    ):
        ldk = ctx.sem("ld_wk")
        b0 = ldk.v

        @blk.gpsimd
        def _(g):
            ldk.inc(g.dma_start(wk[:], wk_dram.rearrange("p (kc m) -> p kc m", m=1024)), 16)

        jobs, outs = [], []
        for m in range(8):
            for th in range(2):
                jobs.append((
                    [wk[:, kc, 128 * m:128 * m + 128] for kc in range(KC)],
                    [xn_sb[:, kc, 512 * th:512 * th + 512] for kc in range(KC)],
                ))
                outs.append(k_sb[:, m, 512 * th:512 * th + 512])
        _emit_jobs_inline(ctx, blk, [p0, p1], jobs, outs, {0: [(ldk, b0 + 16)]})

    with (
        nc.sbuf_tensor(f"wv_sb_{uid()}", [128, 32, 1024], BF16) as wv,
        nc.psum_tensor(f"kvps{uid()}a", [128, 512], F32) as p0,
        nc.psum_tensor(f"kvps{uid()}b", [128, 512], F32) as p1,
        nc.Block() as blk,
    ):
        ldv = ctx.sem("ld_wv")
        b0 = ldv.v

        @blk.gpsimd
        def _(g):
            ldv.inc(g.dma_start(wv[:], wv_dram.rearrange("p (kc m) -> p kc m", m=1024)), 16)

        jobs, outs = [], []
        for s in range(8):
            for vh in range(2):
                jobs.append((
                    [xn_sb[:, kc, 128 * s:128 * s + 128] for kc in range(KC)],
                    [wv[:, kc, 512 * vh:512 * vh + 512] for kc in range(KC)],
                ))
                outs.append(v_sb[:, s, 512 * vh:512 * vh + 512])
        _emit_jobs_inline(ctx, blk, [p0, p1], jobs, outs, {0: [(ldv, b0 + 16)]})


def emit_kv_l1(ctx, dram, xn_sb, v_sb, k_dram):
    nc = ctx.nc
    s_s = ctx.sem("s_s")
    s_dve = ctx.sem("s_dve")
    s_dve2 = ctx.sem("s_dve2")
    s_c = ctx.sem("s_c")
    kcos, ksin = ctx.t["kcos"], ctx.t["ksin"]

    with (
        nc.sbuf_tensor(f"ckv_sb_{uid()}", [128, 4, 1024], BF16) as ckv,
        nc.sbuf_tensor(f"krot_sb_{uid()}", [128, 4, 1024], BF16) as krot,
        nc.sbuf_tensor(f"kc_sb_{uid()}", [128, 4, 1024], BF16) as kcs,
    ):
        with (
            nc.sbuf_tensor(f"wdkT_{uid()}", [128, 32, 512], BF16) as wdkT,
            nc.psum_tensor(f"kv1ps{uid()}a", [128, 512], F32) as p0,
            nc.psum_tensor(f"kv1ps{uid()}b", [128, 512], F32) as p1,
            nc.Block() as blk,
        ):
            lddk = ctx.sem("ld_wdk")
            sbase = lddk.v

            @blk.gpsimd
            def _(g):
                lddk.inc(g.dma_start(
                    wdkT[:], dram["wdkp"].rearrange("p (kc m) -> p kc m", m=512)), 16)

            jobs, outs = [], []
            for m in range(4):
                for th in range(2):
                    jobs.append((
                        [wdkT[:, kc, 128 * m:128 * m + 128] for kc in range(KC)],
                        [xn_sb[:, kc, 512 * th:512 * th + 512] for kc in range(KC)],
                    ))
                    outs.append(ckv[:, m, 512 * th:512 * th + 512])
            _emit_jobs_inline(ctx, blk, [p0, p1], jobs, outs,
                              {0: [(lddk, sbase + 16)]})

        with (
            nc.sbuf_tensor(f"wkrT_{uid()}", [128, 32, 512], BF16) as wkrT,
            nc.sbuf_tensor(f"kr_sb_{uid()}", [128, 4, 1024], F32) as kr,
            nc.sbuf_tensor(f"kv1_scr_{uid()}", [128, 2, 1024], F32) as scr,
            nc.psum_tensor(f"kv1ps{uid()}a", [128, 512], F32) as p0,
            nc.psum_tensor(f"kv1ps{uid()}b", [128, 512], F32) as p1,
            nc.Block() as blk,
        ):
            ldkr = ctx.sem("ld_wkr")
            sbase = ldkr.v

            @blk.gpsimd
            def _(g):
                ldkr.inc(g.dma_start(
                    wkrT[:], dram["wkrp"].rearrange("p (kc m) -> p kc m", m=512)), 16)

            jobs, outs = [], []
            for m in range(4):
                for th in range(2):
                    jobs.append((
                        [wkrT[:, kc, 128 * m:128 * m + 128] for kc in range(KC)],
                        [xn_sb[:, kc, 512 * th:512 * th + 512] for kc in range(KC)],
                    ))
                    outs.append(kr[:, m, 512 * th:512 * th + 512])
            _emit_jobs_inline(ctx, blk, [p0, p1], jobs, outs,
                              {0: [(ldkr, sbase + 16)]})

            @blk.vector
            def _(v):
                for j in range(2):
                    a = kr[:, j, :]
                    b = kr[:, j + 2, :]
                    v.tensor_mul(scr[:, 0, :], a, kcos[:, :])
                    v.tensor_mul(scr[:, 1, :], b, ksin[:, :])
                    v.tensor_sub(krot[:, j, :], scr[:, 0, :], scr[:, 1, :])
                    v.tensor_mul(scr[:, 0, :], b, kcos[:, :])
                    v.tensor_mul(scr[:, 1, :], a, ksin[:, :])
                    s_dve.inc(v.tensor_add(krot[:, j + 2, :], scr[:, 0, :], scr[:, 1, :]))

        rope_done = s_dve.v

        with (
            nc.sbuf_tensor(f"wupkT_{uid()}", [128, 4, 512], BF16) as wupkT,
            nc.sbuf_tensor(f"wupvT_{uid()}", [128, 4, 2, 512], BF16) as wupvT,
            nc.psum_tensor(f"kv1ps{uid()}a", [128, 512], F32) as p0,
            nc.psum_tensor(f"kv1ps{uid()}b", [128, 512], F32) as p1,
            nc.Block() as blk,
        ):
            ldupk = ctx.sem("ld_upk")
            ldupv = ctx.sem("ld_upv")
            bk0, bv0 = ldupk.v, ldupv.v

            @blk.gpsimd
            def _(g):
                ldupk.inc(g.dma_start(
                    wupkT[:], dram["wupkp"].rearrange("p (kc m) -> p kc m", m=512)), 16)
                ldupv.inc(g.dma_start(
                    wupvT[:], dram["wupvp"].rearrange("p (kc v m) -> p kc v m",
                                                      v=2, m=512)), 16)

            jobs2, outs2 = [], []
            for m in range(4):
                for th in range(2):
                    jobs2.append((
                        [wupkT[:, kc, 128 * m:128 * m + 128] for kc in range(4)],
                        [ckv[:, kc, 512 * th:512 * th + 512] for kc in range(4)],
                    ))
                    outs2.append(kcs[:, m, 512 * th:512 * th + 512])
            for s in range(8):
                for vhh in range(2):
                    jobs2.append((
                        [ckv[:, kc, 128 * s:128 * s + 128] for kc in range(4)],
                        [wupvT[:, kc, vhh, :] for kc in range(4)],
                    ))
                    outs2.append(v_sb[:, s, 512 * vhh:512 * vhh + 512])
            pe_waits2 = {0: [(ldupk, bk0 + 16)], 8: [(ldupv, bv0 + 16)]}
            _, dve_v2 = _emit_jobs_inline(ctx, blk, [p0, p1], jobs2, outs2, pe_waits2)

            kvst = ctx.sem("kvst")

            @blk.scalar
            def _(sc):
                sc.wait_ge(s_dve.h, rope_done)
                sc.wait_ge(s_dve2.h, dve_v2)
                for j in range(2):
                    for h in range(4):
                        g = 4 * j + h
                        kvst.inc(sc.dma_start(
                            k_dram[128 * g:128 * g + 32, :],
                            krot[32 * h:32 * h + 32, j, :]), 16)
                        kvst.inc(sc.dma_start(
                            k_dram[128 * g + 64:128 * g + 96, :],
                            krot[32 * h:32 * h + 32, j + 2, :]), 16)
                for m in range(4):
                    for gp in range(2):
                        g = 2 * m + gp
                        kvst.inc(sc.dma_start(
                            k_dram[128 * g + 32:128 * g + 64, :],
                            kcs[64 * gp:64 * gp + 32, m, :]), 16)
                        kvst.inc(sc.dma_start(
                            k_dram[128 * g + 96:128 * g + 128, :],
                            kcs[64 * gp + 32:64 * gp + 64, m, :]), 16)


def emit_q(ctx, layer, dram, xn_sb, q_dram):
    nc = ctx.nc
    s_pe = ctx.sem("s_pe")
    s_act = ctx.sem("s_act")
    s_dve = ctx.sem("s_dve")
    qw = [ctx.sem("qw0"), ctx.sem("qw1")]
    qs = [ctx.sem("qs0"), ctx.sem("qs1")]
    kvst = ctx.sem("kvst")
    NG = 8
    per_g = 16

    with (
        nc.sbuf_tensor(f"wq_sb_{uid()}", [128, 2, 32, 512], BF16) as wq,
        nc.sbuf_tensor(f"q_out_{uid()}", [128, 2, 1024], BF16) as qo,
        nc.sbuf_tensor(f"q_sw_{uid()}", [128, 2, 1024], F32) as qsw,
        nc.sbuf_tensor(f"q_t1_{uid()}", [128, 1024], F32) as qt1,
        nc.sbuf_tensor(f"q_t2_{uid()}", [128, 1024], F32) as qt2,
        nc.psum_tensor(f"q_ps0_{uid()}", [128, 1024], F32) as qps0,
        nc.psum_tensor(f"q_ps1_{uid()}", [128, 1024], F32) as qps1,
        nc.Block() as blk,
    ):
        qps = [qps0, qps1]
        pbase, abase, vbase = s_pe.v, s_act.v, s_dve.v
        qwb = [qw[0].v, qw[1].v]
        qsb = [qs[0].v, qs[1].v]
        kvst_tot = kvst.v

        wq_src = dram["wq0p"] if layer == 0 else dram["wq1p"]

        @blk.gpsimd
        def _(g):
            g.wait_ge(kvst.h, kvst_tot)
            for mg in range(NG):
                if mg >= 2:
                    g.wait_ge(s_pe.h, pbase + 4 * (mg - 1))
                qw[mg % 2].inc(g.dma_start(
                    wq[:, mg % 2, :, :],
                    wq_src[:, 16384 * mg:16384 * mg + 16384]
                    .rearrange("p (kc m) -> p kc m", m=512)), 16)

        @blk.tensor
        def _(t):
            it = 0
            for mg in range(NG):
                t.wait_ge(qw[mg % 2].h, qwb[mg % 2] + per_g * (mg // 2 + 1))
                for mi in range(4):
                    if it >= 2:
                        t.wait_ge(s_dve.h, vbase + it - 1)
                    p = qps[it % 2]
                    for th in range(2):
                        for kc in range(KC):
                            ins = t.matmul(p[:, 512 * th:512 * th + 512],
                                           wq[:, mg % 2, kc, 128 * mi:128 * mi + 128],
                                           xn_sb[:, kc, 512 * th:512 * th + 512],
                                           start=(kc == 0), stop=(kc == KC - 1))
                    s_pe.inc(ins)
                    it += 1

        if layer == 1:
            qcos, qsin = ctx.t["qcos"], ctx.t["qsin"]

            @blk.scalar
            def _(sc):
                for it in range(KC):
                    sc.wait_ge(s_pe.h, pbase + it + 1)
                    if it >= 2:
                        sc.wait_ge(s_dve.h, vbase + it - 1)
                    p = qps[it % 2]
                    sc.copy(qsw[0:64, it % 2, :], p[64:128, :])
                    s_act.inc(sc.activation(qsw[64:128, it % 2, :], p[0:64, :], AF.Copy))
                    if it >= 2:
                        jt = it - 2
                        qs[jt % 2].inc(sc.dma_start(q_dram[128 * jt:128 * jt + 128, :],
                                                    qo[:, jt % 2, :]), 16)
                for jt in (KC - 2, KC - 1):
                    sc.wait_ge(s_dve.h, vbase + jt + 1)
                    qs[jt % 2].inc(sc.dma_start(q_dram[128 * jt:128 * jt + 128, :],
                                                qo[:, jt % 2, :]), 16)

            @blk.vector
            def _(v):
                for it in range(KC):
                    v.wait_ge(s_act.h, abase + it + 1)
                    if it >= 2:
                        v.wait_ge(qs[it % 2].h, qsb[it % 2] + 16 * (it // 2))
                    p = qps[it % 2]
                    v.tensor_mul(qt1[:, :], p[:, :], qcos[:, :])
                    v.tensor_mul(qt2[:, :], qsw[:, it % 2, :], qsin[:, :])
                    s_dve.inc(v.tensor_add(qo[:, it % 2, :], qt1[:, :], qt2[:, :]))
        else:
            @blk.vector
            def _(v):
                for it in range(KC):
                    v.wait_ge(s_pe.h, pbase + it + 1)
                    if it >= 2:
                        v.wait_ge(qs[it % 2].h, qsb[it % 2] + 16 * (it // 2))
                    s_dve.inc(v.tensor_copy(qo[:, it % 2, :], qps[it % 2][:, :]))

            @blk.scalar
            def _(sc):
                for it in range(KC):
                    sc.wait_ge(s_dve.h, vbase + it + 1)
                    qs[it % 2].inc(sc.dma_start(q_dram[128 * it:128 * it + 128, :],
                                                qo[:, it % 2, :]), 16)


def emit_ffn_l0(ctx, dram, xn_sb, ffn_dram, upto='all', dbg_out=None):
    nc = ctx.nc
    s_g = ctx.sem("s_g")
    s_pe = ctx.sem("s_pe")
    s_pe2 = ctx.sem("s_pe2")
    s_act = ctx.sem("s_act")
    s_dve = ctx.sem("s_dve")
    s_dve2 = ctx.sem("s_dve2")
    s_c = ctx.sem("s_c")
    b1g, b1u, b2t = ctx.t["b1g"], ctx.t["b1u"], ctx.t["b2t"]

    with nc.sbuf_tensor(f"p_sb_{uid()}", [128, 8, 1024], BF16) as p_sb:
        with (
            nc.sbuf_tensor(f"u1_sb_{uid()}", [128, 32, 1024], BF16) as u1,
            nc.psum_tensor(f"pps{uid()}a", [128, 512], F32) as pp0,
            nc.psum_tensor(f"pps{uid()}b", [128, 512], F32) as pp1,
            nc.Block() as blk,
        ):
            ldu1 = ctx.sem("ld_u1")
            gbase = ldu1.v
            qs = [ctx.sem("qs0"), ctx.sem("qs1")]
            qs_tot = [qs[0].v, qs[1].v]

            @blk.gpsimd
            def _(g):
                g.wait_ge(qs[0].h, qs_tot[0])
                g.wait_ge(qs[1].h, qs_tot[1])
                ldu1.inc(g.dma_start(u1[:], dram["u1p"].rearrange("p (kc m) -> p kc m", m=1024)), 16)

            jobs, outs = [], []
            for m in range(8):
                for th in range(2):
                    jobs.append((
                        [u1[:, kc, 128 * m:128 * m + 128] for kc in range(KC)],
                        [xn_sb[:, kc, 512 * th:512 * th + 512] for kc in range(KC)],
                    ))
                    outs.append(p_sb[:, m, 512 * th:512 * th + 512])
            _emit_jobs_inline(ctx, blk, [pp0, pp1], jobs, outs, {0: [(ldu1, gbase + 16)]})

        if upto == "ffnp":
            dbg = ctx.sem("dbg")
            with nc.Block() as blk:
                @blk.scalar
                def _(sc):
                    dbg.inc(sc.dma_start(
                        dbg_out.rearrange("(c p) t -> p c t", p=128), p_sb[:]), 16)
                    sc.wait_ge(dbg.h, dbg.v)
            return

        fblocks = [list(range(4 * b, min(4 * b + 4, FCH))) for b in range((FCH + 3) // 4)]
        NB = len(fblocks)
        cum_units = [0]
        for fb in fblocks:
            cum_units.append(cum_units[-1] + 2 * len(fb))

        def par_load_val(b):
            return 16 * (b // 2 + 1)

        with (
            nc.sbuf_tensor(f"ffn0w_sb_{uid()}", [128, 2, 4, 3072], BF16) as ffn0_w,
            nc.sbuf_tensor(f"act0_sb_{uid()}", [128, 2, 4, 1024], BF16) as act_sb,
            nc.sbuf_tensor(f"sg_sb_{uid()}", [128, 2, 512], BF16) as sg_sb,
            nc.sbuf_tensor(f"su_sb_{uid()}", [128, 2, 512], BF16) as su_sb,
            nc.sbuf_tensor(f"z_sb_{uid()}", [128, 8192], F32) as z_sb,
            nc.sbuf_tensor(f"zb_sb_{uid()}", [128, 8, 1024], BF16) as zb_sb,
            nc.psum_tensor(f"hg_ps0_{uid()}", [128, 1024], F32) as hg0,
            nc.psum_tensor(f"hg_ps1_{uid()}", [128, 1024], F32) as hg1,
            nc.psum_tensor(f"z_ps_{uid()}", [128, 2048], F32) as z_ps,
            nc.Block() as blk,
        ):
            hg = [hg0, hg1]
            ldv = [ctx.sem("ldv0"), ctx.sem("ldv1")]
            ldvb = [ldv[0].v, ldv[1].v]
            pbase = s_pe.v
            p2base = s_pe2.v
            abase = s_act.v
            vbase = s_dve.v
            v2base = s_dve2.v

            @blk.vector
            def _(v):
                s_dve2.inc(v.memset(z_sb[:, :], 0.0))

            @blk.gpsimd
            def _(g):
                for b, fb in enumerate(fblocks):
                    if b >= 2:
                        g.wait_ge(s_pe2.h, p2base + 4 * (b - 1))
                    nf = len(fb)
                    src = dram["ffn0p"][:, 3072 * 4 * b:3072 * (4 * b + nf)] \
                        .rearrange("p (f x) -> p f x", x=3072)
                    ldv[b % 2].inc(g.dma_start(ffn0_w[:, b % 2, 0:nf, :], src), 16)

            def z_group(t, bb, tq):
                t.wait_ge(s_dve.h, vbase + cum_units[bb + 1])
                t.wait_ge(s_dve2.h, v2base + 1 + 4 * bb + tq)
                nf = len(fblocks[bb])
                for fi, f in enumerate(fblocks[bb]):
                    for m in range(8):
                        ins = t.matmul(
                            z_ps[:, 256 * m:256 * m + 256],
                            ffn0_w[:, bb % 2, fi, 2048 + 128 * m:2048 + 128 * m + 128],
                            act_sb[:, bb % 2, fi, 256 * tq:256 * tq + 256],
                            start=(fi == 0 and m % 2 == 0), stop=(fi == nf - 1),
                            skip_group_check=True)
                s_pe2.inc(ins)

            @blk.tensor
            def _(t):
                for b, fb in enumerate(fblocks):
                    t.wait_ge(ldv[b % 2].h, ldvb[b % 2] + par_load_val(b))
                    for fi, f in enumerate(fb):
                        for th in range(2):
                            u = cum_units[b] + 2 * fi + th
                            if u >= 2:
                                t.wait_ge(s_dve.h, vbase + (u - 1))
                            hp = hg[u % 2]
                            for kc in range(8):
                                t.matmul(hp[:, 0:512],
                                         ffn0_w[:, b % 2, fi, 128 * kc:128 * kc + 128],
                                         p_sb[:, kc, 512 * th:512 * th + 512],
                                         start=(kc == 0), stop=(kc == 7))
                            for kc in range(8):
                                ins = t.matmul(hp[:, 512:1024],
                                               ffn0_w[:, b % 2, fi,
                                                      1024 + 128 * kc:1024 + 128 * kc + 128],
                                               p_sb[:, kc, 512 * th:512 * th + 512],
                                               start=(kc == 0), stop=(kc == 7))
                            s_pe.inc(ins)
                    if b >= 1:
                        for tq in range(4):
                            z_group(t, b - 1, tq)
                for tq in range(4):
                    z_group(t, NB - 1, tq)

            @blk.scalar
            def _(sc):
                for b, fb in enumerate(fblocks):
                    for fi, f in enumerate(fb):
                        for th in range(2):
                            u = cum_units[b] + 2 * fi + th
                            sc.wait_ge(s_pe.h, pbase + u + 1)
                            if u >= 2:
                                sc.wait_ge(s_dve.h, vbase + u - 1)
                            s_act.inc(sc.activation(sg_sb[:, u % 2, :], hg[u % 2][:, 0:512],
                                                    AF.Silu, bias=b1g[:, f:f + 1]))

            @blk.vector
            def _(v):
                for b, fb in enumerate(fblocks):
                    for fi, f in enumerate(fb):
                        for th in range(2):
                            u = cum_units[b] + 2 * fi + th
                            v.wait_ge(s_act.h, abase + (u + 1))
                            if b >= 2:
                                v.wait_ge(s_pe2.h, p2base + 4 * (b - 1))
                            s_dve.inc(v.scalar_tensor_tensor(
                                act_sb[:, b % 2, fi, 512 * th:512 * th + 512],
                                hg[u % 2][:, 512:1024], b1u[:, f:f + 1],
                                sg_sb[:, u % 2, :], OP.add, OP.mult))
                    if b >= 1:
                        for tq in range(4):
                            v.wait_ge(s_pe2.h, p2base + 4 * (b - 1) + tq + 1)
                            dst = z_sb[:, :].rearrange("p (m x) -> p m x", x=1024)[
                                :, :, 256 * tq:256 * tq + 256]
                            src = z_ps[:, :].rearrange("p (m x) -> p m x", x=256)
                            s_dve2.inc(v.tensor_add(dst, dst, src))
                for tq in range(4):
                    v.wait_ge(s_pe2.h, p2base + 4 * (NB - 1) + tq + 1)
                    dst = z_sb[:, :].rearrange("p (m x) -> p m x", x=1024)[
                        :, :, 256 * tq:256 * tq + 256]
                    src = z_ps[:, :].rearrange("p (m x) -> p m x", x=256)
                    s_dve2.inc(v.tensor_add(dst, dst, src))
                zv = z_sb[:, :].rearrange("p (m x) -> p m x", x=1024)
                for m in range(8):
                    s_dve.inc(v.tensor_copy(zb_sb[:, m, :], zv[:, m, :]))

            zcast_done = s_dve.v

        if upto == "ffna":
            dbg = ctx.sem("dbg")
            with nc.Block() as blk:
                ad = s_dve.v

                @blk.scalar
                def _(sc):
                    sc.wait_ge(s_dve.h, ad)
                    dbg.inc(sc.dma_start(
                        dbg_out[0:512].rearrange("(c p) t -> p c t", p=128),
                        act_sb[:, 0, :, :]), 16)
                    sc.wait_ge(dbg.h, dbg.v)
            return

        if upto == "ffnz":
            dbg = ctx.sem("dbg")
            with nc.Block() as blk:
                zd = s_dve.v

                @blk.scalar
                def _(sc):
                    sc.wait_ge(s_dve.h, zd)
                    dbg.inc(sc.dma_start(
                        dbg_out.rearrange("(c p) t -> p c t", p=128), zb_sb[:]), 16)
                    sc.wait_ge(dbg.h, dbg.v)
            return

        with (
            nc.sbuf_tensor(f"v2_sb_{uid()}", [128, 2, 8, 1024], BF16) as v2,
            nc.sbuf_tensor(f"fo_sb_{uid()}", [128, 2, 512], F32) as fo,
            nc.psum_tensor(f"fo_ps0_{uid()}", [128, 512], F32) as fps0,
            nc.psum_tensor(f"fo_ps1_{uid()}", [128, 512], F32) as fps1,
            nc.Block() as blk,
        ):
            fps = [fps0, fps1]
            pbase, abase = s_pe.v, s_act.v
            vbase_f = s_dve.v
            ldv2 = [ctx.sem("ldv2a"), ctx.sem("ldv2b")]
            ldv2b_ = [ldv2[0].v, ldv2[1].v]
            fs = [ctx.sem("fs0"), ctx.sem("fs1")]
            fsb = [fs[0].v, fs[1].v]

            @blk.gpsimd
            def _(g):
                for mg in range(4):
                    if mg >= 2:
                        g.wait_ge(s_pe.h, pbase + 16 * (mg - 1))
                    ldv2[mg % 2].inc(g.dma_start(
                        v2[:, mg % 2, :, :],
                        dram["v2p"][:, 8192 * mg:8192 * mg + 8192]
                        .rearrange("p (kc m) -> p kc m", m=1024)), 16)

            @blk.tensor
            def _(t):
                t.wait_ge(s_dve.h, zcast_done)
                it = 0
                for mg in range(4):
                    t.wait_ge(ldv2[mg % 2].h, ldv2b_[mg % 2] + 16 * (mg // 2 + 1))
                    for mi in range(8):
                        for th in range(2):
                            if it >= 2:
                                t.wait_ge(s_dve.h, vbase_f + it - 1)
                            p = fps[it % 2]
                            for kc in range(8):
                                ins = t.matmul(p[:, :],
                                               v2[:, mg % 2, kc, 128 * mi:128 * mi + 128],
                                               zb_sb[:, kc, 512 * th:512 * th + 512],
                                               start=(kc == 0), stop=(kc == 7))
                            s_pe.inc(ins)
                            it += 1

            vbase_c = s_dve.v

            @blk.vector
            def _(v):
                it = 0
                for mg in range(4):
                    for mi in range(8):
                        m = 8 * mg + mi
                        for th in range(2):
                            v.wait_ge(s_pe.h, pbase + it + 1)
                            if it >= 2:
                                v.wait_ge(fs[it % 2].h, fsb[it % 2] + 16 * (it // 2))
                            s_dve.inc(v.tensor_scalar_add(fo[:, it % 2, :], fps[it % 2][:, :],
                                                          b2t[:, m:m + 1]))
                            it += 1

            @blk.scalar
            def _(sc):
                it = 0
                for mg in range(4):
                    for mi in range(8):
                        m = 8 * mg + mi
                        for th in range(2):
                            sc.wait_ge(s_dve.h, vbase_c + it + 1)
                            fs[it % 2].inc(sc.dma_start(
                                ffn_dram[128 * m:128 * m + 128, 512 * th:512 * th + 512],
                                fo[:, it % 2, :]), 16)
                            it += 1


def emit_ffn_l1(ctx, dram, xn_sb, act_sb, ffn_dram):
    nc = ctx.nc
    s_s = ctx.sem("s_s")
    s_pe = ctx.sem("s_pe")
    s_act = ctx.sem("s_act")
    s_dve = ctx.sem("s_dve")
    s_c = ctx.sem("s_c")
    b12g, b12u, b3t = ctx.t["b12g"], ctx.t["b12u"], ctx.t["b3t"]

    with (
        nc.sbuf_tensor(f"w12_sb_{uid()}", [128, 2, 2, 32, 256], BF16) as w12sb,
        nc.sbuf_tensor(f"sg1_sb_{uid()}", [128, 2, 512], BF16) as sg_sb,
        nc.sbuf_tensor(f"su1_sb_{uid()}", [128, 2, 512], BF16) as su_sb,
        nc.psum_tensor(f"h1_ps0_{uid()}", [128, 1024], F32) as h0,
        nc.psum_tensor(f"h1_ps1_{uid()}", [128, 1024], F32) as h1,
        nc.Block() as blk,
    ):
        hp = [h0, h1]
        pbase, abase, vbase = s_pe.v, s_act.v, s_dve.v
        lw = [ctx.sem("lw0"), ctx.sem("lw1")]
        lwb = [lw[0].v, lw[1].v]
        qs = [ctx.sem("qs0"), ctx.sem("qs1")]
        qs_tot = [qs[0].v, qs[1].v]

        @blk.gpsimd
        def _(g):
            g.wait_ge(qs[0].h, qs_tot[0])
            g.wait_ge(qs[1].h, qs_tot[1])
            for og in range(16):
                if og >= 2:
                    g.wait_ge(s_pe.h, pbase + 4 * (og - 1))
                lw[og % 2].inc(g.dma_start(
                    w12sb[:, og % 2, :, :, :],
                    dram["w12p"][:, 16384 * og:16384 * og + 16384]
                    .rearrange("p (gu kc m) -> p gu kc m", gu=2, m=256)), 16)

        @blk.tensor
        def _(t):
            u = 0
            for og in range(16):
                t.wait_ge(lw[og % 2].h, lwb[og % 2] + 16 * (og // 2 + 1))
                for fo_ in range(2):
                    for th in range(2):
                        if u >= 2:
                            t.wait_ge(s_dve.h, vbase + (u - 1))
                        p = hp[u % 2]
                        for kc in range(KC):
                            t.matmul(p[:, 0:512],
                                     w12sb[:, og % 2, 0, kc, 128 * fo_:128 * fo_ + 128],
                                     xn_sb[:, kc, 512 * th:512 * th + 512],
                                     start=(kc == 0), stop=(kc == KC - 1))
                        for kc in range(KC):
                            ins = t.matmul(p[:, 512:1024],
                                           w12sb[:, og % 2, 1, kc, 128 * fo_:128 * fo_ + 128],
                                           xn_sb[:, kc, 512 * th:512 * th + 512],
                                           start=(kc == 0), stop=(kc == KC - 1))
                        s_pe.inc(ins)
                        u += 1

        @blk.scalar
        def _(sc):
            u = 0
            for og in range(16):
                for fo_ in range(2):
                    f = 2 * og + fo_
                    for th in range(2):
                        sc.wait_ge(s_pe.h, pbase + u + 1)
                        if u >= 2:
                            sc.wait_ge(s_dve.h, vbase + u - 1)
                        s_act.inc(sc.activation(sg_sb[:, u % 2, :], hp[u % 2][:, 0:512],
                                                AF.Silu, bias=b12g[:, f:f + 1]))
                        u += 1

        @blk.vector
        def _(v):
            u = 0
            for og in range(16):
                for fo_ in range(2):
                    f = 2 * og + fo_
                    for th in range(2):
                        v.wait_ge(s_act.h, abase + (u + 1))
                        s_dve.inc(v.scalar_tensor_tensor(
                            act_sb[:, f, 512 * th:512 * th + 512],
                            hp[u % 2][:, 512:1024], b12u[:, f:f + 1],
                            sg_sb[:, u % 2, :], OP.add, OP.mult))
                        u += 1

    act_done = ctx.sem("s_dve").v

    with (
        nc.sbuf_tensor(f"w3_sb_{uid()}", [128, 2, 32, 512], BF16) as w3,
        nc.sbuf_tensor(f"fo1_sb_{uid()}", [128, 2, 512], F32) as fo,
        nc.psum_tensor(f"fo1_ps0_{uid()}", [128, 512], F32) as fps0,
        nc.psum_tensor(f"fo1_ps1_{uid()}", [128, 512], F32) as fps1,
        nc.Block() as blk,
    ):
        fps = [fps0, fps1]
        pbase, abase = s_pe.v, s_act.v
        vbase2 = s_dve.v
        lw3 = [ctx.sem("lw3a"), ctx.sem("lw3b")]
        lw3b_ = [lw3[0].v, lw3[1].v]
        fs = [ctx.sem("fs0"), ctx.sem("fs1")]
        fsb = [fs[0].v, fs[1].v]

        @blk.gpsimd
        def _(g):
            for mg in range(8):
                if mg >= 2:
                    g.wait_ge(s_pe.h, pbase + 8 * (mg - 1))
                lw3[mg % 2].inc(g.dma_start(
                    w3[:, mg % 2, :, :],
                    dram["w3p"][:, 16384 * mg:16384 * mg + 16384]
                    .rearrange("p (kc m) -> p kc m", m=512)), 16)

        @blk.tensor
        def _(t):
            t.wait_ge(s_dve.h, act_done)
            it = 0
            for mg in range(8):
                t.wait_ge(lw3[mg % 2].h, lw3b_[mg % 2] + 16 * (mg // 2 + 1))
                for mi in range(4):
                    for th in range(2):
                        if it >= 2:
                            t.wait_ge(s_dve.h, vbase2 + it - 1)
                        p = fps[it % 2]
                        for kc in range(KC):
                            ins = t.matmul(p[:, :],
                                           w3[:, mg % 2, kc, 128 * mi:128 * mi + 128],
                                           act_sb[:, kc, 512 * th:512 * th + 512],
                                           start=(kc == 0), stop=(kc == KC - 1))
                        s_pe.inc(ins)
                        it += 1

        @blk.vector
        def _(v):
            it = 0
            for mg in range(8):
                for mi in range(4):
                    m = 4 * mg + mi
                    for th in range(2):
                        v.wait_ge(s_pe.h, pbase + it + 1)
                        if it >= 2:
                            v.wait_ge(fs[it % 2].h, fsb[it % 2] + 16 * (it // 2))
                        s_dve.inc(v.tensor_scalar_add(fo[:, it % 2, :], fps[it % 2][:, :],
                                                      b3t[:, m:m + 1]))
                        it += 1

        @blk.scalar
        def _(sc):
            it = 0
            for mg in range(8):
                for mi in range(4):
                    for th in range(2):
                        m = 4 * mg + mi
                        sc.wait_ge(s_dve.h, vbase2 + it + 1)
                        fs[it % 2].inc(sc.dma_start(
                            ffn_dram[128 * m:128 * m + 128, 512 * th:512 * th + 512],
                            fo[:, it % 2, :]), 16)
                        it += 1


def emit_store_kv(ctx, k_sb, v_sb, k_dram, v_dram, which=("k", "v")):
    nc = ctx.nc
    kvst = ctx.sem("kvst")
    s_dve2 = ctx.sem("s_dve2")
    dve_done = s_dve2.v
    with nc.Block() as blk:
        @blk.scalar
        def _(sc):
            sc.wait_ge(s_dve2.h, dve_done)
            if "k" in which:
                kvst.inc(sc.dma_start(
                    k_dram.rearrange("(m p) t -> p m t", p=128), k_sb[:]), 16)
            if "v" in which:
                kvst.inc(sc.dma_start(
                    v_dram.rearrange("(s p) t -> p s t", p=128), v_sb[:]), 16)


def cum_pairs(g, s):
    return g * 144 + 2 * s * s + 2 * s


def emit_attention(ctx, layer, q_dram, k_full, v_full, o_all,
                   mask_wait=None, gather_wait=None):
    nc = ctx.nc
    s_kv = [ctx.sem("at_kv0"), ctx.sem("at_kv1")]
    s_q = [ctx.sem("at_q0"), ctx.sem("at_q1")]
    s_sp = ctx.sem("s_at_s")
    s_e = ctx.sem("s_at_e")
    s_m = ctx.sem("s_at_m")
    s_pv = ctx.sem("s_at_pv")
    s_n = ctx.sem("s_at_n")
    s_b = ctx.sem("s_at_b")
    mask = ctx.t["mask"]
    ones_col = ctx.t["ones_col"]
    ones_row = ctx.t["ones_row"]

    kvb = [s_kv[0].v, s_kv[1].v]
    qb = [s_q[0].v, s_q[1].v]
    sb, eb, mb_, pvb, nb, bb = (s_sp.v, s_e.v, s_m.v, s_pv.v, s_n.v, s_b.v)

    def P(g, s, p):
        return cum_pairs(g, s) + p

    def BI(g, s, p):
        return g * 32 + 4 * s + (p - 4 * s)

    with (
        nc.sbuf_tensor(f"at_kh_{uid()}", [128, 2, 8192], BF16) as kh,
        nc.sbuf_tensor(f"at_vh_{uid()}", [128, 2, 8192], BF16) as vh,
        nc.sbuf_tensor(f"at_qg_{uid()}", [128, 2, 512], BF16) as qg,
        nc.sbuf_tensor(f"at_e_{uid()}", [128, 2, 1024], BF16) as e_sb,
        nc.sbuf_tensor(f"at_t_{uid()}", [128, 2, 1024], F32) as t_sb,
        nc.sbuf_tensor(f"at_rrow_{uid()}", [1, 512], F32) as rrow,
        nc.sbuf_tensor(f"at_rsb_{uid()}", [128, 512], F32) as rsb,
        nc.psum_tensor(f"at_s0_{uid()}", [128, 1024], F32) as sp0,
        nc.psum_tensor(f"at_s1_{uid()}", [128, 1024], F32) as sp1,
        nc.psum_tensor(f"at_o_{uid()}", [128, 512], F32) as o_ps,
        nc.psum_tensor(f"at_d_{uid()}", [1, 512], F32) as d_ps,
        nc.psum_tensor(f"at_rb_{uid()}", [128, 512], F32) as rb_ps,
        nc.Block() as blk,
    ):
        sps = [sp0, sp1]

        @blk.sync
        def _(sy):
            if gather_wait is not None:
                sy.wait_ge(gather_wait[0].h, gather_wait[1])
            for g in range(NKV):
                if g >= 2:
                    sy.wait_ge(s_pv.h, pvb + (g - 1) * 144)
                for r in range(8):
                    src = k_full[1024 * r + 128 * g:1024 * r + 128 * g + 128, :] \
                        .rearrange("p (u q) -> p u q", q=128)
                    dst = kh[:, g % 2, :].rearrange("p (u r q) -> p r u q", r=8, q=128)[:, r, :, :]
                    s_kv[g % 2].inc(sy.dma_start(dst, src), 16)
                for r in range(8):
                    src = v_full[1024 * r:1024 * r + 1024, 128 * g:128 * g + 128] \
                        .rearrange("(u p) dv -> p u dv", p=128)
                    dst = vh[:, g % 2, :].rearrange("p (u r dv) -> p r u dv", r=8, dv=128)[:, r, :, :]
                    s_kv[g % 2].inc(sy.dma_start(dst, src), 16)
                for s in range(NSLOT):
                    gs = 8 * g + s
                    if gs >= 2:
                        pg, ps_ = divmod(gs - 2, 8)
                        sy.wait_ge(s_sp.h, sb + cum_pairs(pg, ps_) + 4 * ps_ + 4)
                    src = q_dram[512 * g:512 * g + 512, 128 * s:128 * s + 128] \
                        .rearrange("(h d) q -> d h q", d=128)
                    s_q[gs % 2].inc(sy.dma_start(
                        qg[:, gs % 2, :].rearrange("d (h q) -> d h q", q=128), src), 16)

        @blk.scalar
        def _(sc):
            for g in range(NKV):
                for s in range(NSLOT):
                    for p in range(4 * s + 4):
                        pi = P(g, s, p)
                        sc.wait_ge(s_sp.h, sb + pi + 1)
                        if pi >= 2:
                            sc.wait_ge(s_pv.h, pvb + pi - 1)
                        if layer == 0:
                            sc.activation(t_sb[:, pi % 2, :], sps[pi % 2][:, :], AF.Tanh,
                                          scale=ATTN_MULT / 30.0)
                            s_e.inc(sc.activation(e_sb[:, pi % 2, :], t_sb[:, pi % 2, :],
                                                  AF.Exp, scale=30.0,
                                                  bias=ctx.t["cm30"][:, 0:1]))
                        else:
                            s_e.inc(sc.activation(e_sb[:, pi % 2, :], sps[pi % 2][:, :],
                                                  AF.Exp, scale=ATTN_MULT))

        @blk.tensor
        def _(t):
            for g in range(NKV):
                for s in range(NSLOT):
                    gs = 8 * g + s
                    if s == 0:
                        t.wait_ge(s_kv[g % 2].h, kvb[g % 2] + 256 * (g // 2 + 1))
                    t.wait_ge(s_q[gs % 2].h, qb[gs % 2] + 16 * (gs // 2 + 1))
                    if gs >= 1:
                        t.wait_ge(s_n.h, nb + 3 * gs)
                    npair = 4 * s + 4
                    for p in range(npair):
                        pi = P(g, s, p)
                        if pi >= 2:
                            t.wait_ge(s_e.h, eb + pi - 1)
                        sp = sps[pi % 2]
                        for jj in range(2):
                            j = 2 * p + jj
                            ins = t.matmul(sp[:, 512 * jj:512 * jj + 512],
                                           kh[:, g % 2, 128 * j:128 * j + 128],
                                           qg[:, gs % 2, :],
                                           start=True, stop=True)
                        s_sp.inc(ins)
                        if p >= 4 * s:
                            t.wait_ge(s_m.h, mb_ + BI(g, s, p) + 1)
                        else:
                            t.wait_ge(s_e.h, eb + pi + 1)
                        for jj in range(2):
                            j = 2 * p + jj
                            t.matmul(o_ps[:, :],
                                     vh[:, g % 2, 128 * j:128 * j + 128],
                                     e_sb[:, pi % 2, 512 * jj:512 * jj + 512],
                                     start=(p == 0 and jj == 0),
                                     stop=(p == npair - 1 and jj == 1),
                                     skip_group_check=True)
                        for jj in range(2):
                            ins = t.matmul(d_ps[:, :], ones_col[:, :],
                                           e_sb[:, pi % 2, 512 * jj:512 * jj + 512],
                                           start=(p == 0 and jj == 0),
                                           stop=(p == npair - 1 and jj == 1),
                                           skip_group_check=True)
                        s_pv.inc(ins)
                    t.wait_ge(s_n.h, nb + 3 * gs + 1)
                    s_b.inc(t.matmul(rb_ps[:, :], ones_row[:, :], rrow[:, :],
                                     start=True, stop=True))

        @blk.vector
        def _(v):
            if mask_wait is not None:
                v.wait_ge(mask_wait[0].h, mask_wait[1])
            for g in range(NKV):
                for s in range(NSLOT):
                    gs = 8 * g + s
                    for p in range(4 * s, 4 * s + 4):
                        pi = P(g, s, p)
                        v.wait_ge(s_e.h, eb + pi + 1)
                        jb0 = 2 * p - 8 * s
                        s_m.inc(v.tensor_mul(e_sb[:, pi % 2, :], e_sb[:, pi % 2, :],
                                             mask[:, 512 * jb0:512 * jb0 + 1024]))
                    v.wait_ge(s_pv.h, pvb + cum_pairs(g, s) + 4 * s + 4)
                    s_n.inc(v.reciprocal(rrow[:, :], d_ps[:, :]))
                    v.wait_ge(s_b.h, bb + gs + 1)
                    s_n.inc(v.tensor_copy(rsb[:, :], rb_ps[:, :]))
                    dst = o_all[:, 4 * g:4 * g + 4, 128 * s:128 * s + 128]
                    s_n.inc(v.tensor_mul(
                        dst,
                        o_ps[:, :].rearrange("p (h q) -> p h q", q=128),
                        rsb[:, :].rearrange("p (h q) -> p h q", q=128)))

    return s_n.v


def emit_wo_epilogue(ctx, layer, dram, o_all, x_dram, ffn_dram, out_dram,
                     attn_done):
    nc = ctx.nc
    s_pe = ctx.sem("s_pe")
    s_dve = ctx.sem("s_dve")
    s_n = ctx.sem("s_at_n")
    wo_s = [ctx.sem("wo0s"), ctx.sem("wo1s")]
    xf = [ctx.sem("xf0"), ctx.sem("xf1")]
    eps = [ctx.sem("eps0"), ctx.sem("eps1")]
    fs = [ctx.sem("fs0"), ctx.sem("fs1")]

    with (
        nc.sbuf_tensor(f"wo_sb_{uid()}", [128, 2, 32, 512], BF16) as wo,
        nc.sbuf_tensor(f"ep_x_{uid()}", [128, 2, 512], F32) as xbuf,
        nc.sbuf_tensor(f"ep_f_{uid()}", [128, 2, 512], F32) as fbuf,
        nc.sbuf_tensor(f"ep_o_{uid()}", [128, 2, 512], F32) as obuf,
        nc.psum_tensor(f"ep_ps0_{uid()}", [128, 512], F32) as ps0,
        nc.psum_tensor(f"ep_ps1_{uid()}", [128, 512], F32) as ps1,
        nc.Block() as blk,
    ):
        ps = [ps0, ps1]
        pbase, vbase = s_pe.v, s_dve.v
        wob = [wo_s[0].v, wo_s[1].v]
        xfb = [xf[0].v, xf[1].v]
        epb = [eps[0].v, eps[1].v]
        fs_tot = [fs[0].v, fs[1].v]
        per_g = 16
        wo_src = dram["wo0p"] if layer == 0 else dram["wo1p"]

        @blk.gpsimd
        def _(g):
            for mg in range(8):
                if mg >= 2:
                    g.wait_ge(s_pe.h, pbase + 8 * (mg - 1))
                wo_s[mg % 2].inc(g.dma_start(
                    wo[:, mg % 2, :, :],
                    wo_src[:, 16384 * mg:16384 * mg + 16384]
                    .rearrange("p (kc m) -> p kc m", m=512)), 16)

        @blk.sync
        def _(sy):
            sy.wait_ge(fs[0].h, fs_tot[0])
            sy.wait_ge(fs[1].h, fs_tot[1])
            it = 0
            for mg in range(8):
                for mi in range(4):
                    m = 4 * mg + mi
                    for th in range(2):
                        if it >= 2:
                            sy.wait_ge(s_dve.h, vbase + it - 1)
                        sl = slice(512 * th, 512 * th + 512)
                        xf[it % 2].inc(sy.dma_start(xbuf[:, it % 2, :],
                                                    x_dram[128 * m:128 * m + 128, sl]), 16)
                        xf[it % 2].inc(sy.dma_start(fbuf[:, it % 2, :],
                                                    ffn_dram[128 * m:128 * m + 128, sl]), 16)
                        it += 1

        @blk.tensor
        def _(t):
            t.wait_ge(s_n.h, attn_done)
            it = 0
            for mg in range(8):
                t.wait_ge(wo_s[mg % 2].h, wob[mg % 2] + per_g * (mg // 2 + 1))
                for mi in range(4):
                    for th in range(2):
                        if it >= 2:
                            t.wait_ge(s_dve.h, vbase + it - 1)
                        p = ps[it % 2]
                        for kc in range(KC):
                            ins = t.matmul(p[:, :],
                                           wo[:, mg % 2, kc, 128 * mi:128 * mi + 128],
                                           o_all[:, kc, 512 * th:512 * th + 512],
                                           start=(kc == 0), stop=(kc == KC - 1))
                        s_pe.inc(ins)
                        it += 1

        @blk.vector
        def _(v):
            it = 0
            for mg in range(8):
                for mi in range(4):
                    for th in range(2):
                        v.wait_ge(s_pe.h, pbase + it + 1)
                        v.wait_ge(xf[it % 2].h, xfb[it % 2] + 32 * (it // 2 + 1))
                        if it >= 2:
                            v.wait_ge(eps[it % 2].h, epb[it % 2] + 16 * (it // 2))
                        v.tensor_add(obuf[:, it % 2, :], ps[it % 2][:, :], xbuf[:, it % 2, :])
                        s_dve.inc(v.tensor_add(obuf[:, it % 2, :], obuf[:, it % 2, :],
                                               fbuf[:, it % 2, :]))
                        it += 1

        @blk.scalar
        def _(sc):
            it = 0
            for mg in range(8):
                for mi in range(4):
                    m = 4 * mg + mi
                    for th in range(2):
                        sc.wait_ge(s_dve.h, vbase + it + 1)
                        eps[it % 2].inc(sc.dma_start(
                            out_dram[128 * m:128 * m + 128, 512 * th:512 * th + 512],
                            obuf[:, it % 2, :]), 16)
                        it += 1




def token_cols(core):
    return np.concatenate([np.arange(128 * (8 * s + core), 128 * (8 * s + core) + 128)
                           for s in range(NSLOT)])


def host_constants(core):
    import ml_dtypes
    pos = token_cols(core).astype(np.float64)
    inv = ROPE_THETA ** (-np.arange(64, dtype=np.float64) / 64.0)
    ang = pos[None, :] * inv[:, None]
    qcos = np.empty((128, TPC), np.float32)
    qsin = np.empty((128, TPC), np.float32)
    qcos[0:64] = np.cos(ang)
    qcos[64:128] = np.cos(ang)
    qsin[0:64] = -np.sin(ang)
    qsin[64:128] = np.sin(ang)
    kcos = np.empty((128, TPC), np.float32)
    ksin = np.empty((128, TPC), np.float32)
    for h in range(4):
        kcos[32 * h:32 * h + 32] = np.cos(ang[:32])
        ksin[32 * h:32 * h + 32] = np.sin(ang[:32])
    k_idx = np.arange(128)
    q_idx = np.arange(128)
    mask = np.zeros((128, NSLOT, GSIZE, 128), np.float32)
    for jb in range(NSLOT):
        m = ((128 * jb + k_idx[:, None]) <= (128 * core + q_idx[None, :]))
        mask[:, jb, :, :] = m[:, None, :]
    return {
        "qcos_in": qcos, "qsin_in": qsin, "kcos_in": kcos, "ksin_in": ksin,
        "mask_in": np.ascontiguousarray(
            mask.reshape(128, NSLOT * GSIZE * 128).astype(ml_dtypes.bfloat16)),
    }


def _f32(a):
    return np.ascontiguousarray(np.asarray(a, dtype=np.float32))


def emit_gather(ctx, k_loc_t, v_loc_t, k_full_t, v_full_t, tag):
    nc = ctx.nc
    s_coll = ctx.sem("s_coll")
    kvst = ctx.sem("kvst")
    kv_tot = kvst.v
    with nc.Block() as blk:
        @blk.gpsimd
        def _(g):
            g.wait_ge(kvst.h, kv_tot)
            s_coll.inc(g.collective_compute(
                "AllGather", OP.bypass,
                replica_groups=[list(range(NCORES))],
                ins=[k_loc_t.ap().opt()],
                outs=[k_full_t.ap().opt()]))
            s_coll.inc(g.collective_compute(
                "AllGather", OP.bypass,
                replica_groups=[list(range(NCORES))],
                ins=[v_loc_t.ap().opt()],
                outs=[v_full_t.ap().opt()]))
    return (s_coll, s_coll.v)


PRE_SHAPES = [
    ("wq0p", 8 * 16384), ("wkp", 32768), ("wvp", 32768), ("u1p", 32768),
    ("ffn0p", FCH * 3072), ("v2p", 32768), ("wo0p", 8 * 16384),
    ("wq1p", 8 * 16384), ("wkrp", 16384), ("wdkp", 16384),
    ("wupkp", 2048), ("wupvp", 4096), ("w12p", 16 * 16384),
    ("w3p", 8 * 16384), ("wo1p", 8 * 16384),
]


def build_fused():
    nc = bass.Bass("TRN2", target_bir_lowering=False)
    E_IN, E_OUT = "ExternalInput", "ExternalOutput"
    d = {}
    d["x"] = nc.dram_tensor("x", [D, TPC], F32, kind=E_IN).ap()
    for nm, w in PRE_SHAPES:
        d[nm] = nc.dram_tensor(nm, [128, w], F32, kind=E_IN).ap()
    for nm, w in [("b1g", FCH), ("b1u", FCH), ("b2t", 32),
                  ("b12g", 32), ("b12u", 32), ("b3t", 32)]:
        d[nm] = nc.dram_tensor(nm, [128, w], F32, kind=E_IN).ap()
    for nm in ["qcos_in", "qsin_in", "kcos_in", "ksin_in"]:
        d[nm] = nc.dram_tensor(nm, [128, TPC], F32, kind=E_IN).ap()
    d["mask_in"] = nc.dram_tensor("mask_in", [128, NSLOT * GSIZE * 128], BF16, kind=E_IN).ap()
    k_loc_t = nc.dram_tensor("k_loc", [1024, TPC], BF16, kind="Internal")
    v_loc_t = nc.dram_tensor("v_loc", [1024, TPC], BF16, kind="Internal")
    k_full_t = nc.dram_tensor("k_full", [SEQ, 1024], BF16, kind="Internal")
    v_full_t = nc.dram_tensor("v_full", [SEQ, 1024], BF16, kind="Internal")
    d["k_loc"], d["v_loc"] = k_loc_t.ap(), v_loc_t.ap()
    d["k_full"], d["v_full"] = k_full_t.ap(), v_full_t.ap()
    d["q_d"] = nc.dram_tensor("q_d", [D, TPC], BF16, kind="Internal").ap()
    d["ffn_d"] = nc.dram_tensor("ffn_d", [D, TPC], F32, kind="Internal").ap()
    d["x_mid"] = nc.dram_tensor("x_mid", [D, TPC], F32, kind="Internal").ap()
    d["xo"] = nc.dram_tensor("xo", [D, TPC], F32, kind=E_OUT).ap()

    with ExitStack() as stack:
        ctx = Ctx(nc, stack)
        ctx.t["ones_col"] = stack.enter_context(nc.sbuf_tensor(f"ones_col_{uid()}", [128, 1], BF16))
        ctx.t["ones_row"] = stack.enter_context(nc.sbuf_tensor(f"ones_row_{uid()}", [1, 128], F32))
        ctx.t["cm30"] = stack.enter_context(nc.sbuf_tensor(f"cm30_{uid()}", [128, 1], F32))
        ctx.t["mask"] = stack.enter_context(
            nc.sbuf_tensor(f"mask_sb_{uid()}", [128, NSLOT * GSIZE * 128], BF16))
        for nm, w in [("b1g", FCH), ("b1u", FCH), ("b2t", 32),
                      ("b12g", 32), ("b12u", 32), ("b3t", 32)]:
            ctx.t[nm] = stack.enter_context(nc.sbuf_tensor(nm + "_sb", [128, w], F32))
        emit_consts(ctx)
        s_g = ctx.sem("s_g")
        with nc.Block() as blk:
            @blk.gpsimd
            def _(g):
                for nm in ["b1g", "b1u", "b2t", "b12g", "b12u", "b3t"]:
                    s_g.inc(g.dma_start(ctx.t[nm][:, :], d[nm]), 16)
                s_g.inc(g.dma_start(ctx.t["mask"][:, :], d["mask_in"]), 16)
                g.wait_ge(s_g.h, s_g.v)

        with ExitStack() as l0:
            xn_sb = l0.enter_context(nc.sbuf_tensor(f"xn_sb0_{uid()}", [128, KC, TPC], BF16))
            emit_ln(ctx, d["x"], xn_sb)
            with ExitStack() as kvs:
                v_sb = kvs.enter_context(nc.sbuf_tensor(f"v_sb0_{uid()}", [128, 8, 1024], BF16))
                k_sb = kvs.enter_context(nc.sbuf_tensor(f"k_sb0_{uid()}", [128, 8, 1024], BF16))
                emit_kv_l0(ctx, d["wkp"], d["wvp"], xn_sb, k_sb, v_sb)
                emit_store_kv(ctx, k_sb, v_sb, d["k_loc"], d["v_loc"])
                gw0 = emit_gather(ctx, k_loc_t, v_loc_t, k_full_t, v_full_t, 0)
                emit_q(ctx, 0, d, xn_sb, d["q_d"])
            emit_ffn_l0(ctx, d, xn_sb, d["ffn_d"])
        with ExitStack() as l0a:
            o_all = l0a.enter_context(nc.sbuf_tensor(f"o_all0_{uid()}", [128, KC, TPC], BF16))
            attn_done = emit_attention(ctx, 0, d["q_d"], d["k_full"], d["v_full"],
                                       o_all, gather_wait=gw0)
            emit_wo_epilogue(ctx, 0, d, o_all, d["x"], d["ffn_d"], d["x_mid"], attn_done)

        with ExitStack() as l1:
            xn_sb = l1.enter_context(nc.sbuf_tensor(f"xn_sb1_{uid()}", [128, KC, TPC], BF16))
            eps0, eps1 = ctx.sem("eps0"), ctx.sem("eps1")
            e0v, e1v = eps0.v, eps1.v
            with nc.Block() as blk:
                @blk.sync
                def _(sy):
                    sy.wait_ge(eps0.h, e0v)
                    sy.wait_ge(eps1.h, e1v)
            emit_ln(ctx, d["x_mid"], xn_sb)
            with ExitStack() as kvs:
                v_sb = kvs.enter_context(nc.sbuf_tensor(f"v_sb1_{uid()}", [128, 8, 1024], BF16))
                for nm, key in [("qcos", "qcos_in"), ("qsin", "qsin_in"),
                                ("kcos", "kcos_in"), ("ksin", "ksin_in")]:
                    ctx.t[nm] = kvs.enter_context(nc.sbuf_tensor(nm, [128, TPC], F32))
                with nc.Block() as blk:
                    @blk.gpsimd
                    def _(g):
                        for nm, key in [("qcos", "qcos_in"), ("qsin", "qsin_in"),
                                        ("kcos", "kcos_in"), ("ksin", "ksin_in")]:
                            s_g.inc(g.dma_start(ctx.t[nm][:, :], d[key]), 16)
                        g.wait_ge(s_g.h, s_g.v)
                emit_kv_l1(ctx, d, xn_sb, v_sb, d["k_loc"])
                emit_store_kv(ctx, None, v_sb, None, d["v_loc"], which=("v",))
                gw1 = emit_gather(ctx, k_loc_t, v_loc_t, k_full_t, v_full_t, 1)
                emit_q(ctx, 1, d, xn_sb, d["q_d"])
            with nc.sbuf_tensor(f"act1_sb_{uid()}", [128, KC, TPC], BF16) as act_sb:
                emit_ffn_l1(ctx, d, xn_sb, act_sb, d["ffn_d"])
        with ExitStack() as l1a:
            o_all = l1a.enter_context(nc.sbuf_tensor(f"o_all1_{uid()}", [128, KC, TPC], BF16))
            attn_done = emit_attention(ctx, 1, d["q_d"], d["k_full"], d["v_full"],
                                       o_all, gather_wait=gw1)
            emit_wo_epilogue(ctx, 1, d, o_all, d["x_mid"], d["ffn_d"], d["xo"], attn_done)

        with nc.Block() as blk:
            waits = []
            for nm in ["eps0", "eps1"]:
                waits.append((ctx.sems[nm].h, ctx.sems[nm].v))

            @blk.sync
            def _(sy):
                for h, v in waits:
                    sy.wait_ge(h, v)
    return nc


def _c(a):
    return np.ascontiguousarray(a, dtype=np.float32)


def prelayout_weights(inputs):
    w = {}
    Wq0 = _f32(inputs["L0_attn_Wq"])
    w["wq0p"] = _c(Wq0.reshape(32, 128, 8, 512).transpose(1, 2, 0, 3).reshape(128, -1))
    w["wkp"] = _c(_f32(inputs["L0_attn_Wk"]).reshape(32, 128, 1024)
                  .transpose(1, 0, 2).reshape(128, -1))
    w["wvp"] = _c(_f32(inputs["L0_attn_Wv"]).reshape(32, 128, 1024)
                  .transpose(1, 0, 2).reshape(128, -1))
    w["wo0p"] = _c(_f32(inputs["L0_attn_Wo"]).reshape(32, 128, 8, 512)
                   .transpose(1, 2, 0, 3).reshape(128, -1))
    w["u1p"] = _c(_f32(inputs["L0_ffn_U1"]).reshape(32, 128, 1024)
                  .transpose(1, 0, 2).reshape(128, -1))
    V1 = _f32(inputs["L0_ffn_V1"])
    U2 = _f32(inputs["L0_ffn_U2"])
    A = V1[:, :INTER0].reshape(8, 128, FCH, 128).transpose(1, 2, 0, 3).reshape(128, FCH, 1024)
    B = V1[:, INTER0:].reshape(8, 128, FCH, 128).transpose(1, 2, 0, 3).reshape(128, FCH, 1024)
    C = U2.reshape(FCH, 128, 1024).transpose(1, 0, 2)
    w["ffn0p"] = _c(np.concatenate([A, B, C], axis=2).reshape(128, -1))
    w["v2p"] = _c(_f32(inputs["L0_ffn_V2"]).reshape(8, 128, 4, 1024)
                  .transpose(1, 2, 0, 3).reshape(128, -1))
    w["wq1p"] = _c(_f32(inputs["L1_attn_Wq"]).reshape(8, 512, 32, 128)
                   .transpose(3, 0, 2, 1).reshape(128, -1))
    wkr = _f32(inputs["L1_attn_Wkr"])
    a_rows = np.concatenate([np.arange(64 * g, 64 * g + 32) for g in range(8)])
    b_rows = np.concatenate([np.arange(64 * g + 32, 64 * g + 64) for g in range(8)])
    wkr_perm = np.concatenate([wkr[a_rows], wkr[b_rows]], axis=0)
    w["wkrp"] = _c(wkr_perm.reshape(512, 32, 128).transpose(2, 1, 0).reshape(128, -1))
    w["wdkp"] = _c(_f32(inputs["L1_attn_Wdk"]).reshape(512, 32, 128)
                   .transpose(2, 1, 0).reshape(128, -1))
    w["wupkp"] = _c(_f32(inputs["L1_attn_Wupk"]).reshape(512, 4, 128)
                    .transpose(2, 1, 0).reshape(128, -1))
    w["wupvp"] = _c(_f32(inputs["L1_attn_Wupv"]).reshape(2, 512, 4, 128)
                    .transpose(3, 2, 0, 1).reshape(128, -1))
    w["w12p"] = _c(_f32(inputs["L1_ffn_w12"]).reshape(2, 16, 256, 32, 128)
                   .transpose(4, 1, 0, 3, 2).reshape(128, -1))
    w["w3p"] = _c(_f32(inputs["L1_ffn_w3"]).reshape(8, 512, 32, 128)
                  .transpose(3, 0, 2, 1).reshape(128, -1))
    w["wo1p"] = _c(_f32(inputs["L1_attn_Wo"]).reshape(8, 512, 32, 128)
                   .transpose(3, 0, 2, 1).reshape(128, -1))
    b1 = _f32(inputs["L0_ffn_b1"])
    b2 = _f32(inputs["L0_ffn_b2"])
    b12 = _f32(inputs["L1_ffn_b12"])
    b3 = _f32(inputs["L1_ffn_b3"])
    w["b1g"] = _c(b1[:INTER0].reshape(FCH, 128).T)
    w["b1u"] = _c(b1[INTER0:].reshape(FCH, 128).T)
    w["b2t"] = _c(b2.reshape(32, 128).T)
    w["b12g"] = _c(b12[:D].reshape(32, 128).T)
    w["b12u"] = _c(b12[D:].reshape(32, 128).T)
    w["b3t"] = _c(b3.reshape(32, 128).T)
    return w


def kernel(**inputs):
    cols = [token_cols(c) for c in range(NCORES)]
    consts = [host_constants(c) for c in range(NCORES)]
    xT = _f32(inputs["L0_attn_hidden_states"][0]).T
    w = prelayout_weights(inputs)
    nc = build_fused()
    in_maps = []
    for c in range(NCORES):
        m = dict(x=np.ascontiguousarray(xT[:, cols[c]]), **w,
                 mask_in=consts[c]["mask_in"])
        for nm in ["qcos_in", "qsin_in", "kcos_in", "ksin_in"]:
            m[nm] = consts[c][nm]
        in_maps.append(m)
    res = run_bass_kernel_spmd(nc, in_maps, core_ids=list(range(NCORES)))
    out = np.empty((SEQ, D), np.float32)
    for c in range(NCORES):
        out[cols[c], :] = res.results[c]["xo"].T
    return out[None, :, :]


# version 06
# speedup vs baseline: 1.6668x; 1.4861x over previous
import numpy as np
from contextlib import ExitStack

import concourse.bass as bass
import concourse.mybir as mybir
from concourse.bass_utils import run_bass_kernel_spmd

F32 = mybir.dt.float32
BF16 = mybir.dt.bfloat16
AF = mybir.ActivationFunctionType
OP = mybir.AluOpType

D = 4096
SEQ = 8192
NCORES = 8
TPC = 1024
NSLOT = 8
KC = 32
NKV = 8
GSIZE = 4
ATTN_MULT = 0.08838834764831845
INTER0 = 11008
FCH = 86
EPS = 1e-5
ROPE_THETA = 10000.0

_uid = [0]


def uid():
    _uid[0] += 1
    return _uid[0]


class Sem:
    def __init__(self, stack, nc, name):
        self.h = stack.enter_context(nc.semaphore(name))
        self.v = 0

    def inc(self, ins, k=1):
        ins.then_inc(self.h, k)
        self.v += k
        return ins


class Ctx:
    def __init__(self, nc, stack):
        self.nc = nc
        self.stack = stack
        self.sems = {}
        self.t = {}

    def sem(self, name):
        if name not in self.sems:
            self.sems[name] = Sem(self.stack, self.nc, name)
        return self.sems[name]




def emit_consts(ctx):
    nc = ctx.nc
    s_dve = ctx.sem("s_dve")
    with nc.Block() as blk:
        @blk.vector
        def _(v):
            v.memset(ctx.t["ones_col"][:, :], 1.0)
            v.memset(ctx.t["cm30"][:, :], -30.0)
            s_dve.inc(v.memset(ctx.t["ones_row"][:, :], 1.0))


def emit_ln(ctx, x_dram, xn_sb):
    nc = ctx.nc
    lnx = [ctx.sem("lnx0"), ctx.sem("lnx1")]
    s_act = ctx.sem("s_act")
    s_pe = ctx.sem("s_pe")
    s_dve = ctx.sem("s_dve")
    ones_col = ctx.t["ones_col"]
    ones_row = ctx.t["ones_row"]

    with (
        nc.sbuf_tensor(f"ln_xin_{uid()}", [128, 2, 1024], F32) as xin,
        nc.sbuf_tensor(f"ln_xb_{uid()}", [128, 2, 1024], BF16) as xb,
        nc.sbuf_tensor(f"ln_xq_{uid()}", [128, 2, 1024], BF16) as xq,
        nc.sbuf_tensor(f"ln_rows_{uid()}", [1, 8, 1024], F32) as rows,
        nc.sbuf_tensor(f"ln_mb_{uid()}", [128, 1024], F32) as mb,
        nc.sbuf_tensor(f"ln_rb_{uid()}", [128, 1024], F32) as rb,
    ):
        with (
            nc.psum_tensor(f"ln_sum_{uid()}", [1, 1024], F32) as sum_ps,
            nc.psum_tensor(f"ln_sq_{uid()}", [1, 1024], F32) as sq_ps,
            nc.Block() as blk,
        ):
            xb0 = [lnx[0].v, lnx[1].v]
            abase, pbase = s_act.v, s_pe.v

            @blk.sync
            def _(sy):
                for c in range(KC):
                    if c >= 2:
                        sy.wait_ge(s_act.h, abase + c - 1)
                    lnx[c % 2].inc(sy.dma_start(xin[:, c % 2, :],
                                                x_dram[128 * c:128 * c + 128, :]), 16)

            @blk.scalar
            def _(s):
                for c in range(KC):
                    s.wait_ge(lnx[c % 2].h, xb0[c % 2] + 16 * (c // 2 + 1))
                    if c >= 2:
                        s.wait_ge(s_pe.h, pbase + c - 1)
                    s.copy(xb[:, c % 2, :], xin[:, c % 2, :])
                    s_act.inc(s.activation(xq[:, c % 2, :], xin[:, c % 2, :], AF.Square))

            @blk.tensor
            def _(t):
                for c in range(KC):
                    t.wait_ge(s_act.h, abase + c + 1)
                    for h in range(2):
                        sl = slice(512 * h, 512 * h + 512)
                        t.matmul(sum_ps[:, sl], ones_col[:, :], xb[:, c % 2, sl],
                                 start=(c == 0), stop=(c == KC - 1))
                        ins = t.matmul(sq_ps[:, sl], ones_col[:, :], xq[:, c % 2, sl],
                                       start=(c == 0), stop=(c == KC - 1))
                    s_pe.inc(ins)

            pe_done = s_pe.v

            @blk.vector
            def _(v):
                v.wait_ge(s_pe.h, pe_done)
                m = rows[:, 0, :]
                v.tensor_scalar_mul(m, sum_ps[:, :], 1.0 / D)
                v.tensor_scalar_mul(rows[:, 1, :], sq_ps[:, :], 1.0 / D)
                v.tensor_mul(rows[:, 2, :], m, m)
                v.tensor_sub(rows[:, 3, :], rows[:, 1, :], rows[:, 2, :])
                s_dve.inc(v.tensor_scalar_add(rows[:, 6, :], rows[:, 3, :], float(EPS)))

            vdone = s_dve.v

            @blk.scalar
            def _(s):
                s.wait_ge(s_dve.h, vdone)
                s.activation(rows[:, 4, :], rows[:, 6, :], AF.Ln)
                s_act.inc(s.activation(rows[:, 5, :], rows[:, 4, :], AF.Exp, scale=-0.5))

            adone = s_act.v

        with (
            nc.psum_tensor(f"ln_mbp_{uid()}", [128, 1024], F32) as mb_ps,
            nc.psum_tensor(f"ln_rbp_{uid()}", [128, 1024], F32) as rb_ps,
            nc.Block() as blk,
        ):
            @blk.tensor
            def _(t):
                t.wait_ge(s_act.h, adone)
                for h in range(2):
                    sl = slice(512 * h, 512 * h + 512)
                    t.matmul(mb_ps[:, sl], ones_row[:, :], rows[:, 0, sl], start=True, stop=True)
                    ins = t.matmul(rb_ps[:, sl], ones_row[:, :], rows[:, 5, sl], start=True, stop=True)
                s_pe.inc(ins)

            bdone = s_pe.v

            @blk.vector
            def _(v):
                v.wait_ge(s_pe.h, bdone)
                v.tensor_copy(mb[:, :], mb_ps[:, :])
                s_dve.inc(v.tensor_copy(rb[:, :], rb_ps[:, :]))

            cdone = s_dve.v
            xb2 = [lnx[0].v, lnx[1].v]
            vbase2 = s_dve.v

            @blk.sync
            def _(sy):
                for c in range(KC):
                    if c >= 2:
                        sy.wait_ge(s_dve.h, vbase2 + (c - 1))
                    lnx[c % 2].inc(sy.dma_start(xin[:, c % 2, :],
                                                x_dram[128 * c:128 * c + 128, :]), 16)

            @blk.vector
            def _(v):
                v.wait_ge(s_dve.h, cdone)
                for c in range(KC):
                    v.wait_ge(lnx[c % 2].h, xb2[c % 2] + 16 * (c // 2 + 1))
                    v.tensor_sub(xin[:, c % 2, :], xin[:, c % 2, :], mb[:, :])
                    s_dve.inc(v.tensor_mul(xn_sb[:, c, :], xin[:, c % 2, :], rb[:, :]))


def _emit_jobs_inline(ctx, blk, ps, jobs, outs, pe_waits=None, dve_extra=None):
    s_pe = ctx.sem("s_pe")
    s_dve2 = ctx.sem("s_dve2")
    pebase, evbase = s_pe.v, s_dve2.v

    @blk.tensor
    def _(t):
        for it, (lhs_list, rhs_list) in enumerate(jobs):
            if pe_waits and it in pe_waits:
                for sem, val in pe_waits[it]:
                    t.wait_ge(sem.h, val)
            if it >= 2:
                t.wait_ge(s_dve2.h, evbase + it - 1)
            p = ps[it % 2]
            n = len(lhs_list)
            for k in range(n):
                ins = t.matmul(p[:, :], lhs_list[k], rhs_list[k],
                               start=(k == 0), stop=(k == n - 1))
            s_pe.inc(ins)

    @blk.vector
    def _(v):
        if dve_extra:
            for sem, val in dve_extra:
                v.wait_ge(sem.h, val)
        for it, out_ap in enumerate(outs):
            v.wait_ge(s_pe.h, pebase + it + 1)
            s_dve2.inc(v.tensor_copy(out_ap, ps[it % 2][:, :]))
    return s_pe.v, s_dve2.v


def emit_kv_l0(ctx, wk_dram, wv_dram, xn_sb, k_sb, v_sb):
    nc = ctx.nc
    with (
        nc.sbuf_tensor(f"wk_sb_{uid()}", [128, 32, 1024], BF16) as wk,
        nc.psum_tensor(f"kvps{uid()}a", [128, 512], F32) as p0,
        nc.psum_tensor(f"kvps{uid()}b", [128, 512], F32) as p1,
        nc.Block() as blk,
    ):
        ldk = ctx.sem("ld_wk")
        b0 = ldk.v

        @blk.gpsimd
        def _(g):
            ldk.inc(g.dma_start(wk[:], wk_dram.rearrange("p (kc m) -> p kc m", m=1024)), 16)

        jobs, outs = [], []
        for m in range(8):
            for th in range(2):
                jobs.append((
                    [wk[:, kc, 128 * m:128 * m + 128] for kc in range(KC)],
                    [xn_sb[:, kc, 512 * th:512 * th + 512] for kc in range(KC)],
                ))
                outs.append(k_sb[:, m, 512 * th:512 * th + 512])
        _emit_jobs_inline(ctx, blk, [p0, p1], jobs, outs, {0: [(ldk, b0 + 16)]})

    with (
        nc.sbuf_tensor(f"wv_sb_{uid()}", [128, 32, 1024], BF16) as wv,
        nc.psum_tensor(f"kvps{uid()}a", [128, 512], F32) as p0,
        nc.psum_tensor(f"kvps{uid()}b", [128, 512], F32) as p1,
        nc.Block() as blk,
    ):
        ldv = ctx.sem("ld_wv")
        b0 = ldv.v

        @blk.gpsimd
        def _(g):
            ldv.inc(g.dma_start(wv[:], wv_dram.rearrange("p (kc m) -> p kc m", m=1024)), 16)

        jobs, outs = [], []
        for s in range(8):
            for vh in range(2):
                jobs.append((
                    [xn_sb[:, kc, 128 * s:128 * s + 128] for kc in range(KC)],
                    [wv[:, kc, 512 * vh:512 * vh + 512] for kc in range(KC)],
                ))
                outs.append(v_sb[:, s, 512 * vh:512 * vh + 512])
        _emit_jobs_inline(ctx, blk, [p0, p1], jobs, outs, {0: [(ldv, b0 + 16)]})


def emit_kv_l1(ctx, dram, xn_sb, v_sb, k_dram):
    nc = ctx.nc
    s_s = ctx.sem("s_s")
    s_dve = ctx.sem("s_dve")
    s_dve2 = ctx.sem("s_dve2")
    s_c = ctx.sem("s_c")
    kcos, ksin = ctx.t["kcos"], ctx.t["ksin"]

    with (
        nc.sbuf_tensor(f"ckv_sb_{uid()}", [128, 4, 1024], BF16) as ckv,
        nc.sbuf_tensor(f"krot_sb_{uid()}", [128, 4, 1024], BF16) as krot,
        nc.sbuf_tensor(f"kc_sb_{uid()}", [128, 4, 1024], BF16) as kcs,
    ):
        with (
            nc.sbuf_tensor(f"wdkT_{uid()}", [128, 32, 512], BF16) as wdkT,
            nc.psum_tensor(f"kv1ps{uid()}a", [128, 512], F32) as p0,
            nc.psum_tensor(f"kv1ps{uid()}b", [128, 512], F32) as p1,
            nc.Block() as blk,
        ):
            lddk = ctx.sem("ld_wdk")
            sbase = lddk.v

            @blk.gpsimd
            def _(g):
                lddk.inc(g.dma_start(
                    wdkT[:], dram["wdkp"].rearrange("p (kc m) -> p kc m", m=512)), 16)

            jobs, outs = [], []
            for m in range(4):
                for th in range(2):
                    jobs.append((
                        [wdkT[:, kc, 128 * m:128 * m + 128] for kc in range(KC)],
                        [xn_sb[:, kc, 512 * th:512 * th + 512] for kc in range(KC)],
                    ))
                    outs.append(ckv[:, m, 512 * th:512 * th + 512])
            _emit_jobs_inline(ctx, blk, [p0, p1], jobs, outs,
                              {0: [(lddk, sbase + 16)]})

        with (
            nc.sbuf_tensor(f"wkrT_{uid()}", [128, 32, 512], BF16) as wkrT,
            nc.sbuf_tensor(f"kr_sb_{uid()}", [128, 4, 1024], F32) as kr,
            nc.sbuf_tensor(f"kv1_scr_{uid()}", [128, 2, 1024], F32) as scr,
            nc.psum_tensor(f"kv1ps{uid()}a", [128, 512], F32) as p0,
            nc.psum_tensor(f"kv1ps{uid()}b", [128, 512], F32) as p1,
            nc.Block() as blk,
        ):
            ldkr = ctx.sem("ld_wkr")
            sbase = ldkr.v

            @blk.gpsimd
            def _(g):
                ldkr.inc(g.dma_start(
                    wkrT[:], dram["wkrp"].rearrange("p (kc m) -> p kc m", m=512)), 16)

            jobs, outs = [], []
            for m in range(4):
                for th in range(2):
                    jobs.append((
                        [wkrT[:, kc, 128 * m:128 * m + 128] for kc in range(KC)],
                        [xn_sb[:, kc, 512 * th:512 * th + 512] for kc in range(KC)],
                    ))
                    outs.append(kr[:, m, 512 * th:512 * th + 512])
            _emit_jobs_inline(ctx, blk, [p0, p1], jobs, outs,
                              {0: [(ldkr, sbase + 16)]})

            @blk.vector
            def _(v):
                for j in range(2):
                    a = kr[:, j, :]
                    b = kr[:, j + 2, :]
                    v.tensor_mul(scr[:, 0, :], a, kcos[:, :])
                    v.tensor_mul(scr[:, 1, :], b, ksin[:, :])
                    v.tensor_sub(krot[:, j, :], scr[:, 0, :], scr[:, 1, :])
                    v.tensor_mul(scr[:, 0, :], b, kcos[:, :])
                    v.tensor_mul(scr[:, 1, :], a, ksin[:, :])
                    s_dve.inc(v.tensor_add(krot[:, j + 2, :], scr[:, 0, :], scr[:, 1, :]))

        rope_done = s_dve.v

        with (
            nc.sbuf_tensor(f"wupkT_{uid()}", [128, 4, 512], BF16) as wupkT,
            nc.sbuf_tensor(f"wupvT_{uid()}", [128, 4, 2, 512], BF16) as wupvT,
            nc.psum_tensor(f"kv1ps{uid()}a", [128, 512], F32) as p0,
            nc.psum_tensor(f"kv1ps{uid()}b", [128, 512], F32) as p1,
            nc.Block() as blk,
        ):
            ldupk = ctx.sem("ld_upk")
            ldupv = ctx.sem("ld_upv")
            bk0, bv0 = ldupk.v, ldupv.v

            @blk.gpsimd
            def _(g):
                ldupk.inc(g.dma_start(
                    wupkT[:], dram["wupkp"].rearrange("p (kc m) -> p kc m", m=512)), 16)
                ldupv.inc(g.dma_start(
                    wupvT[:], dram["wupvp"].rearrange("p (kc v m) -> p kc v m",
                                                      v=2, m=512)), 16)

            jobs2, outs2 = [], []
            for m in range(4):
                for th in range(2):
                    jobs2.append((
                        [wupkT[:, kc, 128 * m:128 * m + 128] for kc in range(4)],
                        [ckv[:, kc, 512 * th:512 * th + 512] for kc in range(4)],
                    ))
                    outs2.append(kcs[:, m, 512 * th:512 * th + 512])
            for s in range(8):
                for vhh in range(2):
                    jobs2.append((
                        [ckv[:, kc, 128 * s:128 * s + 128] for kc in range(4)],
                        [wupvT[:, kc, vhh, :] for kc in range(4)],
                    ))
                    outs2.append(v_sb[:, s, 512 * vhh:512 * vhh + 512])
            pe_waits2 = {0: [(ldupk, bk0 + 16)], 8: [(ldupv, bv0 + 16)]}
            _, dve_v2 = _emit_jobs_inline(ctx, blk, [p0, p1], jobs2, outs2, pe_waits2)

            kvst = ctx.sem("kvst")

            @blk.scalar
            def _(sc):
                sc.wait_ge(s_dve.h, rope_done)
                sc.wait_ge(s_dve2.h, dve_v2)
                for j in range(2):
                    for h in range(4):
                        g = 4 * j + h
                        kvst.inc(sc.dma_start(
                            k_dram[128 * g:128 * g + 32, :],
                            krot[32 * h:32 * h + 32, j, :]), 16)
                        kvst.inc(sc.dma_start(
                            k_dram[128 * g + 64:128 * g + 96, :],
                            krot[32 * h:32 * h + 32, j + 2, :]), 16)
                for m in range(4):
                    for gp in range(2):
                        g = 2 * m + gp
                        kvst.inc(sc.dma_start(
                            k_dram[128 * g + 32:128 * g + 64, :],
                            kcs[64 * gp:64 * gp + 32, m, :]), 16)
                        kvst.inc(sc.dma_start(
                            k_dram[128 * g + 96:128 * g + 128, :],
                            kcs[64 * gp + 32:64 * gp + 64, m, :]), 16)


def emit_q(ctx, layer, dram, xn_sb, q_dram):
    nc = ctx.nc
    s_pe = ctx.sem("s_pe")
    s_act = ctx.sem("s_act")
    s_dve = ctx.sem("s_dve")
    qw = [ctx.sem("qw0"), ctx.sem("qw1")]
    qs = [ctx.sem("qs0"), ctx.sem("qs1")]
    kvst = ctx.sem("kvst")
    NG = 8
    per_g = 16

    with (
        nc.sbuf_tensor(f"wq_sb_{uid()}", [128, 2, 32, 512], BF16) as wq,
        nc.sbuf_tensor(f"q_out_{uid()}", [128, 2, 1024], BF16) as qo,
        nc.sbuf_tensor(f"q_sw_{uid()}", [128, 2, 1024], F32) as qsw,
        nc.sbuf_tensor(f"q_t1_{uid()}", [128, 1024], F32) as qt1,
        nc.sbuf_tensor(f"q_t2_{uid()}", [128, 1024], F32) as qt2,
        nc.psum_tensor(f"q_ps0_{uid()}", [128, 1024], F32) as qps0,
        nc.psum_tensor(f"q_ps1_{uid()}", [128, 1024], F32) as qps1,
        nc.Block() as blk,
    ):
        qps = [qps0, qps1]
        pbase, abase, vbase = s_pe.v, s_act.v, s_dve.v
        qwb = [qw[0].v, qw[1].v]
        qsb = [qs[0].v, qs[1].v]
        kvst_tot = kvst.v

        wq_src = dram["wq0p"] if layer == 0 else dram["wq1p"]

        @blk.gpsimd
        def _(g):
            g.wait_ge(kvst.h, kvst_tot)
            for mg in range(NG):
                if mg >= 2:
                    g.wait_ge(s_pe.h, pbase + 4 * (mg - 1))
                qw[mg % 2].inc(g.dma_start(
                    wq[:, mg % 2, :, :],
                    wq_src[:, 16384 * mg:16384 * mg + 16384]
                    .rearrange("p (kc m) -> p kc m", m=512)), 16)

        @blk.tensor
        def _(t):
            it = 0
            for mg in range(NG):
                t.wait_ge(qw[mg % 2].h, qwb[mg % 2] + per_g * (mg // 2 + 1))
                for mi in range(4):
                    if it >= 2:
                        t.wait_ge(s_dve.h, vbase + it - 1)
                    p = qps[it % 2]
                    for th in range(2):
                        for kc in range(KC):
                            ins = t.matmul(p[:, 512 * th:512 * th + 512],
                                           wq[:, mg % 2, kc, 128 * mi:128 * mi + 128],
                                           xn_sb[:, kc, 512 * th:512 * th + 512],
                                           start=(kc == 0), stop=(kc == KC - 1))
                    s_pe.inc(ins)
                    it += 1

        if layer == 1:
            qcos, qsin = ctx.t["qcos"], ctx.t["qsin"]

            @blk.scalar
            def _(sc):
                for it in range(KC):
                    sc.wait_ge(s_pe.h, pbase + it + 1)
                    if it >= 2:
                        sc.wait_ge(s_dve.h, vbase + it - 1)
                    p = qps[it % 2]
                    sc.copy(qsw[0:64, it % 2, :], p[64:128, :])
                    s_act.inc(sc.activation(qsw[64:128, it % 2, :], p[0:64, :], AF.Copy))
                    if it >= 2:
                        jt = it - 2
                        qs[jt % 2].inc(sc.dma_start(q_dram[128 * jt:128 * jt + 128, :],
                                                    qo[:, jt % 2, :]), 16)
                for jt in (KC - 2, KC - 1):
                    sc.wait_ge(s_dve.h, vbase + jt + 1)
                    qs[jt % 2].inc(sc.dma_start(q_dram[128 * jt:128 * jt + 128, :],
                                                qo[:, jt % 2, :]), 16)

            @blk.vector
            def _(v):
                for it in range(KC):
                    v.wait_ge(s_act.h, abase + it + 1)
                    if it >= 2:
                        v.wait_ge(qs[it % 2].h, qsb[it % 2] + 16 * (it // 2))
                    p = qps[it % 2]
                    v.tensor_mul(qt1[:, :], p[:, :], qcos[:, :])
                    v.tensor_mul(qt2[:, :], qsw[:, it % 2, :], qsin[:, :])
                    s_dve.inc(v.tensor_add(qo[:, it % 2, :], qt1[:, :], qt2[:, :]))
        else:
            @blk.vector
            def _(v):
                for it in range(KC):
                    v.wait_ge(s_pe.h, pbase + it + 1)
                    if it >= 2:
                        v.wait_ge(qs[it % 2].h, qsb[it % 2] + 16 * (it // 2))
                    s_dve.inc(v.tensor_copy(qo[:, it % 2, :], qps[it % 2][:, :]))

            @blk.scalar
            def _(sc):
                for it in range(KC):
                    sc.wait_ge(s_dve.h, vbase + it + 1)
                    qs[it % 2].inc(sc.dma_start(q_dram[128 * it:128 * it + 128, :],
                                                qo[:, it % 2, :]), 16)


def emit_ffn_l0(ctx, dram, xn_sb, ffn_dram, upto='all', dbg_out=None):
    nc = ctx.nc
    s_g = ctx.sem("s_g")
    s_pe = ctx.sem("s_pe")
    s_pe2 = ctx.sem("s_pe2")
    s_act = ctx.sem("s_act")
    s_dve = ctx.sem("s_dve")
    s_dve2 = ctx.sem("s_dve2")
    s_c = ctx.sem("s_c")
    b1g, b1u, b2t = ctx.t["b1g"], ctx.t["b1u"], ctx.t["b2t"]

    with nc.sbuf_tensor(f"p_sb_{uid()}", [128, 8, 1024], BF16) as p_sb:
        with (
            nc.sbuf_tensor(f"u1_sb_{uid()}", [128, 32, 1024], BF16) as u1,
            nc.psum_tensor(f"pps{uid()}a", [128, 512], F32) as pp0,
            nc.psum_tensor(f"pps{uid()}b", [128, 512], F32) as pp1,
            nc.Block() as blk,
        ):
            ldu1 = ctx.sem("ld_u1")
            gbase = ldu1.v
            qs = [ctx.sem("qs0"), ctx.sem("qs1")]
            qs_tot = [qs[0].v, qs[1].v]

            @blk.gpsimd
            def _(g):
                g.wait_ge(qs[0].h, qs_tot[0])
                g.wait_ge(qs[1].h, qs_tot[1])
                ldu1.inc(g.dma_start(u1[:], dram["u1p"].rearrange("p (kc m) -> p kc m", m=1024)), 16)

            jobs, outs = [], []
            for m in range(8):
                for th in range(2):
                    jobs.append((
                        [u1[:, kc, 128 * m:128 * m + 128] for kc in range(KC)],
                        [xn_sb[:, kc, 512 * th:512 * th + 512] for kc in range(KC)],
                    ))
                    outs.append(p_sb[:, m, 512 * th:512 * th + 512])
            _emit_jobs_inline(ctx, blk, [pp0, pp1], jobs, outs, {0: [(ldu1, gbase + 16)]})

        if upto == "ffnp":
            dbg = ctx.sem("dbg")
            with nc.Block() as blk:
                @blk.scalar
                def _(sc):
                    dbg.inc(sc.dma_start(
                        dbg_out.rearrange("(c p) t -> p c t", p=128), p_sb[:]), 16)
                    sc.wait_ge(dbg.h, dbg.v)
            return

        fblocks = [list(range(4 * b, min(4 * b + 4, FCH))) for b in range((FCH + 3) // 4)]
        NB = len(fblocks)
        cum_units = [0]
        for fb in fblocks:
            cum_units.append(cum_units[-1] + 2 * len(fb))

        def par_load_val(b):
            return 16 * (b // 2 + 1)

        with (
            nc.sbuf_tensor(f"ffn0w_sb_{uid()}", [128, 2, 4, 3072], BF16) as ffn0_w,
            nc.sbuf_tensor(f"act0_sb_{uid()}", [128, 2, 4, 1024], BF16) as act_sb,
            nc.sbuf_tensor(f"sg_sb_{uid()}", [128, 2, 512], BF16) as sg_sb,
            nc.sbuf_tensor(f"su_sb_{uid()}", [128, 2, 512], BF16) as su_sb,
            nc.sbuf_tensor(f"z_sb_{uid()}", [128, 8192], F32) as z_sb,
            nc.sbuf_tensor(f"zb_sb_{uid()}", [128, 8, 1024], BF16) as zb_sb,
            nc.psum_tensor(f"hg_ps0_{uid()}", [128, 1024], F32) as hg0,
            nc.psum_tensor(f"hg_ps1_{uid()}", [128, 1024], F32) as hg1,
            nc.psum_tensor(f"z_ps_{uid()}", [128, 2048], F32) as z_ps,
            nc.Block() as blk,
        ):
            hg = [hg0, hg1]
            ldv = [ctx.sem("ldv0"), ctx.sem("ldv1")]
            ldvb = [ldv[0].v, ldv[1].v]
            pbase = s_pe.v
            p2base = s_pe2.v
            abase = s_act.v
            vbase = s_dve.v
            v2base = s_dve2.v

            @blk.vector
            def _(v):
                s_dve2.inc(v.memset(z_sb[:, :], 0.0))

            @blk.gpsimd
            def _(g):
                for b, fb in enumerate(fblocks):
                    if b >= 2:
                        g.wait_ge(s_pe2.h, p2base + 4 * (b - 1))
                    nf = len(fb)
                    src = dram["ffn0p"][:, 3072 * 4 * b:3072 * (4 * b + nf)] \
                        .rearrange("p (f x) -> p f x", x=3072)
                    ldv[b % 2].inc(g.dma_start(ffn0_w[:, b % 2, 0:nf, :], src), 16)

            def z_group(t, bb, tq):
                t.wait_ge(s_dve.h, vbase + cum_units[bb + 1])
                t.wait_ge(s_dve2.h, v2base + 1 + 4 * bb + tq)
                nf = len(fblocks[bb])
                for fi, f in enumerate(fblocks[bb]):
                    for m in range(8):
                        ins = t.matmul(
                            z_ps[:, 256 * m:256 * m + 256],
                            ffn0_w[:, bb % 2, fi, 2048 + 128 * m:2048 + 128 * m + 128],
                            act_sb[:, bb % 2, fi, 256 * tq:256 * tq + 256],
                            start=(fi == 0 and m % 2 == 0), stop=(fi == nf - 1),
                            skip_group_check=True)
                s_pe2.inc(ins)

            @blk.tensor
            def _(t):
                for b, fb in enumerate(fblocks):
                    t.wait_ge(ldv[b % 2].h, ldvb[b % 2] + par_load_val(b))
                    for fi, f in enumerate(fb):
                        for th in range(2):
                            u = cum_units[b] + 2 * fi + th
                            if u >= 2:
                                t.wait_ge(s_dve.h, vbase + (u - 1))
                            hp = hg[u % 2]
                            for kc in range(8):
                                t.matmul(hp[:, 0:512],
                                         ffn0_w[:, b % 2, fi, 128 * kc:128 * kc + 128],
                                         p_sb[:, kc, 512 * th:512 * th + 512],
                                         start=(kc == 0), stop=(kc == 7))
                            for kc in range(8):
                                ins = t.matmul(hp[:, 512:1024],
                                               ffn0_w[:, b % 2, fi,
                                                      1024 + 128 * kc:1024 + 128 * kc + 128],
                                               p_sb[:, kc, 512 * th:512 * th + 512],
                                               start=(kc == 0), stop=(kc == 7))
                            s_pe.inc(ins)
                    if b >= 1:
                        for tq in range(4):
                            z_group(t, b - 1, tq)
                for tq in range(4):
                    z_group(t, NB - 1, tq)

            @blk.scalar
            def _(sc):
                for b, fb in enumerate(fblocks):
                    for fi, f in enumerate(fb):
                        for th in range(2):
                            u = cum_units[b] + 2 * fi + th
                            sc.wait_ge(s_pe.h, pbase + u + 1)
                            if u >= 2:
                                sc.wait_ge(s_dve.h, vbase + u - 1)
                            s_act.inc(sc.activation(sg_sb[:, u % 2, :], hg[u % 2][:, 0:512],
                                                    AF.Silu, bias=b1g[:, f:f + 1]))

            @blk.vector
            def _(v):
                for b, fb in enumerate(fblocks):
                    for fi, f in enumerate(fb):
                        for th in range(2):
                            u = cum_units[b] + 2 * fi + th
                            v.wait_ge(s_act.h, abase + (u + 1))
                            if b >= 2:
                                v.wait_ge(s_pe2.h, p2base + 4 * (b - 1))
                            s_dve.inc(v.scalar_tensor_tensor(
                                act_sb[:, b % 2, fi, 512 * th:512 * th + 512],
                                hg[u % 2][:, 512:1024], b1u[:, f:f + 1],
                                sg_sb[:, u % 2, :], OP.add, OP.mult))
                    if b >= 1:
                        for tq in range(4):
                            v.wait_ge(s_pe2.h, p2base + 4 * (b - 1) + tq + 1)
                            dst = z_sb[:, :].rearrange("p (m x) -> p m x", x=1024)[
                                :, :, 256 * tq:256 * tq + 256]
                            src = z_ps[:, :].rearrange("p (m x) -> p m x", x=256)
                            s_dve2.inc(v.tensor_add(dst, dst, src))
                for tq in range(4):
                    v.wait_ge(s_pe2.h, p2base + 4 * (NB - 1) + tq + 1)
                    dst = z_sb[:, :].rearrange("p (m x) -> p m x", x=1024)[
                        :, :, 256 * tq:256 * tq + 256]
                    src = z_ps[:, :].rearrange("p (m x) -> p m x", x=256)
                    s_dve2.inc(v.tensor_add(dst, dst, src))
                zv = z_sb[:, :].rearrange("p (m x) -> p m x", x=1024)
                for m in range(8):
                    s_dve.inc(v.tensor_copy(zb_sb[:, m, :], zv[:, m, :]))

            zcast_done = s_dve.v

        if upto == "ffna":
            dbg = ctx.sem("dbg")
            with nc.Block() as blk:
                ad = s_dve.v

                @blk.scalar
                def _(sc):
                    sc.wait_ge(s_dve.h, ad)
                    dbg.inc(sc.dma_start(
                        dbg_out[0:512].rearrange("(c p) t -> p c t", p=128),
                        act_sb[:, 0, :, :]), 16)
                    sc.wait_ge(dbg.h, dbg.v)
            return

        if upto == "ffnz":
            dbg = ctx.sem("dbg")
            with nc.Block() as blk:
                zd = s_dve.v

                @blk.scalar
                def _(sc):
                    sc.wait_ge(s_dve.h, zd)
                    dbg.inc(sc.dma_start(
                        dbg_out.rearrange("(c p) t -> p c t", p=128), zb_sb[:]), 16)
                    sc.wait_ge(dbg.h, dbg.v)
            return

        with (
            nc.sbuf_tensor(f"v2_sb_{uid()}", [128, 2, 8, 1024], BF16) as v2,
            nc.sbuf_tensor(f"fo_sb_{uid()}", [128, 2, 512], F32) as fo,
            nc.psum_tensor(f"fo_ps0_{uid()}", [128, 512], F32) as fps0,
            nc.psum_tensor(f"fo_ps1_{uid()}", [128, 512], F32) as fps1,
            nc.Block() as blk,
        ):
            fps = [fps0, fps1]
            pbase, abase = s_pe.v, s_act.v
            vbase_f = s_dve.v
            ldv2 = [ctx.sem("ldv2a"), ctx.sem("ldv2b")]
            ldv2b_ = [ldv2[0].v, ldv2[1].v]
            fs = [ctx.sem("fs0"), ctx.sem("fs1")]
            fsb = [fs[0].v, fs[1].v]

            @blk.gpsimd
            def _(g):
                for mg in range(4):
                    if mg >= 2:
                        g.wait_ge(s_pe.h, pbase + 16 * (mg - 1))
                    ldv2[mg % 2].inc(g.dma_start(
                        v2[:, mg % 2, :, :],
                        dram["v2p"][:, 8192 * mg:8192 * mg + 8192]
                        .rearrange("p (kc m) -> p kc m", m=1024)), 16)

            @blk.tensor
            def _(t):
                t.wait_ge(s_dve.h, zcast_done)
                it = 0
                for mg in range(4):
                    t.wait_ge(ldv2[mg % 2].h, ldv2b_[mg % 2] + 16 * (mg // 2 + 1))
                    for mi in range(8):
                        for th in range(2):
                            if it >= 2:
                                t.wait_ge(s_dve.h, vbase_f + it - 1)
                            p = fps[it % 2]
                            for kc in range(8):
                                ins = t.matmul(p[:, :],
                                               v2[:, mg % 2, kc, 128 * mi:128 * mi + 128],
                                               zb_sb[:, kc, 512 * th:512 * th + 512],
                                               start=(kc == 0), stop=(kc == 7))
                            s_pe.inc(ins)
                            it += 1

            vbase_c = s_dve.v

            @blk.vector
            def _(v):
                it = 0
                for mg in range(4):
                    for mi in range(8):
                        m = 8 * mg + mi
                        for th in range(2):
                            v.wait_ge(s_pe.h, pbase + it + 1)
                            if it >= 2:
                                v.wait_ge(fs[it % 2].h, fsb[it % 2] + 16 * (it // 2))
                            s_dve.inc(v.tensor_scalar_add(fo[:, it % 2, :], fps[it % 2][:, :],
                                                          b2t[:, m:m + 1]))
                            it += 1

            @blk.scalar
            def _(sc):
                it = 0
                for mg in range(4):
                    for mi in range(8):
                        m = 8 * mg + mi
                        for th in range(2):
                            sc.wait_ge(s_dve.h, vbase_c + it + 1)
                            fs[it % 2].inc(sc.dma_start(
                                ffn_dram[128 * m:128 * m + 128, 512 * th:512 * th + 512],
                                fo[:, it % 2, :]), 16)
                            it += 1


def emit_ffn_l1(ctx, dram, xn_sb, act_sb, ffn_dram):
    nc = ctx.nc
    s_s = ctx.sem("s_s")
    s_pe = ctx.sem("s_pe")
    s_act = ctx.sem("s_act")
    s_dve = ctx.sem("s_dve")
    s_c = ctx.sem("s_c")
    b12g, b12u, b3t = ctx.t["b12g"], ctx.t["b12u"], ctx.t["b3t"]

    with (
        nc.sbuf_tensor(f"w12_sb_{uid()}", [128, 2, 2, 32, 256], BF16) as w12sb,
        nc.sbuf_tensor(f"sg1_sb_{uid()}", [128, 2, 512], BF16) as sg_sb,
        nc.sbuf_tensor(f"su1_sb_{uid()}", [128, 2, 512], BF16) as su_sb,
        nc.psum_tensor(f"h1_ps0_{uid()}", [128, 1024], F32) as h0,
        nc.psum_tensor(f"h1_ps1_{uid()}", [128, 1024], F32) as h1,
        nc.Block() as blk,
    ):
        hp = [h0, h1]
        pbase, abase, vbase = s_pe.v, s_act.v, s_dve.v
        lw = [ctx.sem("lw0"), ctx.sem("lw1")]
        lwb = [lw[0].v, lw[1].v]
        qs = [ctx.sem("qs0"), ctx.sem("qs1")]
        qs_tot = [qs[0].v, qs[1].v]

        @blk.gpsimd
        def _(g):
            g.wait_ge(qs[0].h, qs_tot[0])
            g.wait_ge(qs[1].h, qs_tot[1])
            for og in range(16):
                if og >= 2:
                    g.wait_ge(s_pe.h, pbase + 4 * (og - 1))
                lw[og % 2].inc(g.dma_start(
                    w12sb[:, og % 2, :, :, :],
                    dram["w12p"][:, 16384 * og:16384 * og + 16384]
                    .rearrange("p (gu kc m) -> p gu kc m", gu=2, m=256)), 16)

        @blk.tensor
        def _(t):
            u = 0
            for og in range(16):
                t.wait_ge(lw[og % 2].h, lwb[og % 2] + 16 * (og // 2 + 1))
                for fo_ in range(2):
                    for th in range(2):
                        if u >= 2:
                            t.wait_ge(s_dve.h, vbase + (u - 1))
                        p = hp[u % 2]
                        for kc in range(KC):
                            t.matmul(p[:, 0:512],
                                     w12sb[:, og % 2, 0, kc, 128 * fo_:128 * fo_ + 128],
                                     xn_sb[:, kc, 512 * th:512 * th + 512],
                                     start=(kc == 0), stop=(kc == KC - 1))
                        for kc in range(KC):
                            ins = t.matmul(p[:, 512:1024],
                                           w12sb[:, og % 2, 1, kc, 128 * fo_:128 * fo_ + 128],
                                           xn_sb[:, kc, 512 * th:512 * th + 512],
                                           start=(kc == 0), stop=(kc == KC - 1))
                        s_pe.inc(ins)
                        u += 1

        @blk.scalar
        def _(sc):
            u = 0
            for og in range(16):
                for fo_ in range(2):
                    f = 2 * og + fo_
                    for th in range(2):
                        sc.wait_ge(s_pe.h, pbase + u + 1)
                        if u >= 2:
                            sc.wait_ge(s_dve.h, vbase + u - 1)
                        s_act.inc(sc.activation(sg_sb[:, u % 2, :], hp[u % 2][:, 0:512],
                                                AF.Silu, bias=b12g[:, f:f + 1]))
                        u += 1

        @blk.vector
        def _(v):
            u = 0
            for og in range(16):
                for fo_ in range(2):
                    f = 2 * og + fo_
                    for th in range(2):
                        v.wait_ge(s_act.h, abase + (u + 1))
                        s_dve.inc(v.scalar_tensor_tensor(
                            act_sb[:, f, 512 * th:512 * th + 512],
                            hp[u % 2][:, 512:1024], b12u[:, f:f + 1],
                            sg_sb[:, u % 2, :], OP.add, OP.mult))
                        u += 1

    act_done = ctx.sem("s_dve").v

    with (
        nc.sbuf_tensor(f"w3_sb_{uid()}", [128, 2, 32, 512], BF16) as w3,
        nc.sbuf_tensor(f"fo1_sb_{uid()}", [128, 2, 512], F32) as fo,
        nc.psum_tensor(f"fo1_ps0_{uid()}", [128, 512], F32) as fps0,
        nc.psum_tensor(f"fo1_ps1_{uid()}", [128, 512], F32) as fps1,
        nc.Block() as blk,
    ):
        fps = [fps0, fps1]
        pbase, abase = s_pe.v, s_act.v
        vbase2 = s_dve.v
        lw3 = [ctx.sem("lw3a"), ctx.sem("lw3b")]
        lw3b_ = [lw3[0].v, lw3[1].v]
        fs = [ctx.sem("fs0"), ctx.sem("fs1")]
        fsb = [fs[0].v, fs[1].v]

        @blk.gpsimd
        def _(g):
            for mg in range(8):
                if mg >= 2:
                    g.wait_ge(s_pe.h, pbase + 8 * (mg - 1))
                lw3[mg % 2].inc(g.dma_start(
                    w3[:, mg % 2, :, :],
                    dram["w3p"][:, 16384 * mg:16384 * mg + 16384]
                    .rearrange("p (kc m) -> p kc m", m=512)), 16)

        @blk.tensor
        def _(t):
            t.wait_ge(s_dve.h, act_done)
            it = 0
            for mg in range(8):
                t.wait_ge(lw3[mg % 2].h, lw3b_[mg % 2] + 16 * (mg // 2 + 1))
                for mi in range(4):
                    for th in range(2):
                        if it >= 2:
                            t.wait_ge(s_dve.h, vbase2 + it - 1)
                        p = fps[it % 2]
                        for kc in range(KC):
                            ins = t.matmul(p[:, :],
                                           w3[:, mg % 2, kc, 128 * mi:128 * mi + 128],
                                           act_sb[:, kc, 512 * th:512 * th + 512],
                                           start=(kc == 0), stop=(kc == KC - 1))
                        s_pe.inc(ins)
                        it += 1

        @blk.vector
        def _(v):
            it = 0
            for mg in range(8):
                for mi in range(4):
                    m = 4 * mg + mi
                    for th in range(2):
                        v.wait_ge(s_pe.h, pbase + it + 1)
                        if it >= 2:
                            v.wait_ge(fs[it % 2].h, fsb[it % 2] + 16 * (it // 2))
                        s_dve.inc(v.tensor_scalar_add(fo[:, it % 2, :], fps[it % 2][:, :],
                                                      b3t[:, m:m + 1]))
                        it += 1

        @blk.scalar
        def _(sc):
            it = 0
            for mg in range(8):
                for mi in range(4):
                    for th in range(2):
                        m = 4 * mg + mi
                        sc.wait_ge(s_dve.h, vbase2 + it + 1)
                        fs[it % 2].inc(sc.dma_start(
                            ffn_dram[128 * m:128 * m + 128, 512 * th:512 * th + 512],
                            fo[:, it % 2, :]), 16)
                        it += 1


def emit_store_kv(ctx, k_sb, v_sb, k_dram, v_dram, which=("k", "v")):
    nc = ctx.nc
    kvst = ctx.sem("kvst")
    s_dve2 = ctx.sem("s_dve2")
    dve_done = s_dve2.v
    with nc.Block() as blk:
        @blk.scalar
        def _(sc):
            sc.wait_ge(s_dve2.h, dve_done)
            if "k" in which:
                kvst.inc(sc.dma_start(
                    k_dram.rearrange("(m p) t -> p m t", p=128), k_sb[:]), 16)
            if "v" in which:
                kvst.inc(sc.dma_start(
                    v_dram.rearrange("(s p) t -> p s t", p=128), v_sb[:]), 16)


def cum_pairs(g, s):
    return g * 144 + 2 * s * s + 2 * s


def emit_attention(ctx, layer, q_dram, k_full, v_full, o_all,
                   mask_wait=None, gather_wait=None):
    nc = ctx.nc
    s_kv = [ctx.sem("at_kv0"), ctx.sem("at_kv1")]
    s_q = [ctx.sem("at_q0"), ctx.sem("at_q1")]
    s_sp = ctx.sem("s_at_s")
    s_e = ctx.sem("s_at_e")
    s_m = ctx.sem("s_at_m")
    s_pv = ctx.sem("s_at_pv")
    s_n = ctx.sem("s_at_n")
    s_b = ctx.sem("s_at_b")
    mask = ctx.t["mask"]
    ones_col = ctx.t["ones_col"]
    ones_row = ctx.t["ones_row"]

    kvb = [s_kv[0].v, s_kv[1].v]
    qb = [s_q[0].v, s_q[1].v]
    sb, eb, mb_, pvb, nb, bb = (s_sp.v, s_e.v, s_m.v, s_pv.v, s_n.v, s_b.v)

    def P(g, s, p):
        return cum_pairs(g, s) + p

    def BI(g, s, p):
        return g * 32 + 4 * s + (p - 4 * s)

    with (
        nc.sbuf_tensor(f"at_kh_{uid()}", [128, 2, 8192], BF16) as kh,
        nc.sbuf_tensor(f"at_vh_{uid()}", [128, 2, 8192], BF16) as vh,
        nc.sbuf_tensor(f"at_qg_{uid()}", [128, 2, 512], BF16) as qg,
        nc.sbuf_tensor(f"at_e_{uid()}", [128, 2, 1024], BF16) as e_sb,
        nc.sbuf_tensor(f"at_t_{uid()}", [128, 2, 1024], F32) as t_sb,
        nc.sbuf_tensor(f"at_rrow_{uid()}", [1, 512], F32) as rrow,
        nc.sbuf_tensor(f"at_rsb_{uid()}", [128, 512], F32) as rsb,
        nc.psum_tensor(f"at_s0_{uid()}", [128, 1024], F32) as sp0,
        nc.psum_tensor(f"at_s1_{uid()}", [128, 1024], F32) as sp1,
        nc.psum_tensor(f"at_o_{uid()}", [128, 2, 512], F32) as o_ps,
        nc.psum_tensor(f"at_d_{uid()}", [1, 512], F32) as d_ps,
        nc.psum_tensor(f"at_rb_{uid()}", [128, 512], F32) as rb_ps,
        nc.Block() as blk,
    ):
        sps = [sp0, sp1]

        @blk.sync
        def _(sy):
            if gather_wait is not None:
                sy.wait_ge(gather_wait[0].h, gather_wait[1])
            for g in range(NKV):
                if g >= 2:
                    sy.wait_ge(s_pv.h, pvb + (g - 1) * 144)
                for r in range(8):
                    src = k_full[1024 * r + 128 * g:1024 * r + 128 * g + 128, :] \
                        .rearrange("p (u q) -> p u q", q=128)
                    dst = kh[:, g % 2, :].rearrange("p (u r q) -> p r u q", r=8, q=128)[:, r, :, :]
                    s_kv[g % 2].inc(sy.dma_start(dst, src), 16)
                for r in range(8):
                    src = v_full[1024 * r:1024 * r + 1024, 128 * g:128 * g + 128] \
                        .rearrange("(u p) dv -> p u dv", p=128)
                    dst = vh[:, g % 2, :].rearrange("p (u r dv) -> p r u dv", r=8, dv=128)[:, r, :, :]
                    s_kv[g % 2].inc(sy.dma_start(dst, src), 16)
                for s in range(NSLOT):
                    gs = 8 * g + s
                    if gs >= 2:
                        pg, ps_ = divmod(gs - 2, 8)
                        sy.wait_ge(s_sp.h, sb + cum_pairs(pg, ps_) + 4 * ps_ + 4)
                    src = q_dram[512 * g:512 * g + 512, 128 * s:128 * s + 128] \
                        .rearrange("(h d) q -> d h q", d=128)
                    s_q[gs % 2].inc(sy.dma_start(
                        qg[:, gs % 2, :].rearrange("d (h q) -> d h q", q=128), src), 16)

        @blk.scalar
        def _(sc):
            for g in range(NKV):
                for s in range(NSLOT):
                    for p in range(4 * s + 4):
                        pi = P(g, s, p)
                        sc.wait_ge(s_sp.h, sb + pi + 1)
                        if pi >= 2:
                            sc.wait_ge(s_pv.h, pvb + pi - 1)
                        if layer == 0:
                            sc.activation(t_sb[:, pi % 2, :], sps[pi % 2][:, :], AF.Tanh,
                                          scale=ATTN_MULT / 30.0)
                            s_e.inc(sc.activation(e_sb[:, pi % 2, :], t_sb[:, pi % 2, :],
                                                  AF.Exp, scale=30.0,
                                                  bias=ctx.t["cm30"][:, 0:1]))
                        else:
                            s_e.inc(sc.activation(e_sb[:, pi % 2, :], sps[pi % 2][:, :],
                                                  AF.Exp, scale=ATTN_MULT))

        @blk.tensor
        def _(t):
            flat = [(g, s, p) for g in range(NKV) for s in range(NSLOT)
                    for p in range(4 * s + 4)]

            def emit_score(g, s, p):
                gs, pi = 8 * g + s, P(g, s, p)
                if s == 0 and p == 0:
                    t.wait_ge(s_kv[g % 2].h, kvb[g % 2] + 256 * (g // 2 + 1))
                if p == 0:
                    t.wait_ge(s_q[gs % 2].h, qb[gs % 2] + 16 * (gs // 2 + 1))
                if pi >= 2:
                    t.wait_ge(s_e.h, eb + pi - 1)
                sp = sps[pi % 2]
                for jj in range(2):
                    j = 2 * p + jj
                    ins = t.matmul(sp[:, 512 * jj:512 * jj + 512],
                                   kh[:, g % 2, 128 * j:128 * j + 128],
                                   qg[:, gs % 2, :],
                                   start=True, stop=True)
                s_sp.inc(ins)

            def emit_pv(g, s, p):
                gs, pi = 8 * g + s, P(g, s, p)
                npair = 4 * s + 4
                if p == 0 and gs >= 1:
                    t.wait_ge(s_n.h, nb + 3 * (gs - 1) + 1)
                if p >= 4 * s:
                    t.wait_ge(s_m.h, mb_ + BI(g, s, p) + 1)
                else:
                    t.wait_ge(s_e.h, eb + pi + 1)
                for jj in range(2):
                    j = 2 * p + jj
                    t.matmul(o_ps[:, gs % 2, :],
                             vh[:, g % 2, 128 * j:128 * j + 128],
                             e_sb[:, pi % 2, 512 * jj:512 * jj + 512],
                             start=(p == 0 and jj == 0),
                             stop=(p == npair - 1 and jj == 1),
                             skip_group_check=True)
                for jj in range(2):
                    ins = t.matmul(d_ps[:, :], ones_col[:, :],
                                   e_sb[:, pi % 2, 512 * jj:512 * jj + 512],
                                   start=(p == 0 and jj == 0),
                                   stop=(p == npair - 1 and jj == 1),
                                   skip_group_check=True)
                s_pv.inc(ins)

            def emit_rb(g, s):
                gs = 8 * g + s
                t.wait_ge(s_n.h, nb + 3 * gs + 1)
                s_b.inc(t.matmul(rb_ps[:, :], ones_row[:, :], rrow[:, :],
                                 start=True, stop=True))

            rb_pend = []
            prev = None
            for cur in flat:
                emit_score(*cur)
                if rb_pend:
                    emit_rb(*rb_pend.pop(0))
                if prev is not None:
                    emit_pv(*prev)
                    pg, ps_, pp = prev
                    if pp == 4 * ps_ + 3:
                        rb_pend.append((pg, ps_))
                prev = cur
            emit_pv(*prev)
            rb_pend.append((prev[0], prev[1]))
            for r in rb_pend:
                emit_rb(*r)

        @blk.vector
        def _(v):
            if mask_wait is not None:
                v.wait_ge(mask_wait[0].h, mask_wait[1])
            for g in range(NKV):
                for s in range(NSLOT):
                    gs = 8 * g + s
                    for p in range(4 * s, 4 * s + 4):
                        pi = P(g, s, p)
                        v.wait_ge(s_e.h, eb + pi + 1)
                        jb0 = 2 * p - 8 * s
                        s_m.inc(v.tensor_mul(e_sb[:, pi % 2, :], e_sb[:, pi % 2, :],
                                             mask[:, 512 * jb0:512 * jb0 + 1024]))
                    v.wait_ge(s_pv.h, pvb + cum_pairs(g, s) + 4 * s + 4)
                    s_n.inc(v.reciprocal(rrow[:, :], d_ps[:, :]))
                    v.wait_ge(s_b.h, bb + gs + 1)
                    s_n.inc(v.tensor_copy(rsb[:, :], rb_ps[:, :]))
                    dst = o_all[:, 4 * g:4 * g + 4, 128 * s:128 * s + 128]
                    s_n.inc(v.tensor_mul(
                        dst,
                        o_ps[:, gs % 2, :].rearrange("p (h q) -> p h q", q=128),
                        rsb[:, :].rearrange("p (h q) -> p h q", q=128)))

    return s_n.v


def emit_wo_epilogue(ctx, layer, dram, o_all, x_dram, ffn_dram, out_dram,
                     attn_done):
    nc = ctx.nc
    s_pe = ctx.sem("s_pe")
    s_dve = ctx.sem("s_dve")
    s_n = ctx.sem("s_at_n")
    wo_s = [ctx.sem("wo0s"), ctx.sem("wo1s")]
    xf = [ctx.sem("xf0"), ctx.sem("xf1")]
    eps = [ctx.sem("eps0"), ctx.sem("eps1")]
    fs = [ctx.sem("fs0"), ctx.sem("fs1")]

    with (
        nc.sbuf_tensor(f"wo_sb_{uid()}", [128, 2, 32, 512], BF16) as wo,
        nc.sbuf_tensor(f"ep_x_{uid()}", [128, 2, 512], F32) as xbuf,
        nc.sbuf_tensor(f"ep_f_{uid()}", [128, 2, 512], F32) as fbuf,
        nc.sbuf_tensor(f"ep_o_{uid()}", [128, 2, 512], F32) as obuf,
        nc.psum_tensor(f"ep_ps0_{uid()}", [128, 512], F32) as ps0,
        nc.psum_tensor(f"ep_ps1_{uid()}", [128, 512], F32) as ps1,
        nc.Block() as blk,
    ):
        ps = [ps0, ps1]
        pbase, vbase = s_pe.v, s_dve.v
        wob = [wo_s[0].v, wo_s[1].v]
        xfb = [xf[0].v, xf[1].v]
        epb = [eps[0].v, eps[1].v]
        fs_tot = [fs[0].v, fs[1].v]
        per_g = 16
        wo_src = dram["wo0p"] if layer == 0 else dram["wo1p"]

        @blk.gpsimd
        def _(g):
            for mg in range(8):
                if mg >= 2:
                    g.wait_ge(s_pe.h, pbase + 8 * (mg - 1))
                wo_s[mg % 2].inc(g.dma_start(
                    wo[:, mg % 2, :, :],
                    wo_src[:, 16384 * mg:16384 * mg + 16384]
                    .rearrange("p (kc m) -> p kc m", m=512)), 16)

        @blk.sync
        def _(sy):
            sy.wait_ge(fs[0].h, fs_tot[0])
            sy.wait_ge(fs[1].h, fs_tot[1])
            it = 0
            for mg in range(8):
                for mi in range(4):
                    m = 4 * mg + mi
                    for th in range(2):
                        if it >= 2:
                            sy.wait_ge(s_dve.h, vbase + it - 1)
                        sl = slice(512 * th, 512 * th + 512)
                        xf[it % 2].inc(sy.dma_start(xbuf[:, it % 2, :],
                                                    x_dram[128 * m:128 * m + 128, sl]), 16)
                        xf[it % 2].inc(sy.dma_start(fbuf[:, it % 2, :],
                                                    ffn_dram[128 * m:128 * m + 128, sl]), 16)
                        it += 1

        @blk.tensor
        def _(t):
            t.wait_ge(s_n.h, attn_done)
            it = 0
            for mg in range(8):
                t.wait_ge(wo_s[mg % 2].h, wob[mg % 2] + per_g * (mg // 2 + 1))
                for mi in range(4):
                    for th in range(2):
                        if it >= 2:
                            t.wait_ge(s_dve.h, vbase + it - 1)
                        p = ps[it % 2]
                        for kc in range(KC):
                            ins = t.matmul(p[:, :],
                                           wo[:, mg % 2, kc, 128 * mi:128 * mi + 128],
                                           o_all[:, kc, 512 * th:512 * th + 512],
                                           start=(kc == 0), stop=(kc == KC - 1))
                        s_pe.inc(ins)
                        it += 1

        @blk.vector
        def _(v):
            it = 0
            for mg in range(8):
                for mi in range(4):
                    for th in range(2):
                        v.wait_ge(s_pe.h, pbase + it + 1)
                        v.wait_ge(xf[it % 2].h, xfb[it % 2] + 32 * (it // 2 + 1))
                        if it >= 2:
                            v.wait_ge(eps[it % 2].h, epb[it % 2] + 16 * (it // 2))
                        v.tensor_add(obuf[:, it % 2, :], ps[it % 2][:, :], xbuf[:, it % 2, :])
                        s_dve.inc(v.tensor_add(obuf[:, it % 2, :], obuf[:, it % 2, :],
                                               fbuf[:, it % 2, :]))
                        it += 1

        @blk.scalar
        def _(sc):
            it = 0
            for mg in range(8):
                for mi in range(4):
                    m = 4 * mg + mi
                    for th in range(2):
                        sc.wait_ge(s_dve.h, vbase + it + 1)
                        eps[it % 2].inc(sc.dma_start(
                            out_dram[128 * m:128 * m + 128, 512 * th:512 * th + 512],
                            obuf[:, it % 2, :]), 16)
                        it += 1




def token_cols(core):
    return np.concatenate([np.arange(128 * (8 * s + core), 128 * (8 * s + core) + 128)
                           for s in range(NSLOT)])


def host_constants(core):
    import ml_dtypes
    pos = token_cols(core).astype(np.float64)
    inv = ROPE_THETA ** (-np.arange(64, dtype=np.float64) / 64.0)
    ang = pos[None, :] * inv[:, None]
    qcos = np.empty((128, TPC), np.float32)
    qsin = np.empty((128, TPC), np.float32)
    qcos[0:64] = np.cos(ang)
    qcos[64:128] = np.cos(ang)
    qsin[0:64] = -np.sin(ang)
    qsin[64:128] = np.sin(ang)
    kcos = np.empty((128, TPC), np.float32)
    ksin = np.empty((128, TPC), np.float32)
    for h in range(4):
        kcos[32 * h:32 * h + 32] = np.cos(ang[:32])
        ksin[32 * h:32 * h + 32] = np.sin(ang[:32])
    k_idx = np.arange(128)
    q_idx = np.arange(128)
    mask = np.zeros((128, NSLOT, GSIZE, 128), np.float32)
    for jb in range(NSLOT):
        m = ((128 * jb + k_idx[:, None]) <= (128 * core + q_idx[None, :]))
        mask[:, jb, :, :] = m[:, None, :]
    return {
        "qcos_in": qcos, "qsin_in": qsin, "kcos_in": kcos, "ksin_in": ksin,
        "mask_in": np.ascontiguousarray(
            mask.reshape(128, NSLOT * GSIZE * 128).astype(ml_dtypes.bfloat16)),
    }


def _f32(a):
    return np.ascontiguousarray(np.asarray(a, dtype=np.float32))


def emit_gather(ctx, k_loc_t, v_loc_t, k_full_t, v_full_t, tag):
    nc = ctx.nc
    s_coll = ctx.sem("s_coll")
    kvst = ctx.sem("kvst")
    kv_tot = kvst.v
    with nc.Block() as blk:
        @blk.gpsimd
        def _(g):
            g.wait_ge(kvst.h, kv_tot)
            s_coll.inc(g.collective_compute(
                "AllGather", OP.bypass,
                replica_groups=[list(range(NCORES))],
                ins=[k_loc_t.ap().opt()],
                outs=[k_full_t.ap().opt()]))
            s_coll.inc(g.collective_compute(
                "AllGather", OP.bypass,
                replica_groups=[list(range(NCORES))],
                ins=[v_loc_t.ap().opt()],
                outs=[v_full_t.ap().opt()]))
    return (s_coll, s_coll.v)


PRE_SHAPES = [
    ("wq0p", 8 * 16384), ("wkp", 32768), ("wvp", 32768), ("u1p", 32768),
    ("ffn0p", FCH * 3072), ("v2p", 32768), ("wo0p", 8 * 16384),
    ("wq1p", 8 * 16384), ("wkrp", 16384), ("wdkp", 16384),
    ("wupkp", 2048), ("wupvp", 4096), ("w12p", 16 * 16384),
    ("w3p", 8 * 16384), ("wo1p", 8 * 16384),
]


def build_fused():
    nc = bass.Bass("TRN2", target_bir_lowering=False)
    E_IN, E_OUT = "ExternalInput", "ExternalOutput"
    d = {}
    d["x"] = nc.dram_tensor("x", [D, TPC], F32, kind=E_IN).ap()
    for nm, w in PRE_SHAPES:
        d[nm] = nc.dram_tensor(nm, [128, w], F32, kind=E_IN).ap()
    for nm, w in [("b1g", FCH), ("b1u", FCH), ("b2t", 32),
                  ("b12g", 32), ("b12u", 32), ("b3t", 32)]:
        d[nm] = nc.dram_tensor(nm, [128, w], F32, kind=E_IN).ap()
    for nm in ["qcos_in", "qsin_in", "kcos_in", "ksin_in"]:
        d[nm] = nc.dram_tensor(nm, [128, TPC], F32, kind=E_IN).ap()
    d["mask_in"] = nc.dram_tensor("mask_in", [128, NSLOT * GSIZE * 128], BF16, kind=E_IN).ap()
    k_loc_t = nc.dram_tensor("k_loc", [1024, TPC], BF16, kind="Internal")
    v_loc_t = nc.dram_tensor("v_loc", [1024, TPC], BF16, kind="Internal")
    k_full_t = nc.dram_tensor("k_full", [SEQ, 1024], BF16, kind="Internal")
    v_full_t = nc.dram_tensor("v_full", [SEQ, 1024], BF16, kind="Internal")
    d["k_loc"], d["v_loc"] = k_loc_t.ap(), v_loc_t.ap()
    d["k_full"], d["v_full"] = k_full_t.ap(), v_full_t.ap()
    d["q_d"] = nc.dram_tensor("q_d", [D, TPC], BF16, kind="Internal").ap()
    d["ffn_d"] = nc.dram_tensor("ffn_d", [D, TPC], F32, kind="Internal").ap()
    d["x_mid"] = nc.dram_tensor("x_mid", [D, TPC], F32, kind="Internal").ap()
    d["xo"] = nc.dram_tensor("xo", [D, TPC], F32, kind=E_OUT).ap()

    with ExitStack() as stack:
        ctx = Ctx(nc, stack)
        ctx.t["ones_col"] = stack.enter_context(nc.sbuf_tensor(f"ones_col_{uid()}", [128, 1], BF16))
        ctx.t["ones_row"] = stack.enter_context(nc.sbuf_tensor(f"ones_row_{uid()}", [1, 128], F32))
        ctx.t["cm30"] = stack.enter_context(nc.sbuf_tensor(f"cm30_{uid()}", [128, 1], F32))
        ctx.t["mask"] = stack.enter_context(
            nc.sbuf_tensor(f"mask_sb_{uid()}", [128, NSLOT * GSIZE * 128], BF16))
        for nm, w in [("b1g", FCH), ("b1u", FCH), ("b2t", 32),
                      ("b12g", 32), ("b12u", 32), ("b3t", 32)]:
            ctx.t[nm] = stack.enter_context(nc.sbuf_tensor(nm + "_sb", [128, w], F32))
        emit_consts(ctx)
        s_g = ctx.sem("s_g")
        with nc.Block() as blk:
            @blk.gpsimd
            def _(g):
                for nm in ["b1g", "b1u", "b2t", "b12g", "b12u", "b3t"]:
                    s_g.inc(g.dma_start(ctx.t[nm][:, :], d[nm]), 16)
                s_g.inc(g.dma_start(ctx.t["mask"][:, :], d["mask_in"]), 16)
                g.wait_ge(s_g.h, s_g.v)

        with ExitStack() as l0:
            xn_sb = l0.enter_context(nc.sbuf_tensor(f"xn_sb0_{uid()}", [128, KC, TPC], BF16))
            emit_ln(ctx, d["x"], xn_sb)
            with ExitStack() as kvs:
                v_sb = kvs.enter_context(nc.sbuf_tensor(f"v_sb0_{uid()}", [128, 8, 1024], BF16))
                k_sb = kvs.enter_context(nc.sbuf_tensor(f"k_sb0_{uid()}", [128, 8, 1024], BF16))
                emit_kv_l0(ctx, d["wkp"], d["wvp"], xn_sb, k_sb, v_sb)
                emit_store_kv(ctx, k_sb, v_sb, d["k_loc"], d["v_loc"])
                gw0 = emit_gather(ctx, k_loc_t, v_loc_t, k_full_t, v_full_t, 0)
                emit_q(ctx, 0, d, xn_sb, d["q_d"])
            emit_ffn_l0(ctx, d, xn_sb, d["ffn_d"])
        with ExitStack() as l0a:
            o_all = l0a.enter_context(nc.sbuf_tensor(f"o_all0_{uid()}", [128, KC, TPC], BF16))
            attn_done = emit_attention(ctx, 0, d["q_d"], d["k_full"], d["v_full"],
                                       o_all, gather_wait=gw0)
            emit_wo_epilogue(ctx, 0, d, o_all, d["x"], d["ffn_d"], d["x_mid"], attn_done)

        with ExitStack() as l1:
            xn_sb = l1.enter_context(nc.sbuf_tensor(f"xn_sb1_{uid()}", [128, KC, TPC], BF16))
            eps0, eps1 = ctx.sem("eps0"), ctx.sem("eps1")
            e0v, e1v = eps0.v, eps1.v
            with nc.Block() as blk:
                @blk.sync
                def _(sy):
                    sy.wait_ge(eps0.h, e0v)
                    sy.wait_ge(eps1.h, e1v)
            emit_ln(ctx, d["x_mid"], xn_sb)
            with ExitStack() as kvs:
                v_sb = kvs.enter_context(nc.sbuf_tensor(f"v_sb1_{uid()}", [128, 8, 1024], BF16))
                for nm, key in [("qcos", "qcos_in"), ("qsin", "qsin_in"),
                                ("kcos", "kcos_in"), ("ksin", "ksin_in")]:
                    ctx.t[nm] = kvs.enter_context(nc.sbuf_tensor(nm, [128, TPC], F32))
                with nc.Block() as blk:
                    @blk.gpsimd
                    def _(g):
                        for nm, key in [("qcos", "qcos_in"), ("qsin", "qsin_in"),
                                        ("kcos", "kcos_in"), ("ksin", "ksin_in")]:
                            s_g.inc(g.dma_start(ctx.t[nm][:, :], d[key]), 16)
                        g.wait_ge(s_g.h, s_g.v)
                emit_kv_l1(ctx, d, xn_sb, v_sb, d["k_loc"])
                emit_store_kv(ctx, None, v_sb, None, d["v_loc"], which=("v",))
                gw1 = emit_gather(ctx, k_loc_t, v_loc_t, k_full_t, v_full_t, 1)
                emit_q(ctx, 1, d, xn_sb, d["q_d"])
            with nc.sbuf_tensor(f"act1_sb_{uid()}", [128, KC, TPC], BF16) as act_sb:
                emit_ffn_l1(ctx, d, xn_sb, act_sb, d["ffn_d"])
        with ExitStack() as l1a:
            o_all = l1a.enter_context(nc.sbuf_tensor(f"o_all1_{uid()}", [128, KC, TPC], BF16))
            attn_done = emit_attention(ctx, 1, d["q_d"], d["k_full"], d["v_full"],
                                       o_all, gather_wait=gw1)
            emit_wo_epilogue(ctx, 1, d, o_all, d["x_mid"], d["ffn_d"], d["xo"], attn_done)

        with nc.Block() as blk:
            waits = []
            for nm in ["eps0", "eps1"]:
                waits.append((ctx.sems[nm].h, ctx.sems[nm].v))

            @blk.sync
            def _(sy):
                for h, v in waits:
                    sy.wait_ge(h, v)
    return nc


def _c(a):
    return np.ascontiguousarray(a, dtype=np.float32)


def prelayout_weights(inputs):
    w = {}
    Wq0 = _f32(inputs["L0_attn_Wq"])
    w["wq0p"] = _c(Wq0.reshape(32, 128, 8, 512).transpose(1, 2, 0, 3).reshape(128, -1))
    w["wkp"] = _c(_f32(inputs["L0_attn_Wk"]).reshape(32, 128, 1024)
                  .transpose(1, 0, 2).reshape(128, -1))
    w["wvp"] = _c(_f32(inputs["L0_attn_Wv"]).reshape(32, 128, 1024)
                  .transpose(1, 0, 2).reshape(128, -1))
    w["wo0p"] = _c(_f32(inputs["L0_attn_Wo"]).reshape(32, 128, 8, 512)
                   .transpose(1, 2, 0, 3).reshape(128, -1))
    w["u1p"] = _c(_f32(inputs["L0_ffn_U1"]).reshape(32, 128, 1024)
                  .transpose(1, 0, 2).reshape(128, -1))
    V1 = _f32(inputs["L0_ffn_V1"])
    U2 = _f32(inputs["L0_ffn_U2"])
    A = V1[:, :INTER0].reshape(8, 128, FCH, 128).transpose(1, 2, 0, 3).reshape(128, FCH, 1024)
    B = V1[:, INTER0:].reshape(8, 128, FCH, 128).transpose(1, 2, 0, 3).reshape(128, FCH, 1024)
    C = U2.reshape(FCH, 128, 1024).transpose(1, 0, 2)
    w["ffn0p"] = _c(np.concatenate([A, B, C], axis=2).reshape(128, -1))
    w["v2p"] = _c(_f32(inputs["L0_ffn_V2"]).reshape(8, 128, 4, 1024)
                  .transpose(1, 2, 0, 3).reshape(128, -1))
    w["wq1p"] = _c(_f32(inputs["L1_attn_Wq"]).reshape(8, 512, 32, 128)
                   .transpose(3, 0, 2, 1).reshape(128, -1))
    wkr = _f32(inputs["L1_attn_Wkr"])
    a_rows = np.concatenate([np.arange(64 * g, 64 * g + 32) for g in range(8)])
    b_rows = np.concatenate([np.arange(64 * g + 32, 64 * g + 64) for g in range(8)])
    wkr_perm = np.concatenate([wkr[a_rows], wkr[b_rows]], axis=0)
    w["wkrp"] = _c(wkr_perm.reshape(512, 32, 128).transpose(2, 1, 0).reshape(128, -1))
    w["wdkp"] = _c(_f32(inputs["L1_attn_Wdk"]).reshape(512, 32, 128)
                   .transpose(2, 1, 0).reshape(128, -1))
    w["wupkp"] = _c(_f32(inputs["L1_attn_Wupk"]).reshape(512, 4, 128)
                    .transpose(2, 1, 0).reshape(128, -1))
    w["wupvp"] = _c(_f32(inputs["L1_attn_Wupv"]).reshape(2, 512, 4, 128)
                    .transpose(3, 2, 0, 1).reshape(128, -1))
    w["w12p"] = _c(_f32(inputs["L1_ffn_w12"]).reshape(2, 16, 256, 32, 128)
                   .transpose(4, 1, 0, 3, 2).reshape(128, -1))
    w["w3p"] = _c(_f32(inputs["L1_ffn_w3"]).reshape(8, 512, 32, 128)
                  .transpose(3, 0, 2, 1).reshape(128, -1))
    w["wo1p"] = _c(_f32(inputs["L1_attn_Wo"]).reshape(8, 512, 32, 128)
                   .transpose(3, 0, 2, 1).reshape(128, -1))
    b1 = _f32(inputs["L0_ffn_b1"])
    b2 = _f32(inputs["L0_ffn_b2"])
    b12 = _f32(inputs["L1_ffn_b12"])
    b3 = _f32(inputs["L1_ffn_b3"])
    w["b1g"] = _c(b1[:INTER0].reshape(FCH, 128).T)
    w["b1u"] = _c(b1[INTER0:].reshape(FCH, 128).T)
    w["b2t"] = _c(b2.reshape(32, 128).T)
    w["b12g"] = _c(b12[:D].reshape(32, 128).T)
    w["b12u"] = _c(b12[D:].reshape(32, 128).T)
    w["b3t"] = _c(b3.reshape(32, 128).T)
    return w


def kernel(**inputs):
    cols = [token_cols(c) for c in range(NCORES)]
    consts = [host_constants(c) for c in range(NCORES)]
    xT = _f32(inputs["L0_attn_hidden_states"][0]).T
    w = prelayout_weights(inputs)
    nc = build_fused()
    in_maps = []
    for c in range(NCORES):
        m = dict(x=np.ascontiguousarray(xT[:, cols[c]]), **w,
                 mask_in=consts[c]["mask_in"])
        for nm in ["qcos_in", "qsin_in", "kcos_in", "ksin_in"]:
            m[nm] = consts[c][nm]
        in_maps.append(m)
    res = run_bass_kernel_spmd(nc, in_maps, core_ids=list(range(NCORES)))
    out = np.empty((SEQ, D), np.float32)
    for c in range(NCORES):
        out[cols[c], :] = res.results[c]["xo"].T
    return out[None, :, :]
